# version 09
# baseline (speedup 1.0000x reference)
import numpy as np
from contextlib import ExitStack

import concourse.bass as bass
import concourse.mybir as mybir
from concourse.bass_utils import run_bass_kernel_spmd
from concourse.replica_groups import maybe_share_collective_output_space

F32 = mybir.dt.float32
BF16 = mybir.dt.bfloat16
I32 = mybir.dt.int32
AF = mybir.ActivationFunctionType
ALU = mybir.AluOpType
AX = mybir.AxisListType.X

NCORES = 8
D = 1024
SEQ = 8192
NB = 2
SL = SEQ // NCORES
NCOL = NB * SL
CH = D // 128
HALO = 32
LN_EPS = 1e-5
NT512 = [(i * 512, (i + 1) * 512) for i in range(NCOL // 512)]

N_STEPS = 6
SLAB_BYTES = 204800
DEBUG_TAPS = []
W5_TRUNC = 99
W5_NBQ = None



def boxes_overlap(a, b):
    return not (a[1] <= b[0] or b[1] <= a[0] or a[3] <= b[2] or b[3] <= a[2])


def box_covers(o, i):
    return o[0] <= i[0] and o[1] >= i[1] and o[2] <= i[2] and o[3] >= i[3]


class TAP:
    __slots__ = ("ap", "name", "box")

    def __init__(self, ap, name, box):
        self.ap = ap
        self.name = name
        self.box = box


class Op:
    __slots__ = ("idx", "eng", "emit", "kind", "deps", "sem", "val",
                 "pre_waits", "drain_before", "tag")

    def __init__(self, idx, eng, emit, kind, tag):
        self.idx = idx
        self.eng = eng
        self.emit = emit
        self.kind = kind
        self.deps = set()
        self.sem = None
        self.val = 0
        self.pre_waits = []
        self.drain_before = False
        self.tag = tag


class Prog:
    ENGS = ("t", "v", "s", "g", "y")

    def __init__(self, nc, es, n_dma_sems=12):
        self.nc = nc
        self.es = es
        self.ops = []
        self.writers = {}
        self.readers = {}
        self.eng_sem = {e: es.enter_context(nc.semaphore(f"sem_{e}")) for e in self.ENGS}
        self.dma_pools = {e: [es.enter_context(nc.semaphore(f"dma_{e}{i}"))
                              for i in range(n_dma_sems)] for e in ("y", "g")}
        self.dma_rr = {e: 0 for e in ("y", "g")}
        self.dma_tot = {}
        self.cc_sems = []

    def _dep_read(self, op, taps):
        for t in taps:
            for (b, oi) in self.writers.get(t.name, ()):
                if oi != op.idx and boxes_overlap(b, t.box):
                    op.deps.add(oi)
            self.readers.setdefault(t.name, []).append((t.box, op.idx))

    def _dep_write(self, op, taps):
        for t in taps:
            ws = self.writers.setdefault(t.name, [])
            for (b, oi) in ws:
                if oi != op.idx and boxes_overlap(b, t.box):
                    op.deps.add(oi)
            rs = self.readers.setdefault(t.name, [])
            keep = []
            for (b, oi) in rs:
                if oi == op.idx:
                    keep.append((b, oi))
                    continue
                if boxes_overlap(b, t.box):
                    op.deps.add(oi)
                    if not box_covers(t.box, b):
                        keep.append((b, oi))
                else:
                    keep.append((b, oi))
            self.readers[t.name] = keep
            ws2 = [(b, oi) for (b, oi) in ws if not box_covers(t.box, b)]
            ws2.append((t.box, op.idx))
            self.writers[t.name] = ws2

    def add(self, eng, emit, reads=(), writes=(), kind="c", tag=""):
        op = Op(len(self.ops), eng, emit, kind, tag)
        self.ops.append(op)
        self._dep_read(op, reads)
        self._dep_write(op, writes)
        return op

    def tt(self, eng, out, in0, in1, op, tag=""):
        return self.add(eng, (lambda e, o=out.ap, a=in0.ap, b=in1.ap, alu=op:
                              e.tensor_tensor(out=o, in0=a, in1=b, op=alu)),
                        reads=[in0, in1], writes=[out], tag=tag or "tt")

    def ts(self, eng, out, in0, s1, op0, s2=None, op1=None, tag=""):
        s1ap = s1.ap if isinstance(s1, TAP) else s1
        rd = [in0] + ([s1] if isinstance(s1, TAP) else [])
        if op1 is None:
            return self.add(eng, (lambda e, o=out.ap, a=in0.ap, s=s1ap, alu=op0:
                                  e.tensor_scalar(out=o, in0=a, scalar1=s, scalar2=None, op0=alu)),
                            reads=rd, writes=[out], tag=tag or "ts")
        s2ap = s2.ap if isinstance(s2, TAP) else s2
        if isinstance(s2, TAP):
            rd.append(s2)
        return self.add(eng, (lambda e, o=out.ap, a=in0.ap, s=s1ap, s2_=s2ap, a0=op0, a1=op1:
                              e.tensor_scalar(out=o, in0=a, scalar1=s, scalar2=s2_, op0=a0, op1=a1)),
                        reads=rd, writes=[out], tag=tag or "ts2")

    def stt(self, eng, out, in0, scalar, in1, op0, op1, tag=""):
        sap = scalar.ap if isinstance(scalar, TAP) else scalar
        rd = [in0, in1] + ([scalar] if isinstance(scalar, TAP) else [])
        return self.add(eng, (lambda e, o=out.ap, a=in0.ap, s=sap, b=in1.ap, a0=op0, a1=op1:
                              e.scalar_tensor_tensor(out=o, in0=a, scalar=s, in1=b, op0=a0, op1=a1)),
                        reads=rd, writes=[out], tag=tag or "stt")

    def act(self, out, in_, func, bias=None, scale=None, accum=None, tag=""):
        rd = [in_]
        bap = bias.ap if isinstance(bias, TAP) else bias
        sap = scale.ap if isinstance(scale, TAP) else scale
        if isinstance(bias, TAP):
            rd.append(bias)
        if isinstance(scale, TAP):
            rd.append(scale)
        wr = [out] + ([accum] if accum is not None else [])

        def em(e, o=out.ap, a=in_.ap, f=func, b=bap, s=sap, acc=accum):
            kw = {}
            if b is not None:
                kw["bias"] = b
            if s is not None:
                kw["scale"] = s
            if acc is not None:
                kw["accum_out"] = acc.ap
            return e.activation(out=o, in_=a, func=f, **kw)
        return self.add("s", em, reads=rd, writes=wr, tag=tag or "act")

    def copy(self, eng, out, in_, tag=""):
        if eng == "s":
            return self.add(eng, (lambda e, o=out.ap, a=in_.ap: e.copy(out=o, in_=a)),
                            reads=[in_], writes=[out], tag=tag or "scopy")
        return self.add(eng, (lambda e, o=out.ap, a=in_.ap: e.tensor_copy(out=o, in_=a)),
                        reads=[in_], writes=[out], tag=tag or "copy")

    def red(self, eng, out, in_, op, absval=False, negate=False, tag=""):
        return self.add(eng, (lambda e, o=out.ap, a=in_.ap, al=op, av=absval, ng=negate:
                              e.tensor_reduce(out=o, in_=a, axis=AX, op=al,
                                              apply_absolute_value=av, negate=ng)),
                        reads=[in_], writes=[out], tag=tag or "red")

    def scan_add(self, eng, out, data, initial=0.0, tag=""):
        iap = initial.ap if isinstance(initial, TAP) else initial
        rd = [data] + ([initial] if isinstance(initial, TAP) else [])
        return self.add(eng, (lambda e, o=out.ap, d=data.ap, i=iap:
                              e.tensor_tensor_scan(out=o, data0=d, data1=d, initial=i,
                                                   op0=ALU.add, op1=ALU.bypass)),
                        reads=rd, writes=[out], tag=tag or "scan")

    def mmgroup(self, pairs, out, tag=""):
        rd = []
        for l, r in pairs:
            rd += [l, r]
        n = len(pairs)

        def em(e, ps=out.ap, pr=[(l.ap, r.ap) for l, r in pairs], n=n):
            inst = None
            for i, (la, ra) in enumerate(pr):
                inst = e.matmul(ps, la, ra, start=(i == 0), stop=(i == n - 1))
            return inst
        return self.add("t", em, reads=rd, writes=[out], tag=tag or "mm")

    def transpose(self, out, in_, ident, tag=""):
        return self.add("t", (lambda e, o=out.ap, a=in_.ap, i=ident.ap:
                              e.transpose(o, a, i)),
                        reads=[in_, ident], writes=[out], tag=tag or "tr")

    def coll(self, kind, op, cin, cout, rg, tag=""):
        return self.add("g", (lambda e, k=kind, o=op, a=cin.ap, b=cout.ap, g=rg:
                              e.collective_compute(k, o, ins=[a], outs=[b], replica_groups=g)),
                        reads=[cin], writes=[cout], kind="cc", tag=tag or kind)

    def finalize(self):
        eng_ord = {e: 0 for e in self.ENGS}
        for op in self.ops:
            if op.kind == "c":
                eng_ord[op.eng] += 1
                op.sem = self.eng_sem[op.eng]
                op.val = eng_ord[op.eng]
            elif op.kind == "d":
                pool = self.dma_pools[op.eng]
                sem = pool[self.dma_rr[op.eng] % len(pool)]
                self.dma_rr[op.eng] += 1
                prev = self.dma_tot.get(id(sem), 0)
                if prev:
                    op.pre_waits.append((sem, prev))
                self.dma_tot[id(sem)] = prev + 16
                op.sem = sem
                op.val = prev + 16
            elif op.kind == "cc":
                sem = self.es.enter_context(self.nc.semaphore(f"cc{len(self.cc_sems)}"))
                self.cc_sems.append(sem)
                op.sem = sem
                op.val = 1
        watermark = {}
        for op in self.ops:
            per_sem = {}
            for di in sorted(op.deps):
                dop = self.ops[di]
                if dop.sem is None:
                    continue
                k = id(dop.sem)
                if k not in per_sem or per_sem[k][1] < dop.val:
                    per_sem[k] = (dop.sem, dop.val)
            for k, (sem, val) in per_sem.items():
                wk = (op.eng, k)
                if watermark.get(wk, -1) >= val:
                    continue
                watermark[wk] = val
                op.pre_waits.append((sem, val))
        finals = []
        for e in self.ENGS:
            if eng_ord[e]:
                finals.append((self.eng_sem[e], eng_ord[e]))
        for e in ("y", "g"):
            for sem in self.dma_pools[e]:
                tot = self.dma_tot.get(id(sem), 0)
                if tot:
                    finals.append((sem, tot))
        for sem in self.cc_sems:
            finals.append((sem, 1))
        self.final_waits = finals

    def emit_all(self, block):
        per_eng = {e: [] for e in self.ENGS}
        for op in self.ops:
            per_eng[op.eng].append(op)
        prog = self
        attr = {"t": "tensor", "v": "vector", "s": "scalar", "g": "gpsimd", "y": "sync"}

        def mk_body(engname):
            ops = per_eng[engname]

            def body(eng):
                for op in ops:
                    for (sem, val) in op.pre_waits:
                        eng.wait_ge(sem, val)
                    if op.drain_before and engname != "t":
                        eng.drain()
                    inst = op.emit(eng)
                    if op.sem is not None:
                        assert inst is not None, f"op {op.idx} [{op.tag}] returned no inst"
                        inst.then_inc(op.sem, 16 if op.kind == "d" else 1)
                if engname == "y":
                    for (sem, val) in prog.final_waits:
                        eng.wait_ge(sem, val)
            body.__name__ = f"_{engname}"
            getattr(block, attr[engname])(body)
        for e in self.ENGS:
            mk_body(e)



DT_SIZE = {F32: 4, BF16: 2, I32: 4}


class Buf:
    __slots__ = ("slab", "name", "boff", "cols", "dtype")

    def __init__(self, slab, name, boff, cols, dtype):
        self.slab = slab
        self.name = name
        self.boff = boff
        self.cols = cols
        self.dtype = dtype

    def t(self, p0=0, p1=128, c0=0, c1=None):
        if c1 is None:
            c1 = self.cols
        sz = DT_SIZE[self.dtype]
        b0 = self.boff + c0 * sz
        b1 = self.boff + c1 * sz
        ap = self.slab.handle.ap()[p0:p1, b0:b1].bitcast(self.dtype)
        return TAP(ap, self.slab.name, (p0, p1, b0, b1))

    def free(self):
        self.slab.free(self)


class CBuf:
    def __init__(self, buf, nch, width):
        self.buf = buf
        self.nch = nch
        self.width = width

    def t(self, c, c0=0, c1=None, p0=0, p1=128):
        if c1 is None:
            c1 = self.width
        return self.buf.t(p0, p1, c * self.width + c0, c * self.width + c1)

    def free(self):
        self.buf.free()


class FBuf:

    def __init__(self, bufs, width):
        self.bufs = bufs
        self.width = width

    def t(self, c, c0=0, c1=None, p0=0, p1=128):
        if c1 is None:
            c1 = self.width
        return self.bufs[c].t(p0, p1, c0, c1)

    def free(self):
        for b in self.bufs:
            b.free()


class Slab:
    def __init__(self, nc, es, name, nbytes):
        self.handle = es.enter_context(nc.sbuf_tensor(name, [128, nbytes], mybir.dt.uint8))
        self.name = name
        self.free_list = [(0, nbytes)]
        self.peak = 0
        self.used = 0

    def alloc(self, cols, dtype, tag=""):
        nb = (cols * DT_SIZE[dtype] + 63) // 64 * 64
        for i, (o, n) in enumerate(self.free_list):
            if n >= nb:
                if n == nb:
                    self.free_list.pop(i)
                else:
                    self.free_list[i] = (o + nb, n - nb)
                self.used += nb
                self.peak = max(self.peak, self.used)
                return Buf(self, tag, o, cols, dtype)
        raise MemoryError(f"slab alloc {nb}B failed (tag={tag}, used={self.used}, free={self.free_list})")

    def challoc(self, nch, width, dtype, tag=""):
        b = self.alloc(nch * width, dtype, tag)
        return CBuf(b, nch, width)

    def falloc(self, nch, width, dtype, tag=""):
        return FBuf([self.alloc(width, dtype, f"{tag}{c}") for c in range(nch)], width)

    def free(self, buf):
        nb = (buf.cols * DT_SIZE[buf.dtype] + 63) // 64 * 64
        self.used -= nb
        self.free_list.append((buf.boff, nb))
        self.free_list.sort()
        merged = []
        for o, n in self.free_list:
            if merged and merged[-1][0] + merged[-1][1] == o:
                merged[-1] = (merged[-1][0], merged[-1][1] + n)
            else:
                merged.append((o, n))
        self.free_list = merged


class DBuf:
    def __init__(self, handle, name):
        self.handle = handle
        self.name = name

    def t(self, box=None):
        return TAP(self.handle.ap(), self.name, box or (0, 1, 0, 1 << 50))

    def box(self, b):
        return TAP(None, self.name, b)


class MK:
    def __init__(self, nc, es, n_steps):
        self.nc = nc
        self.es = es
        self.n_steps = n_steps
        self.p = Prog(nc, es)
        self.slab = Slab(nc, es, "slab", SLAB_BYTES)
        self.rg = [list(range(NCORES))]
        self.inputs = {}
        self._cc_ctr = 0
        self.pump = lambda n=2: None
        self.psF = []
        for i in range(6):
            h = es.enter_context(nc.psum_tensor(f"psF{i}", [128, 512], F32))
            self.psF.append(_PsBuf(h, f"psF{i}"))
        self.psB = []
        for i in range(2):
            h = es.enter_context(nc.psum_tensor(f"psB{i}", [128, 1024], BF16))
            self.psB.append(_PsBuf(h, f"psB{i}"))
        self.ones_col_bf = es.enter_context(nc.sbuf_tensor("ones_col", [128, 2], BF16))
        self.ones128_bf = es.enter_context(nc.sbuf_tensor("ones128", [128, 128], BF16))
        self.p.add("v", lambda e: e.memset(self.ones_col_bf.ap()[:, :], 1.0),
                   writes=[TAP(None, "ones_col", (0, 128, 0, 2))], tag="ones_c")
        self.p.add("v", lambda e: e.memset(self.ones128_bf.ap()[:, :], 1.0),
                   writes=[TAP(None, "ones128", (0, 128, 0, 128))], tag="ones_m")
        self.ones_row_f32_t = es.enter_context(nc.sbuf_tensor("ones_rf", [1, 128], F32))
        self.p.add("v", lambda e: e.memset(self.ones_row_f32_t.ap()[:, :], 1.0),
                   writes=[TAP(None, "ones_rf", (0, 1, 0, 128))], tag="ones_rf")
        self.ones_row_bf_t = es.enter_context(nc.sbuf_tensor("ones_rb", [1, 128], BF16))
        self.p.add("v", lambda e: e.memset(self.ones_row_bf_t.ap()[:, :], 1.0),
                   writes=[TAP(None, "ones_rb", (0, 1, 0, 128))], tag="ones_rb")
        self.ones512_t = es.enter_context(nc.sbuf_tensor("ones512", [1, 512], BF16))
        self.p.add("v", lambda e: e.memset(self.ones512_t.ap()[:, :], 1.0),
                   writes=[TAP(None, "ones512", (0, 1, 0, 512))], tag="ones512")
        self.ident_t = es.enter_context(nc.sbuf_tensor("ident", [128, 128], BF16))
        self.ident_tap = TAP(self.ident_t.ap()[:, :], "ident", (0, 128, 0, 128))
        self.rank_t = es.enter_context(nc.sbuf_tensor("rankt", [1, 2], I32))

    def ones_col(self):
        return TAP(self.ones_col_bf.ap()[:, 0:1], "ones_col", (0, 128, 0, 1))

    def ones_row_f32(self):
        return TAP(self.ones_row_f32_t.ap()[:, :], "ones_rf", (0, 1, 0, 128))

    def ones512(self):
        return TAP(self.ones512_t.ap()[:, :], "ones512", (0, 1, 0, 512))

    def ones_row_bf(self, *a):
        return TAP(self.ones_row_bf_t.ap()[:, :], "ones_rb", (0, 1, 0, 128))

    def rank_tap(self, side):
        return TAP(self.rank_t.ap()[0:1, side:side + 1], "rankt", (0, 1, side, side + 1))

    def ones128(self):
        return TAP(self.ones128_bf.ap()[:, :], "ones128", (0, 128, 0, 128))

    def din(self, name, shape, dtype=F32):
        h = self.nc.dram_tensor(name, list(shape), dtype, kind="ExternalInput")
        b = DBuf(h, name)
        self.inputs[name] = b
        return b

    def dinternal(self, name, shape, dtype=F32, shared_for=None):
        kw = {}
        if shared_for is not None:
            kw["addr_space"] = maybe_share_collective_output_space(shared_for, self.rg,
                                                                   prefer_local=True)
        h = self.nc.dram_tensor(name, list(shape), dtype, kind="Internal", **kw)
        return DBuf(h, name)

    def dout(self, name, shape, dtype=F32):
        h = self.nc.dram_tensor(name, list(shape), dtype, kind="ExternalOutput")
        return DBuf(h, name)

    def tap(self, name, fb, nch, width, dtype=F32):
        if name not in DEBUG_TAPS:
            return
        d = self.dout(f"dbg_{name}", [nch * 128, width], dtype)
        dr = d.handle.ap().rearrange("(c p) n -> p c n", p=128)
        for c in range(nch):
            self.p.add("y", (lambda e, c=c, a=fb.t(c).ap: e.dma_start(out=dr[:, c, :], in_=a)),
                       reads=[fb.t(c)], writes=[d.box((0, 1, c, c + 1))], kind="d",
                       tag=f"tap_{name}{c}")

    def load_weight_bf16(self, dram, E, e0=0, e1=None, tag=""):
        if e1 is None:
            e1 = E
        w = e1 - e0
        cb = self.slab.challoc(CH, w, BF16, tag or dram.name)
        dst = cb.buf.t()
        dstap = dst.ap.rearrange("p (c n) -> p c n", c=CH)
        src = dram.handle.ap().rearrange("(c p) n -> p c n", p=128)[:, :, e0:e1]
        self.p.add("g", (lambda e, o=dstap, a=src: e.dma_start(out=o, in_=a)),
                   reads=[dram.t()], writes=[dst], kind="d", tag=f"ldw_{dram.name}")
        return cb

    def load_col(self, dram, tag=""):
        b = self.slab.alloc(CH, F32, tag or ("col_" + dram.name))
        dst = b.t()
        src = dram.handle.ap().rearrange("(c p) -> p c", p=128)
        self.p.add("y", (lambda e, o=dst.ap, a=src:
                         e.dma_start(out=o, in_=a, allow_slow_non_contiguous=True)),
                   reads=[dram.t()], writes=[dst], kind="d", tag=f"ldc_{dram.name}")
        return b


class _PsBuf:

    def __init__(self, handle, name):
        self.handle = handle
        self.name = name

    def t(self, p0=0, p1=128, c0=0, c1=None):
        if c1 is None:
            c1 = self.handle.shape[1]
        ap = self.handle.ap()[p0:p1, c0:c1]
        return TAP(ap, self.name, (0, 128, 0, self.handle.shape[1]))



class State:
    def __init__(self, mk, xch):
        self.mk = mk
        self.x = xch
        self.x_spill = None
        self.l0f_cache = None


def ln_stats(mk, st, tag, want_nmr_bc=False):
    p, slab = mk.p, mk.slab
    x = st.x
    xbf = slab.falloc(CH, NCOL, BF16, f"xbf_{tag}")
    for c in range(CH):
        p.copy("v", xbf.t(c), x.t(c), tag=f"{tag}_xbf{c}")
    R_bc = slab.alloc(NCOL, F32, f"Rbc_{tag}")
    nmr_row = slab.alloc(NCOL, BF16, f"nmrrow_{tag}")
    nmr_bc = slab.alloc(NCOL, F32, f"nmrbc_{tag}") if want_nmr_bc else None
    x2 = slab.falloc(CH, 512, BF16, f"x2_{tag}")
    cinv = 1.0 / D
    for nt, (c0, c1) in enumerate(NT512):
        pa = mk.psF[0].t()
        pb = mk.psF[1].t()
        for c in range(CH):
            p.act(x2.t(c), xbf.t(c, c0, c1), AF.Square, tag=f"{tag}_x2_{nt}{c}")
        p.mmgroup([(mk.ones128(), xbf.t(c, c0, c1)) for c in range(CH)], pa,
                  tag=f"{tag}_Sx{nt}")
        p.mmgroup([(mk.ones128(), x2.t(c)) for c in range(CH)], pb,
                  tag=f"{tag}_Sxx{nt}")
        mu = slab.alloc(512, F32, "mu")
        te = slab.alloc(512, F32, "te")
        vpe = slab.alloc(512, F32, "vpe")
        r0 = slab.alloc(512, F32, "r0")
        p.ts("v", mu.t(), pa, cinv, ALU.mult, tag=f"{tag}_mu{nt}")
        p.tt("g", te.t(), mu.t(), mu.t(), ALU.mult, tag=f"{tag}_mu2{nt}")
        p.stt("v", vpe.t(), pb, cinv, te.t(), ALU.mult, ALU.subtract, tag=f"{tag}_var{nt}")
        p.ts("g", vpe.t(), vpe.t(), LN_EPS, ALU.add, tag=f"{tag}_vpe{nt}")
        p.act(te.t(), vpe.t(), AF.Ln, tag=f"{tag}_ln{nt}")
        p.act(r0.t(), te.t(), AF.Exp, scale=-0.5, tag=f"{tag}_rsq0{nt}")
        p.tt("g", te.t(), r0.t(), r0.t(), ALU.mult, tag=f"{tag}_nwa{nt}")
        p.tt("g", te.t(), te.t(), vpe.t(), ALU.mult, tag=f"{tag}_nwb{nt}")
        p.ts("v", te.t(), te.t(), -0.5, ALU.mult, 1.5, ALU.add, tag=f"{tag}_nwc{nt}")
        p.tt("v", R_bc.t(0, 128, c0, c1), r0.t(), te.t(), ALU.mult, tag=f"{tag}_R{nt}")
        p.stt("v", nmr_row.t(0, 1, c0, c1), mu.t(0, 1), -1.0, R_bc.t(0, 1, c0, c1),
              ALU.mult, ALU.mult, tag=f"{tag}_nmr{nt}")
        if want_nmr_bc:
            p.stt("v", nmr_bc.t(0, 128, c0, c1), mu.t(), -1.0, R_bc.t(0, 128, c0, c1),
                  ALU.mult, ALU.mult, tag=f"{tag}_nmrb{nt}")
        for b in (mu, te, vpe, r0):
            b.free()
    x2.free()
    return {"R_bc": R_bc, "nmr_row": nmr_row, "xbf": xbf, "nmr_bc": nmr_bc}


def spill_x(mk, st, tag):
    p = mk.p
    sp = st.x_spill
    spr = sp.handle.ap().rearrange("(c p) n -> p c n", p=128)
    for c in range(CH):
        p.add("y", (lambda e, c=c, a=st.x.t(c).ap: e.dma_start(out=spr[:, c, :], in_=a)),
              reads=[st.x.t(c)], writes=[sp.box((0, 1, c, c + 1))], kind="d",
              tag=f"{tag}_spill{c}")


def restream_x_chunk(mk, st, c, dstbuf, tag):
    p = mk.p
    spr = st.x_spill.handle.ap().rearrange("(c p) n -> p c n", p=128)
    p.add("y", (lambda e, o=dstbuf.t().ap, a=spr[:, c, :]: e.dma_start(out=o, in_=a)),
          reads=[st.x_spill.box((0, 1, c, c + 1))], writes=[dstbuf.t()],
          kind="d", tag=f"{tag}_restr{c}")


def load_row_bf(mk, dram, E, tag):
    p, slab = mk.p, mk.slab
    r32 = slab.alloc(E, F32, f"lr_{tag}")
    p.add("y", (lambda e, o=r32.t(0, 1).ap, a=dram.handle.ap()[None, :]:
                e.dma_start(out=o, in_=a)),
          reads=[dram.t()], writes=[r32.t(0, 1)], kind="d", tag=f"{tag}_ld")
    rb = slab.alloc(E, BF16, f"lrb_{tag}")
    p.copy("v", rb.t(0, 1), r32.t(0, 1), tag=f"{tag}_cast")
    r32.free()
    return rb


def proj_cs_row(mk, wcb, E, gcol_bf=None, tag=""):
    p, slab = mk.p, mk.slab
    csrow = slab.alloc(E, BF16, f"cs_{tag}")
    for s in range(E // 512):
        ps = mk.psF[2].t(0, 1, 0, 512)
        if gcol_bf is None:
            pairs = [(mk.ones_col(), wcb.t(c, s * 512, (s + 1) * 512)) for c in range(CH)]
        else:
            pairs = [(gcol_bf.t(0, 128, c, c + 1), wcb.t(c, s * 512, (s + 1) * 512))
                     for c in range(CH)]
        p.mmgroup(pairs, ps, tag=f"{tag}_cs{s}")
        p.copy("v", csrow.t(0, 1, s * 512, (s + 1) * 512), ps, tag=f"{tag}_csev{s}")
    return csrow



def sub_l0_ffn(mk, st, WD, first):
    p, slab = mk.p, mk.slab
    tag = "f1" if first else "f2"
    w_w = {g: mk.load_weight_bf16(WD[f"l0f_w{g}"], D, tag=f"{tag}_w{g}") for g in "fic"}
    ln = ln_stats(mk, st, tag)
    ln["xbf"].free()
    xs = slab.falloc(CH, NCOL, BF16, f"xs_{tag}")
    for c in range(CH):
        p.stt("v", xs.t(c), st.x.t(c), 1.0, ln["R_bc"].t(), ALU.mult, ALU.mult,
              tag=f"{tag}_xs{c}")
    mk.tap(f"{tag}_xs", xs, CH, NCOL, BF16)
    spill_x(mk, st, tag)
    st.x.free()
    ln["R_bc"].free()
    nmr = ln["nmr_row"]

    w_u = {g: mk.load_weight_bf16(WD[f"l0f_u{g}"], D, tag=f"{tag}_u{g}") for g in "fic"}
    cs = {g: proj_cs_row(mk, w_w[g], D, tag=f"{tag}_cs{g}") for g in "fic"}
    bcol = {g: mk.load_col(WD[f"l0f_b{g}"], tag=f"{tag}_b{g}") for g in "fico"}
    inc_col = mk.load_col(WD["l0f_inc"], tag=f"{tag}_inc")
    inc_bf = slab.alloc(CH, BF16, f"incbf_{tag}")
    p.copy("v", inc_bf.t(), inc_col.t(), tag=f"{tag}_incbf")

    g0b = {}
    for g in "fic":
        col = slab.alloc(CH, F32, f"{tag}_g0b{g}")
        for mj in range(CH):
            ps = mk.psF[3].t(0, 128, 0, 1)
            p.mmgroup([(w_u[g].t(c, mj * 128, (mj + 1) * 128), inc_bf.t(0, 128, c, c + 1))
                       for c in range(CH)], ps, tag=f"{tag}_g0b{g}{mj}")
            p.copy("v", col.t(0, 128, mj, mj + 1), ps, tag=f"{tag}_g0bev{g}{mj}")
        p.tt("v", col.t(), col.t(), bcol[g].t(), ALU.add, tag=f"{tag}_g0bb{g}")
        g0b[g] = col

    cbf = slab.falloc(CH, SL, BF16, f"cbf_{tag}")
    cbf2 = slab.falloc(CH, SL, BF16, f"cbf2_{tag}")

    for tstep in range(2):
        base = tstep * SL
        for mj in range(CH):
            for half in range(SL // 512):
                c0, c1 = half * 512, (half + 1) * 512
                banks = [mk.psF[i] for i in ((0, 1, 2) if (mj + half) % 2 == 0 else (3, 4, 5))]
                gps = {}
                for gi, g in enumerate("fic"):
                    pairs = [(w_w[g].t(c, mj * 128, (mj + 1) * 128), xs.t(c, base + c0, base + c1))
                             for c in range(CH)]
                    if tstep == 1:
                        pairs += [(w_u[g].t(c, mj * 128, (mj + 1) * 128), cbf.t(c, c0, c1))
                                  for c in range(CH)]
                    pairs.append((cs[g].t(0, 1, mj * 128, (mj + 1) * 128),
                                  nmr.t(0, 1, base + c0, base + c1)))
                    ps = banks[gi].t(0, 128, 0, 512)
                    p.mmgroup(pairs, ps, tag=f"{tag}_t{tstep}{g}m{mj}h{half}")
                    gps[g] = ps
                bsrc = g0b if tstep == 0 else bcol
                tf = slab.alloc(512, F32, "tf")
                ti = slab.alloc(512, F32, "ti")
                tg = slab.alloc(512, F32, "tg")
                p.act(tf.t(), gps["f"], AF.Sigmoid, bias=bsrc["f"].t(0, 128, mj, mj + 1),
                      tag=f"{tag}_sf{tstep}{mj}{half}")
                p.act(ti.t(), gps["i"], AF.Sigmoid, bias=bsrc["i"].t(0, 128, mj, mj + 1),
                      tag=f"{tag}_si{tstep}{mj}{half}")
                p.act(tg.t(), gps["c"], AF.Tanh, bias=bsrc["c"].t(0, 128, mj, mj + 1),
                      tag=f"{tag}_tg{tstep}{mj}{half}")
                pr = slab.alloc(512, F32, "pr")
                p.tt("v", pr.t(), ti.t(), tg.t(), ALU.mult, tag=f"{tag}_ig{tstep}{mj}{half}")
                if tstep == 0:
                    p.stt("v", cbf.t(mj, c0, c1), tf.t(), inc_col.t(0, 128, mj, mj + 1), pr.t(),
                          ALU.mult, ALU.add, tag=f"{tag}_c1{mj}{half}")
                else:
                    p.tt("v", tf.t(), tf.t(), cbf.t(mj, c0, c1), ALU.mult, tag=f"{tag}_fc{mj}{half}")
                    p.tt("v", cbf2.t(mj, c0, c1), tf.t(), pr.t(), ALU.add, tag=f"{tag}_c2{mj}{half}")
                for b in (tf, ti, tg, pr):
                    b.free()
                mk.pump()
        if tstep == 0:
            mk.tap(f"{tag}_c1", cbf, CH, SL, BF16)

    mk.tap(f"{tag}_cbf", cbf2, CH, SL, BF16)
    cbf.free()
    cbf = cbf2
    for g in "fic":
        w_u[g].free()
        cs[g].free()
        g0b[g].free()
    w_w["f"].free()
    w_w["i"].free()
    w_w["c"].free()

    w_wo = mk.load_weight_bf16(WD["l0f_wo"], D, tag=f"{tag}_wo")
    w_uo = mk.load_weight_bf16(WD["l0f_uo"], D, tag=f"{tag}_uo")
    cs_o = proj_cs_row(mk, w_wo, D, tag=f"{tag}_cso")
    xnew = slab.falloc(CH, NCOL, F32, f"xnew_{tag}")
    for mj in range(CH):
        xold = slab.alloc(NCOL, F32, f"xold{mj}")
        restream_x_chunk(mk, st, mj, xold, tag)
        for nt, (c0, c1) in enumerate(NT512):
            pairs = [(w_wo.t(c, mj * 128, (mj + 1) * 128), xs.t(c, c0, c1)) for c in range(CH)]
            cc0 = c0 % SL
            pairs += [(w_uo.t(c, mj * 128, (mj + 1) * 128), cbf.t(c, cc0, cc0 + 512))
                      for c in range(CH)]
            pairs.append((cs_o.t(0, 1, mj * 128, (mj + 1) * 128), nmr.t(0, 1, c0, c1)))
            ps = mk.psF[(2 * mj + nt) % 6].t(0, 128, 0, 512)
            p.mmgroup(pairs, ps, tag=f"{tag}_o{mj}{nt}")
            th = slab.alloc(512, F32, "th")
            p.act(th.t(), ps, AF.Tanh, bias=bcol["o"].t(0, 128, mj, mj + 1),
                  tag=f"{tag}_oth{mj}{nt}")
            p.stt("v", xnew.t(mj, c0, c1), th.t(), 0.5, xold.t(0, 128, c0, c1),
                  ALU.mult, ALU.add, tag=f"{tag}_res{mj}{nt}")
            th.free()
            mk.pump()
        xold.free()
    for b in (w_wo, w_uo, cs_o, xs, cbf, nmr, inc_col, inc_bf):
        b.free()
    for g in "fico":
        bcol[g].free()
    st.x = xnew


def matcol(mk, wcb, rhs_col_bf, extra_col=None, tag=""):
    p, slab = mk.p, mk.slab
    col = slab.alloc(CH, F32, f"mc_{tag}")
    ps = mk.psF[2]
    for mj in range(CH):
        pt = ps.t(0, 128, mj, mj + 1)
        p.mmgroup([(wcb.t(c, mj * 128, (mj + 1) * 128), rhs_col_bf.t(0, 128, c, c + 1))
                   for c in range(CH)], pt, tag=f"{tag}_mc{mj}")
    p.copy("v", col.t(), ps.t(0, 128, 0, CH), tag=f"{tag}_mcev")
    if extra_col is not None:
        p.tt("v", col.t(), col.t(), extra_col.t(), ALU.add, tag=f"{tag}_mcb")
    return col


def proj_fm(mk, src_bf, wcb, E, bias_col, evac, tag, cs_row=None, nmr=None,
            ncols=NCOL, extra_pairs=None, bias_row=None):
    p = mk.p
    nnt = ncols // 512
    for mj in range(E // 128):
        for nt in range(nnt):
            c0, c1 = nt * 512, (nt + 1) * 512
            pairs = [(wcb.t(c, mj * 128, (mj + 1) * 128), src_bf.t(c, c0, c1))
                     for c in range(CH)]
            if extra_pairs is not None:
                pairs += extra_pairs(mj, c0, c1)
            if cs_row is not None:
                pairs.append((cs_row.t(0, 1, mj * 128, (mj + 1) * 128), nmr.t(0, 1, c0, c1)))
            if bias_row is not None:
                pairs.append((bias_row.t(0, 1, mj * 128, (mj + 1) * 128), mk.ones512()))
            ps = mk.psF[(mj * nnt + nt) % 6].t()
            p.mmgroup(pairs, ps, tag=f"{tag}_p{mj}_{nt}")
            bias = bias_col.t(0, 128, mj, mj + 1) if bias_col is not None else None
            evac(mj, c0, c1, ps, bias)
            mk.pump()


def sub_l0_attn(mk, st, WD):
    p, slab = mk.p, mk.slab
    tag = "a0"
    w_k = mk.load_weight_bf16(WD["l0a_wk"], D, tag=f"{tag}_wk")
    gamma_col = mk.load_col(WD["l0a_gamma"], tag=f"{tag}_g")
    beta_col = mk.load_col(WD["l0a_beta"], tag=f"{tag}_b")
    bk_col = mk.load_col(WD["l0a_bk"], tag=f"{tag}_bk")
    bka_col = mk.load_col(WD["l0a_bka"], tag=f"{tag}_bka")
    bv_col = mk.load_col(WD["l0a_bv"], tag=f"{tag}_bv")
    ln = ln_stats(mk, st, tag)
    ln["xbf"].free()
    gbf = slab.alloc(CH, BF16, f"{tag}_gbf")
    bbf = slab.alloc(CH, BF16, f"{tag}_bbf")
    p.copy("v", gbf.t(), gamma_col.t(), tag=f"{tag}_gbf")
    p.copy("v", bbf.t(), beta_col.t(), tag=f"{tag}_bbf")
    xg = slab.falloc(CH, NCOL, BF16, f"xg_{tag}")
    for c in range(CH):
        p.stt("v", xg.t(c), st.x.t(c), gamma_col.t(0, 128, c, c + 1), ln["R_bc"].t(),
              ALU.mult, ALU.mult, tag=f"{tag}_xg{c}")
    spill_x(mk, st, tag)
    st.x.free()
    ln["R_bc"].free()
    nmr = ln["nmr_row"]

    cs_k = proj_cs_row(mk, w_k, D, gcol_bf=gbf, tag=f"{tag}_csk")
    br_k = bias_row_bf(mk, w_k, D, bbf, WD["l0a_bk"], f"{tag}_brk")
    keys = slab.falloc(CH, NCOL, BF16, f"keys_{tag}")

    def ev_keys(mj, c0, c1, ps, bias):
        p.copy("s", keys.t(mj, c0, c1), ps, tag=f"{tag}_evk{mj}")
    proj_fm(mk, xg, w_k, D, None, ev_keys, f"{tag}_k", cs_row=cs_k, nmr=nmr,
            bias_row=br_k)
    w_k.free()
    cs_k.free()
    br_k.free()

    w_v = mk.load_weight_bf16(WD["l0a_wv"], D, tag=f"{tag}_wv")
    cs_v = proj_cs_row(mk, w_v, D, gcol_bf=gbf, tag=f"{tag}_csv")
    br_v = bias_row_bf(mk, w_v, D, bbf, WD["l0a_bv"], f"{tag}_brv")
    values = slab.falloc(CH, NCOL, BF16, f"vals_{tag}")

    def ev_vals(mj, c0, c1, ps, bias):
        p.copy("s", values.t(mj, c0, c1), ps, tag=f"{tag}_evv{mj}")
    proj_fm(mk, xg, w_v, D, None, ev_vals, f"{tag}_v", cs_row=cs_v, nmr=nmr,
            bias_row=br_v)
    w_v.free()
    cs_v.free()
    br_v.free()
    xg.free()
    nmr.free()
    v_spill = mk.dinternal(f"v_spill", [D, NCOL], BF16)
    vsp = v_spill.handle.ap().rearrange("(c p) n -> p c n", p=128)
    for c in range(CH):
        p.add("y", (lambda e, c=c, a=values.t(c).ap: e.dma_start(out=vsp[:, c, :], in_=a)),
              reads=[values.t(c)], writes=[v_spill.box((0, 1, c, c + 1))], kind="d",
              tag=f"{tag}_vsp{c}")
    values.free()

    w_ka = mk.load_weight_bf16(WD["l0a_wka"], D, tag=f"{tag}_wka")
    br_ka = load_row_bf(mk, WD["l0a_bka"], D, f"{tag}_brka")
    katt = slab.falloc(CH, NCOL, BF16, f"katt_{tag}")

    def ev_katt(mj, c0, c1, ps, bias):
        p.copy("s", katt.t(mj, c0, c1), ps, tag=f"{tag}_evka{mj}")
    proj_fm(mk, keys, w_ka, D, None, ev_katt, f"{tag}_ka", bias_row=br_ka)
    w_ka.free()
    br_ka.free()

    A = slab.falloc(CH, NCOL, F32, f"A_{tag}")
    T1 = slab.alloc(16, F32, f"T1_{tag}")
    tmp2 = [slab.alloc(NCOL, F32, f"tmpa{i}") for i in range(2)]
    for c in range(CH):
        tm = tmp2[c % 2]
        p.act(tm.t(), katt.t(c), AF.Exp, scale=1.0 / 32.0, tag=f"{tag}_kw1_{c}")
        for b in range(NB):
            p.scan_add("v", A.t(c, b * SL, (b + 1) * SL), tm.t(0, 128, b * SL, (b + 1) * SL),
                       tag=f"{tag}_sc1_{c}{b}")
            p.copy("v", T1.t(0, 128, b * CH + c, b * CH + c + 1),
                   A.t(c, (b + 1) * SL - 1, (b + 1) * SL), tag=f"{tag}_t1_{c}{b}")
            mk.pump()

    carry1 = carry_exchange(mk, T1, f"{tag}_x1")
    T1.free()

    T2 = slab.alloc(16, F32, f"T2_{tag}")
    zt = slab.alloc(NCOL, F32, "zshared")
    for c in range(CH):
        tm = tmp2[c % 2]
        for b in range(NB):
            p.ts("v", A.t(c, b * SL, (b + 1) * SL), A.t(c, b * SL, (b + 1) * SL),
                 carry1.t(0, 128, b * CH + c, b * CH + c + 1), ALU.add, tag=f"{tag}_Ac{c}{b}")
        p.act(tm.t(), A.t(c), AF.Ln, tag=f"{tag}_lnA{c}")
        p.stt("v", tm.t(), katt.t(c), 1.0 / 32.0, tm.t(), ALU.mult, ALU.subtract,
              tag=f"{tag}_dv{c}")
        p.act(tm.t(), tm.t(), AF.Exp, tag=f"{tag}_kw2_{c}")
        p.tt("v", zt.t(), tm.t(), keys.t(c), ALU.mult, tag=f"{tag}_z{c}")
        for b in range(NB):
            p.scan_add("v", A.t(c, b * SL, (b + 1) * SL), zt.t(0, 128, b * SL, (b + 1) * SL),
                       tag=f"{tag}_sc2_{c}{b}")
            p.copy("v", T2.t(0, 128, b * CH + c, b * CH + c + 1),
                   A.t(c, (b + 1) * SL - 1, (b + 1) * SL), tag=f"{tag}_t2_{c}{b}")
            mk.pump()
    zt.free()
    katt.free()
    keys.free()
    for tmq in tmp2:
        tmq.free()

    carry2 = carry_exchange(mk, T2, f"{tag}_x2")
    T2.free()

    xnew = slab.falloc(CH, NCOL, F32, f"xnew_{tag}")
    values = slab.falloc(CH, NCOL, BF16, f"vals2_{tag}")
    for c in range(CH):
        p.add("y", (lambda e, c=c, o=values.t(c).ap: e.dma_start(out=o, in_=vsp[:, c, :])),
              reads=[v_spill.box((0, 1, c, c + 1))], writes=[values.t(c)], kind="d",
              tag=f"{tag}_vld{c}")
        xold = slab.alloc(NCOL, F32, f"xoldA{c}")
        restream_x_chunk(mk, st, c, xold, tag)
        for b in range(NB):
            p.ts("v", A.t(c, b * SL, (b + 1) * SL), A.t(c, b * SL, (b + 1) * SL),
                 carry2.t(0, 128, b * CH + c, b * CH + c + 1), ALU.add, tag=f"{tag}_Pc{c}{b}")
        p.tt("v", A.t(c), A.t(c), values.t(c), ALU.mult, tag=f"{tag}_pv{c}")
        p.tt("v", xnew.t(c), A.t(c), xold.t(), ALU.add, tag=f"{tag}_out{c}")
        mk.pump()
        xold.free()
    A.free()
    values.free()
    carry1.free()
    carry2.free()
    for b in (gamma_col, beta_col, bk_col, bka_col, bv_col, gbf, bbf):
        b.free()
    st.x = xnew


def carry_exchange(mk, Tcol, tag):
    p, slab = mk.p, mk.slab
    cin = mk.dinternal(f"cc{mk._cc_ctr}_in", [128 * 16], F32)
    cout = mk.dinternal(f"cc{mk._cc_ctr}_out", [NCORES * 128 * 16], F32,
                        shared_for="AllGather")
    mk._cc_ctr += 1
    cin_ap = cin.handle.ap().rearrange("(p j) -> p j", p=128)
    p.add("g", (lambda e, o=cin_ap, a=Tcol.t().ap: e.dma_start(out=o, in_=a)),
          reads=[Tcol.t()], writes=[cin.t()], kind="d", tag=f"{tag}_st")
    p.coll("AllGather", ALU.bypass, cin.t(), cout.t(), mk.rg, tag=f"{tag}_ag")
    tall = slab.alloc(NCORES * 16 * 16, F32, f"tall_{tag}")
    tall_tap = TAP(tall.slab.handle.ap()[0:8, tall.boff:tall.boff + 2048 * 4].bitcast(F32),
                   tall.slab.name, (0, 8, tall.boff, tall.boff + 2048 * 4))
    cout_ap = cout.handle.ap().rearrange("(r n) -> r n", r=NCORES)
    p.add("g", (lambda e, o=tall_tap.ap, a=cout_ap: e.dma_start(out=o, in_=a)),
          reads=[cout.t()], writes=[tall_tap], kind="d", tag=f"{tag}_ld")
    carry = slab.alloc(16, F32, f"carry_{tag}")
    ps = mk.psF[3]
    tall3 = tall_tap.ap.rearrange("r (pp j) -> r j pp", j=16)
    for j in range(16):
        p.mmgroup([(TAP(tall3[:, j, :], tall_tap.name, tall_tap.box), mk.pfx_tap)],
                  ps.t(0, 128, j, j + 1), tag=f"{tag}_car{j}")
    p.copy("v", carry.t(), ps.t(0, 128, 0, 16), tag=f"{tag}_carev")
    tall.free()
    return carry



def gmax_exchange(mk, col_tap, k, tag):
    p, slab = mk.p, mk.slab
    cin = mk.dinternal(f"cc{mk._cc_ctr}_in", [128 * k], F32)
    cout = mk.dinternal(f"cc{mk._cc_ctr}_out", [NCORES * 128 * k], F32,
                        shared_for="AllGather")
    mk._cc_ctr += 1
    cin_ap = cin.handle.ap().rearrange("(p j) -> p j", p=128)
    p.add("g", (lambda e, o=cin_ap, a=col_tap.ap: e.dma_start(out=o, in_=a)),
          reads=[col_tap], writes=[cin.t()], kind="d", tag=f"{tag}_st")
    p.coll("AllGather", ALU.bypass, cin.t(), cout.t(), mk.rg, tag=f"{tag}_ag")
    row = slab.alloc(NCORES * 128 * k, F32, f"gmr_{tag}")
    p.add("g", (lambda e, o=row.t(0, 1).ap, a=cout.handle.ap()[None, :]:
                e.dma_start(out=o, in_=a)),
          reads=[cout.t()], writes=[row.t(0, 1)], kind="d", tag=f"{tag}_ld")
    out = slab.alloc(k, F32, f"gmx_{tag}")
    r3 = row.t(0, 1).ap.rearrange("o (r p j) -> o j (r p)", r=NCORES, j=k)
    p.add("v", (lambda e, o=out.t(0, 1).ap, a=r3:
                e.tensor_reduce(out=o, in_=a, axis=AX, op=ALU.max)),
          reads=[row.t(0, 1)], writes=[out.t(0, 1)], tag=f"{tag}_red")
    row.free()
    return out


def scale_cols(mk, gmax_row, k, qmax, tag):
    p, slab = mk.p, mk.slab
    srow = slab.alloc(2 * k, F32, f"srow_{tag}")
    p.ts("v", srow.t(0, 1, 0, k), gmax_row.t(0, 1), 1.0 / qmax, ALU.mult, 1e-12, ALU.max,
         tag=f"{tag}_s")
    p.add("v", (lambda e, o=srow.t(0, 1, k, 2 * k).ap, a=srow.t(0, 1, 0, k).ap:
                e.reciprocal(out=o, in_=a)),
          reads=[srow.t(0, 1, 0, k)], writes=[srow.t(0, 1, k, 2 * k)], tag=f"{tag}_rs")
    ps = mk.psF[4].t(0, 128, 0, 2 * k)
    p.mmgroup([(mk.ones_row_f32(), srow.t(0, 1))], ps, tag=f"{tag}_bc")
    scol = slab.alloc(2 * k, F32, f"scol_{tag}")
    p.copy("v", scol.t(), ps, tag=f"{tag}_scev")
    srow.free()
    return scol


RK = 32768.0

def roundq(mk, out, in_, rs, s, tag, ibuf=None):
    p, slab = mk.p, mk.slab
    w = in_.box[3] - in_.box[2]
    own = ibuf is None
    if own:
        ibuf = slab.alloc((in_.box[3] - in_.box[2]) // 4 + 8, I32, f"ri_{tag}")
    it = ibuf.t(0, 128, 0, (in_.box[3] - in_.box[2]) // DT_SIZE[F32])
    p.ts("v", it, in_, rs, ALU.mult, RK + 0.5, ALU.add, tag=f"{tag}_ri")
    p.ts("v", out, it, -RK, ALU.add, s, ALU.mult, tag=f"{tag}_rq")
    if own:
        ibuf.free()


def sub_l1_prep_gen(mk, st, WD):
    p, slab = mk.p, mk.slab
    mk.dq = {}
    bufs = {
        "up": (WD["l1f_w_up"], 3072, 1024),
        "gate": (WD["l1f_w_gate"], 3072, 1024),
        "down": (WD["l1f_w_down"], 1024, 3072),
    }
    W = 512
    NBW = W // 32
    wtile = [slab.alloc(W, F32, f"dqw{i}") for i in range(2)]
    t1 = [slab.alloc(W, F32, "dqt0")] * 2
    fb = [slab.alloc(W, I32, "dqf0")] * 2
    ob = [slab.alloc(W, BF16, "dqo0")] * 2
    mxb = slab.alloc(2 * NBW, F32, "dqmx")
    sb = slab.alloc(2 * NBW, F32, "dqs")
    rb = slab.alloc(2 * NBW, F32, "dqr")
    mk._dq_transients = wtile + [t1[0], fb[0], ob[0], mxb, sb, rb]
    for nm, (dram, O, I) in bufs.items():
        dq = mk.dinternal(f"dq_{nm}", [O, I], BF16)
        mk.dq[nm] = dq
        src_ap = dram.handle.ap().rearrange("(c p) n -> p c n", p=128)
        dst_ap = dq.handle.ap().rearrange("(c p) n -> p c n", p=128)
        idx = 0
        for oc in range(O // 128):
            for isl in range(I // W):
                j = idx % 2
                idx += 1
                wt, tt1, tfb, tob = wtile[j], t1[j], fb[j], ob[j]
                mxs = mxb.t(0, 128, j * NBW, (j + 1) * NBW)
                sbs = sb.t(0, 128, j * NBW, (j + 1) * NBW)
                rbs = rb.t(0, 128, j * NBW, (j + 1) * NBW)
                i0 = isl * W
                p.add("y", (lambda e, o=wt.t().ap, a=src_ap[:, oc, i0:i0 + W]:
                            e.dma_start(out=o, in_=a)),
                      reads=[dram.t()], writes=[wt.t()], kind="d",
                      tag=f"dq_{nm}_ld{oc}_{isl}")
                w3 = wt.t().ap.rearrange("p (g b) -> p g b", b=32)
                p.add("v", (lambda e, o=mxs.ap, a=w3:
                            e.tensor_reduce(out=o, in_=a, axis=AX, op=ALU.max,
                                            apply_absolute_value=True)),
                      reads=[wt.t()], writes=[mxs], tag=f"dq_{nm}_mx{oc}_{isl}")
                p.ts("g", sbs, mxs, 1.0 / 7.0, ALU.mult, 1e-12, ALU.max,
                     tag=f"dq_{nm}_s{oc}_{isl}")
                p.add("v", (lambda e, o=rbs.ap, a=sbs.ap: e.reciprocal(out=o, in_=a)),
                      reads=[sbs], writes=[rbs], tag=f"dq_{nm}_r{oc}_{isl}")
                r3 = rbs.ap.unsqueeze(2).broadcast_to([128, NBW, 32])
                s3 = sbs.ap.unsqueeze(2).broadcast_to([128, NBW, 32])
                t13 = tt1.t().ap.rearrange("p (g b) -> p g b", b=32)
                p.add("g", (lambda e, o=t13, a=w3, b=r3:
                            e.tensor_tensor(out=o, in0=a, in1=b, op=ALU.mult)),
                      reads=[wt.t(), rbs], writes=[tt1.t()], tag=f"dq_{nm}_t{oc}_{isl}")
                p.ts("g", tt1.t(), tt1.t(), 32768.5, ALU.add, tag=f"dq_{nm}_z{oc}_{isl}")
                p.copy("v", tfb.t(), tt1.t(), tag=f"dq_{nm}_i{oc}_{isl}")
                p.tt("v", wt.t(), tfb.t(), tt1.t(), ALU.is_gt, tag=f"dq_{nm}_g{oc}_{isl}")
                p.tt("v", tt1.t(), tfb.t(), wt.t(), ALU.subtract, tag=f"dq_{nm}_s{oc}_{isl}")
                p.ts("g", tt1.t(), tt1.t(), -32768.0, ALU.add, tag=f"dq_{nm}_q{oc}_{isl}")
                ob3 = tob.t().ap.rearrange("p (g b) -> p g b", b=32)
                p.add("g", (lambda e, o=ob3, a=t13, b=s3:
                            e.tensor_tensor(out=o, in0=a, in1=b, op=ALU.mult)),
                      reads=[tt1.t(), sbs], writes=[tob.t()], tag=f"dq_{nm}_o{oc}_{isl}")
                p.add("g", (lambda e, o=dst_ap[:, oc, i0:i0 + W], a=tob.t().ap:
                            e.dma_start(out=o, in_=a)),
                      reads=[tob.t()], writes=[dq.box((0, 1, oc * 8 + isl, oc * 8 + isl + 1))],
                      kind="d", tag=f"dq_{nm}_st{oc}_{isl}")
                yield


def load_wT_slice(mk, dq, rows0, rows1, kchunk, tag):
    p, slab = mk.p, mk.slab
    w = rows1 - rows0
    b = slab.alloc(w, BF16, f"wt_{tag}")
    srcap = dq.handle.ap()[rows0:rows1, kchunk * 128:(kchunk + 1) * 128]
    p.add("y", (lambda e, o=b.t().ap, a=srcap: e.dma_start(out=o, in_=a, transpose=True)),
          reads=[dq.t()], writes=[b.t()], kind="d", tag=f"ldT_{tag}")
    return b


def sub_l1_ffn(mk, st, WD, first):
    p, slab = mk.p, mk.slab
    tag = "q1" if first else "q2"
    ln = ln_stats(mk, st, tag, want_nmr_bc=True)
    ln["xbf"].free()
    nmr_bc = ln["nmr_bc"]
    xn = slab.falloc(CH, NCOL, F32, f"xn_{tag}")
    mxc = slab.alloc(CH, F32, f"mxc_{tag}")
    for c in range(CH):
        p.stt("v", xn.t(c), st.x.t(c), 1.0, ln["R_bc"].t(), ALU.mult, ALU.mult,
              tag=f"{tag}_xr{c}")
        p.tt("g", xn.t(c), xn.t(c), nmr_bc.t(), ALU.add, tag=f"{tag}_xn{c}")
        p.red("v", mxc.t(0, 128, c, c + 1), xn.t(c), ALU.max, absval=True,
              tag=f"{tag}_mx{c}")
    spill_x(mk, st, tag)
    st.x.free()
    ln["R_bc"].free()
    nmr_bc.free()
    ln["nmr_row"].free()
    mxall = slab.alloc(1, F32, f"mxall_{tag}")
    p.red("v", mxall.t(), mxc.t(), ALU.max, tag=f"{tag}_mxall")
    gmx = gmax_exchange(mk, mxall.t(), 1, f"{tag}_b0")
    scol_x = scale_cols(mk, gmx, 1, 32767.0, f"{tag}_sx")
    gmx.free()
    mxall.free()
    mxc.free()
    qx = slab.falloc(CH, NCOL, BF16, f"qx_{tag}")
    qi = slab.alloc(NCOL, I32, "qi")
    for c in range(CH):
        p.ts("v", qi.t(), xn.t(c), scol_x.t(0, 128, 1, 2), ALU.mult, RK + 0.5, ALU.add,
             tag=f"{tag}_qa{c}")
        p.ts("v", qx.t(c), qi.t(), -RK, ALU.add, scol_x.t(0, 128, 0, 1), ALU.mult,
             tag=f"{tag}_qx{c}")
        xn.bufs[c].free()
    qi.free()
    xn.bufs = []
    scol_x.free()

    E = 3072
    gate = slab.falloc(24, NCOL, BF16, f"gate_{tag}")
    up_spill = mk.dinternal(f"up_spill_{tag}", [E, NCOL], BF16)
    ups_ap = up_spill.handle.ap().rearrange("(c p) n -> p c n", p=128)
    upmx = slab.alloc(96, F32, f"upmx_{tag}")
    gmxc = slab.alloc(96, F32, f"gmxc_{tag}")
    uptmp = [slab.alloc(NCOL, BF16, f"uptmp{i}") for i in range(2)]
    for nm, mxt in (("up", upmx), ("gate", gmxc)):
        dq = mk.dq[nm]
        for mq in range(4):
            r0, r1 = mq * 768, (mq + 1) * 768
            wts = [load_wT_slice(mk, dq, r0, r1, c, f"{tag}_{nm}{mq}_{c}") for c in range(CH)]
            for mloc in range(6):
                mj = mq * 6 + mloc
                dstt = gate.t if nm == "gate" else uptmp[mj % 2].t
                dst = (lambda c0, c1: gate.t(mj, c0, c1)) if nm == "gate" else                       (lambda c0, c1: uptmp[mj % 2].t(0, 128, c0, c1))
                for nt, (c0, c1) in enumerate(NT512):
                    pairs = [(wts[c].t(0, 128, mloc * 128, (mloc + 1) * 128),
                              qx.t(c, c0, c1)) for c in range(CH)]
                    ps = mk.psF[(mj * 4 + nt) % 6].t()
                    p.mmgroup(pairs, ps, tag=f"{tag}_{nm}mm{mj}_{nt}")
                    p.copy("s", dst(c0, c1), ps, tag=f"{tag}_{nm}ev{mj}_{nt}")
                    p.red("v", mxt.t(0, 128, mj * 4 + nt, mj * 4 + nt + 1),
                          dst(c0, c1), ALU.max, absval=True,
                          tag=f"{tag}_{nm}mx{mj}_{nt}")
                if nm == "up":
                    p.add("y", (lambda e, mj=mj, a=uptmp[mj % 2].t().ap:
                                e.dma_start(out=ups_ap[:, mj, :], in_=a)),
                          reads=[uptmp[mj % 2].t()],
                          writes=[up_spill.box((0, 1, mj, mj + 1))], kind="d",
                          tag=f"{tag}_upsp{mj}")
            for w in wts:
                w.free()
    for b in uptmp:
        b.free()
    qx.free()
    mx2 = slab.alloc(2, F32, f"mx2_{tag}")
    p.red("v", mx2.t(0, 128, 0, 1), upmx.t(), ALU.max, tag=f"{tag}_mxu")
    p.red("v", mx2.t(0, 128, 1, 2), gmxc.t(), ALU.max, tag=f"{tag}_mxg")
    upmx.free()
    gmxc.free()
    gmx1 = gmax_exchange(mk, mx2.t(), 2, f"{tag}_b1")
    scol1 = scale_cols(mk, gmx1, 2, 32767.0, f"{tag}_s1")
    gmx1.free()
    mx2.free()
    s_up = lambda: scol1.t(0, 128, 0, 1)
    s_g = lambda: scol1.t(0, 128, 1, 2)
    rs_up = lambda: scol1.t(0, 128, 2, 3)
    rs_g = lambda: scol1.t(0, 128, 3, 4)
    csg = slab.alloc(1, F32, f"csg_{tag}")
    p.ts("v", csg.t(), s_g(), 1.0 / 65536.0, ALU.mult, tag=f"{tag}_csg")
    m32sg = slab.alloc(1, F32, f"m32sg_{tag}")
    p.ts("v", m32sg.t(), s_g(), -RK, ALU.mult, tag=f"{tag}_m32sg")

    tmxc = slab.alloc(24, F32, f"tmx_{tag}")
    bi1 = slab.alloc(NCOL, I32, "bi1")
    bi2 = slab.alloc(NCOL, I32, "bi2")
    bd1 = slab.alloc(NCOL, F32, "bd1")
    bd2 = slab.alloc(NCOL, F32, "bd2")
    for mj in range(24):
        p.ts("v", bi1.t(), gate.t(mj), rs_g(), ALU.mult, RK + 0.5, ALU.add,
             tag=f"{tag}_Ba{mj}")
        p.act(bd2.t(), bi1.t(), AF.Sigmoid, scale=s_g(), bias=m32sg.t(),
              tag=f"{tag}_Bsig{mj}")
        p.act(bi2.t(), bd2.t(), AF.Copy, scale=65536.0, bias=RK + 0.5,
              tag=f"{tag}_Bq{mj}")
        p.act(bd1.t(), bi1.t(), AF.Copy, bias=-RK, tag=f"{tag}_Bd1{mj}")
        p.act(bd2.t(), bi2.t(), AF.Copy, bias=-RK, tag=f"{tag}_Bd2{mj}")
        p.tt("v", bd1.t(), bd1.t(), bd2.t(), ALU.mult, tag=f"{tag}_Bt{mj}")
        p.red("v", tmxc.t(0, 128, mj, mj + 1), bd1.t(), ALU.max, absval=True,
              tag=f"{tag}_Bmx{mj}")
        p.ts("v", gate.t(mj), bd1.t(), csg.t(), ALU.mult, tag=f"{tag}_Btb{mj}")
    tmax = slab.alloc(1, F32, f"tmax_{tag}")
    p.red("v", tmax.t(), tmxc.t(), ALU.max, tag=f"{tag}_Btmax")
    p.tt("v", tmax.t(), tmax.t(), csg.t(0, 128, 0, 1), ALU.mult, tag=f"{tag}_Btsc")
    gmx2 = gmax_exchange(mk, tmax.t(), 1, f"{tag}_b2")
    scol2 = scale_cols(mk, gmx2, 1, 32767.0, f"{tag}_s2")
    gmx2.free()
    tmax.free()
    tmxc.free()
    s_t = lambda: scol2.t(0, 128, 0, 1)
    rs_t = lambda: scol2.t(0, 128, 1, 2)
    cstu = slab.alloc(1, F32, f"cstu_{tag}")
    p.tt("v", cstu.t(), s_t(), s_up(), ALU.mult, tag=f"{tag}_cstu")

    umxc = slab.alloc(24, F32, f"umx_{tag}")
    upld = [slab.alloc(NCOL, BF16, f"upld{i}") for i in range(2)]
    for mj in range(24):
        upb = upld[mj % 2]
        p.add("y", (lambda e, mj=mj, o=upb.t().ap: e.dma_start(out=o, in_=ups_ap[:, mj, :])),
              reads=[up_spill.box((0, 1, mj, mj + 1))], writes=[upb.t()], kind="d",
              tag=f"{tag}_upld{mj}")
        p.ts("v", bi1.t(), gate.t(mj), rs_t(), ALU.mult, RK + 0.5, ALU.add,
             tag=f"{tag}_Ca{mj}")
        p.ts("v", bi2.t(), upb.t(), rs_up(), ALU.mult, RK + 0.5, ALU.add,
             tag=f"{tag}_Cb{mj}")
        p.act(bd1.t(), bi1.t(), AF.Copy, bias=-RK, tag=f"{tag}_Cd1{mj}")
        p.act(bd2.t(), bi2.t(), AF.Copy, bias=-RK, tag=f"{tag}_Cd2{mj}")
        p.tt("v", bd1.t(), bd1.t(), bd2.t(), ALU.mult, tag=f"{tag}_Cu{mj}")
        p.red("v", umxc.t(0, 128, mj, mj + 1), bd1.t(), ALU.max, absval=True,
              tag=f"{tag}_Cmx{mj}")
        p.ts("v", gate.t(mj), bd1.t(), cstu.t(), ALU.mult, tag=f"{tag}_Cub{mj}")
    for b in upld:
        b.free()
    umax = slab.alloc(1, F32, f"umax_{tag}")
    p.red("v", umax.t(), umxc.t(), ALU.max, tag=f"{tag}_Cumax")
    p.tt("v", umax.t(), umax.t(), cstu.t(0, 128, 0, 1), ALU.mult, tag=f"{tag}_Cusc")
    gmx3 = gmax_exchange(mk, umax.t(), 1, f"{tag}_b3")
    scol3 = scale_cols(mk, gmx3, 1, 32767.0, f"{tag}_s3")
    gmx3.free()
    umax.free()
    umxc.free()
    s_u = lambda: scol3.t(0, 128, 0, 1)
    rs_u = lambda: scol3.t(0, 128, 1, 2)

    for mj in range(24):
        p.ts("v", bi1.t(), gate.t(mj), rs_u(), ALU.mult, RK + 0.5, ALU.add,
             tag=f"{tag}_Da{mj}")
        p.ts("v", gate.t(mj), bi1.t(), -RK, ALU.add, s_u(), ALU.mult,
             tag=f"{tag}_Dq{mj}")
    for b in (bi1, bi2, bd1, bd2):
        b.free()
    m32sg.free()

    xnew = slab.falloc(CH, NCOL, F32, f"xnew_{tag}")
    dqd = mk.dq["down"]
    for half in range(2):
        o0 = half * 512
        wts = [load_wT_slice(mk, dqd, o0, o0 + 512, c, f"{tag}_dn{half}_{c}")
               for c in range(24)]
        for mloc in range(4):
            mj = half * 4 + mloc
            xold = slab.alloc(NCOL, F32, f"xoldq{mj}")
            restream_x_chunk(mk, st, mj, xold, tag)
            for nt, (c0, c1) in enumerate(NT512):
                pairs = [(wts[c].t(0, 128, mloc * 128, (mloc + 1) * 128),
                          gate.t(c, c0, c1)) for c in range(24)]
                ps = mk.psF[(mj * 4 + nt) % 6].t()
                p.mmgroup(pairs, ps, tag=f"{tag}_dn{mj}_{nt}")
                p.stt("v", xnew.t(mj, c0, c1), ps, 0.5, xold.t(0, 128, c0, c1),
                      ALU.mult, ALU.add, tag=f"{tag}_dres{mj}_{nt}")
            xold.free()
        for w in wts:
            w.free()
    gate.free()
    for b in (scol1, scol2, scol3, csg, cstu):
        b.free()
    st.x = xnew


class _TruncView:
    def __init__(self, fb, c):
        self.fb = fb
        self.c = c

    def t(self):
        return self.fb.t(self.c)


def bias_row_bf(mk, wcb, E, lnb_bf, bvec_dram, tag):
    p, slab = mk.p, mk.slab
    row = slab.alloc(E, F32, f"br_{tag}")
    for s in range(E // 512):
        ps = mk.psF[2].t(0, 1, 0, 512)
        p.mmgroup([(lnb_bf.t(0, 128, c, c + 1), wcb.t(c, s * 512, (s + 1) * 512))
                   for c in range(CH)], ps, tag=f"{tag}_br{s}")
        p.copy("v", row.t(0, 1, s * 512, (s + 1) * 512), ps, tag=f"{tag}_brev{s}")
    bro = slab.alloc(E, F32, f"brb_{tag}")
    p.add("y", (lambda e, o=bro.t(0, 1).ap, a=bvec_dram.handle.ap()[None, :]:
                e.dma_start(out=o, in_=a)),
          reads=[bvec_dram.t()], writes=[bro.t(0, 1)], kind="d", tag=f"{tag}_brld")
    out = slab.alloc(E, BF16, f"brf_{tag}")
    p.tt("v", out.t(0, 1), row.t(0, 1), bro.t(0, 1), ALU.add, tag=f"{tag}_bradd")
    row.free()
    bro.free()
    return out


def proj_tm(mk, src_bf, wcb, bias_row, nmr_row, cs_row, evac, tag,
            tchunks=16, srcsl=None):
    p = mk.p
    for tc in range(tchunks):
        t0 = tc * 128
        for nt in range(2):
            e0, e1 = nt * 512, (nt + 1) * 512
            pairs = [(src_bf.t(c, t0, t0 + 128), wcb.t(c, e0, e1)) for c in range(CH)]
            pairs.append((mk.ones_row_bf(0, 128), bias_row.t(0, 1, e0, e1)))
            if cs_row is not None:
                pairs.append((nmr_row.t(0, 1, t0, t0 + 128), cs_row.t(0, 1, e0, e1)))
            ps = mk.psF[(tc * 2 + nt) % 6].t()
            p.mmgroup(pairs, ps, tag=f"{tag}_tm{tc}_{nt}")
            evac(tc, e0, e1, ps)


def phi_evac_fm(mk, dst, tag):
    p, slab = mk.p, mk.slab

    def ev(mj, c0, c1, ps, bias):
        tmn = slab.alloc(512, F32, "phn")
        tmx = slab.alloc(512, F32, "phx")
        p.ts("v", tmn.t(), ps, bias, ALU.add, 0.0, ALU.min, tag=f"{tag}_mn{mj}")
        p.ts("v", tmx.t(), ps, bias, ALU.add, 0.0, ALU.max, tag=f"{tag}_mx{mj}")
        p.act(tmn.t(), tmn.t(), AF.Exp, tag=f"{tag}_ex{mj}")
        p.tt("v", dst.t(mj, c0, c1), tmn.t(), tmx.t(), ALU.add, tag=f"{tag}_ph{mj}")
        tmn.free()
        tmx.free()
    return ev


def phi_evac_tm(mk, dst, tag):
    p, slab = mk.p, mk.slab

    def ev(tc, e0, e1, ps):
        tmn = slab.alloc(512, F32, "phn")
        tmx = slab.alloc(512, F32, "phx")
        p.ts("v", tmn.t(), ps, 0.0, ALU.min, tag=f"{tag}_mn{tc}")
        p.ts("v", tmx.t(), ps, 0.0, ALU.max, tag=f"{tag}_mx{tc}")
        p.act(tmn.t(), tmn.t(), AF.Exp, tag=f"{tag}_ex{tc}")
        p.tt("v", dst.t(tc, e0, e1), tmn.t(), tmx.t(), ALU.add, tag=f"{tag}_ph{tc}")
        tmn.free()
        tmx.free()
    return ev


def sub_l1_attn(mk, st, WD):
    p, slab = mk.p, mk.slab
    tag = "w5"
    ln = ln_stats(mk, st, tag, want_nmr_bc=True)
    ln["xbf"].free()
    lng_col = mk.load_col(WD["l1a_ln_g"], tag=f"{tag}_lng")
    lnb_col = mk.load_col(WD["l1a_ln_b"], tag=f"{tag}_lnb")
    lng_bf = slab.alloc(CH, BF16, f"{tag}_lngbf")
    lnb_bf = slab.alloc(CH, BF16, f"{tag}_lnbbf")
    p.copy("v", lng_bf.t(), lng_col.t(), tag=f"{tag}_lngc")
    p.copy("v", lnb_bf.t(), lnb_col.t(), tag=f"{tag}_lnbc")
    xg = slab.falloc(CH, NCOL, BF16, f"xg_{tag}")
    for c in range(CH):
        p.stt("v", xg.t(c), st.x.t(c), lng_col.t(0, 128, c, c + 1), ln["R_bc"].t(),
              ALU.mult, ALU.mult, tag=f"{tag}_xg{c}")
    mk.tap("w5_xg", xg, CH, NCOL, BF16)
    xln_spill = mk.dinternal("xln_spill", [D, NCOL], F32)
    xlsp = xln_spill.handle.ap().rearrange("(c p) n -> p c n", p=128)
    xlt = slab.alloc(NCOL, F32, f"xlt_{tag}")
    for c in range(CH):
        p.stt("v", xlt.t(), st.x.t(c), 1.0, ln["R_bc"].t(), ALU.mult, ALU.mult,
              tag=f"{tag}_xl{c}")
        p.tt("g", xlt.t(), xlt.t(), ln["nmr_bc"].t(), ALU.add, tag=f"{tag}_xlb{c}")
        p.add("y", (lambda e, c=c, a=xlt.t().ap: e.dma_start(out=xlsp[:, c, :], in_=a)),
              reads=[xlt.t()], writes=[xln_spill.box((0, 1, c, c + 1))], kind="d",
              tag=f"{tag}_xlsp{c}")
    xlt.free()
    ln["nmr_bc"].free()
    spill_x(mk, st, tag)
    st.x.free()
    ln["R_bc"].free()
    nmr = ln["nmr_row"]

    HW = CH * 128
    hstage = slab.alloc(HW, BF16, f"hst_{tag}")
    for c in range(CH):
        for b2 in range(NB):
            for j in range(2):
                csrc = b2 * SL + (0 if j == 0 else SL - HALO)
                p.copy("v", hstage.t(0, 128, c * 128 + b2 * 64 + j * 32,
                                     c * 128 + b2 * 64 + j * 32 + 32),
                       xg.t(c, csrc, csrc + HALO), tag=f"{tag}_hs{c}{b2}{j}")
    nstage = slab.alloc(128, BF16, f"nst_{tag}")
    for b2 in range(NB):
        for j in range(2):
            csrc = b2 * SL + (0 if j == 0 else SL - HALO)
            p.copy("v", nstage.t(0, 1, b2 * 64 + j * 32, b2 * 64 + j * 32 + 32),
                   nmr.t(0, 1, csrc, csrc + HALO), tag=f"{tag}_hsn{b2}{j}")
    TOTW = 128 * HW + 128
    hcin = mk.dinternal(f"cc{mk._cc_ctr}_in", [TOTW], BF16)
    hcout = mk.dinternal(f"cc{mk._cc_ctr}_out", [NCORES * TOTW], BF16,
                         shared_for="AllGather")
    mk._cc_ctr += 1
    p.add("g", (lambda e, o=hcin.handle.ap()[0:128 * HW].rearrange("(p n) -> p n", p=128),
                a=hstage.t().ap: e.dma_start(out=o, in_=a)),
          reads=[hstage.t()], writes=[hcin.box((0, 1, 0, 1))], kind="d", tag=f"{tag}_hciw")
    p.add("g", (lambda e, o=hcin.handle.ap()[None, 128 * HW:TOTW],
                a=nstage.t(0, 1).ap: e.dma_start(out=o, in_=a)),
          reads=[nstage.t(0, 1)], writes=[hcin.box((0, 1, 1, 2))], kind="d",
          tag=f"{tag}_hcin")
    p.coll("AllGather", ALU.bypass, hcin.t(), hcout.t(), mk.rg, tag=f"{tag}_hag")
    hstage.free()
    nstage.free()
    hh = slab.alloc(CH * 128, BF16, f"hh_{tag}")
    nmrh = slab.alloc(128, BF16, f"nmrh_{tag}")
    hco = hcout.handle.ap().rearrange("(r q) -> r q", r=NCORES)

    def _halo_dma(e, side):
        with e.register(f"rk{side}") as rk:
            e.reg_load(rk, mk.rank_tap(side).ap)
            ro = e.snap(rk)
            src = hco[bass.ds(ro, 1), 0:128 * HW].rearrange(
                "r (p c b2 j i) -> r p c b2 j i", p=128, c=CH, b2=NB, j=2)[0, :, :, :, 1 - side, :]
            dst = hh.t().ap.rearrange("p (c b2 j i) -> p c b2 j i", c=CH, b2=NB, j=2)[
                :, :, :, side, :]
            return e.dma_start(out=dst, in_=src)

    for side in range(2):
        sd = side
        p.add("g", (lambda e, sd=sd: _halo_dma(e, sd)),
              reads=[hcout.t(), mk.rank_tap(sd)], writes=[hh.t()], kind="d",
              tag=f"{tag}_hld{sd}")

    def _halo_nmr_dma(e, side):
        with e.register(f"rn{side}") as rk:
            e.reg_load(rk, mk.rank_tap(side).ap)
            ro = e.snap(rk)
            src = hco[bass.ds(ro, 1), 128 * HW:TOTW].rearrange(
                "r (b2 j i) -> r b2 j i", b2=NB, j=2)[:, :, 1 - side, :]
            dst = nmrh.t(0, 1).ap.rearrange("p (b2 j i) -> p b2 j i", b2=NB, j=2)[
                :, :, side, :]
            return e.dma_start(out=dst, in_=src)

    for side in range(2):
        sd = side
        p.add("g", (lambda e, sd=sd: _halo_nmr_dma(e, sd)),
              reads=[hcout.t(), mk.rank_tap(sd)], writes=[nmrh.t(0, 1)], kind="d",
              tag=f"{tag}_hldn{sd}")


    if W5_TRUNC <= 1:
        xnew = slab.falloc(CH, NCOL, F32, "xtr")
        for c in range(CH):
            restream_x_chunk(mk, st, c, _TruncView(xnew, c), "tr")
        st.x = xnew
        return
    w_klin = mk.load_weight_bf16(WD["l1a_wk_lin"], D, tag=f"{tag}_wkl")
    w_vlin = mk.load_weight_bf16(WD["l1a_wv_lin"], D, tag=f"{tag}_wvl")
    cs_kl = proj_cs_row(mk, w_klin, D, gcol_bf=lng_bf, tag=f"{tag}_cskl")
    br_kl = bias_row_bf(mk, w_klin, D, lnb_bf, WD["l1a_bk_lin"], f"{tag}_brkl")
    phk = slab.falloc(16, D, BF16, f"phk_{tag}")
    proj_tm(mk, xg, w_klin, br_kl, nmr, cs_kl, phi_evac_tm(mk, phk, f"{tag}_phk"),
            f"{tag}_kl")
    w_klin.free()
    cs_kl.free()
    br_kl.free()
    cs_vl = proj_cs_row(mk, w_vlin, D, gcol_bf=lng_bf, tag=f"{tag}_csvl")
    br_vl = bias_row_bf(mk, w_vlin, D, lnb_bf, WD["l1a_bv_lin"], f"{tag}_brvl")
    vtm = slab.falloc(16, D, BF16, f"vtm_{tag}")

    def ev_v(tc, e0, e1, ps):
        p.copy("s", vtm.t(tc, e0, e1), ps, tag=f"{tag}_evv{tc}_{e0}")
    proj_tm(mk, xg, w_vlin, br_vl, nmr, cs_vl, ev_v, f"{tag}_vl")
    w_vlin.free()
    cs_vl.free()
    br_vl.free()
    kv_in = mk.dinternal(f"cc{mk._cc_ctr}_in", [NB * D * D + NB * D], F32)
    kv_out = mk.dinternal(f"cc{mk._cc_ctr}_out", [NB * D * D + NB * D], F32,
                          shared_for="AllReduce")
    mk._cc_ctr += 1
    kvr = kv_in.handle.ap()[0:NB * D * D].rearrange("(b dc eh p n) -> b dc eh p n",
                                                    b=NB, dc=CH, eh=2, p=128)
    kstage = slab.alloc(512, F32, f"kvst_{tag}")
    for b2 in range(NB):
        for dc in range(CH):
            for eh in range(2):
                ps = mk.psF[(dc * 2 + eh) % 4].t()
                p.mmgroup([(phk.t(b2 * 8 + tc, dc * 128, (dc + 1) * 128),
                            vtm.t(b2 * 8 + tc, eh * 512, (eh + 1) * 512))
                           for tc in range(8)], ps, tag=f"{tag}_kv{b2}{dc}{eh}")
                p.copy("v", kstage.t(), ps, tag=f"{tag}_kvev{b2}{dc}{eh}")
                p.add("y", (lambda e, o=kvr[b2, dc, eh], a=kstage.t().ap:
                            e.dma_start(out=o, in_=a)),
                      reads=[kstage.t()], writes=[kv_in.box((0, 1, b2 * 16 + dc * 2 + eh,
                                                             b2 * 16 + dc * 2 + eh + 1))],
                      kind="d", tag=f"{tag}_kvw{b2}{dc}{eh}")
    ksr = kv_in.handle.ap()[NB * D * D:].rearrange("(p j) -> p j", p=128)
    kss = slab.alloc(16, F32, f"kss_{tag}")
    for b2 in range(NB):
        for dc in range(CH):
            ps = mk.psF[4].t(0, 128, 0, 1)
            p.mmgroup([(phk.t(b2 * 8 + tc, dc * 128, (dc + 1) * 128),
                        mk.ones_col()) for tc in range(8)], ps,
                      tag=f"{tag}_ks{b2}{dc}")
            p.copy("v", kss.t(0, 128, b2 * 8 + dc, b2 * 8 + dc + 1), ps,
                   tag=f"{tag}_ksev{b2}{dc}")
    p.add("y", (lambda e, o=ksr, a=kss.t().ap: e.dma_start(out=o, in_=a)),
          reads=[kss.t()], writes=[kv_in.box((0, 1, 100, 101))], kind="d",
          tag=f"{tag}_ksw")
    kss.free()
    kstage.free()
    phk.free()
    vtm.free()
    p.coll("AllReduce", ALU.add, kv_in.t(), kv_out.t(), mk.rg, tag=f"{tag}_kvar")


    if W5_TRUNC <= 2:
        xnew = slab.falloc(CH, NCOL, F32, "xtr")
        for c in range(CH):
            restream_x_chunk(mk, st, c, _TruncView(xnew, c), "tr")
        st.x = xnew
        return
    w_qloc = mk.load_weight_bf16(WD["l1a_wq_loc"], D, tag=f"{tag}_wql")
    cs_ql = proj_cs_row(mk, w_qloc, D, gcol_bf=lng_bf, tag=f"{tag}_csql")
    br_ql = bias_row_bf(mk, w_qloc, D, lnb_bf, WD["l1a_bq_loc"], f"{tag}_brql")
    qloc = slab.falloc(CH, NCOL, BF16, f"qloc_{tag}")

    def ev_ql(mj, c0, c1, ps, bias):
        p.copy("s", qloc.t(mj, c0, c1), ps, tag=f"{tag}_evql{mj}")
    proj_fm(mk, xg, w_qloc, D, None, ev_ql, f"{tag}_ql", cs_row=cs_ql, nmr=nmr,
            bias_row=br_ql)
    w_qloc.free()
    cs_ql.free()
    br_ql.free()

    w_kloc = mk.load_weight_bf16(WD["l1a_wk_loc"], D, tag=f"{tag}_wkc")
    cs_kc = proj_cs_row(mk, w_kloc, D, gcol_bf=lng_bf, tag=f"{tag}_cskc")
    br_kc = bias_row_bf(mk, w_kloc, D, lnb_bf, WD["l1a_bk_loc"], f"{tag}_brkc")
    kloc = slab.falloc(CH, NCOL, BF16, f"kloc_{tag}")

    def ev_kc(mj, c0, c1, ps, bias):
        p.copy("s", kloc.t(mj, c0, c1), ps, tag=f"{tag}_evkc{mj}")
    proj_fm(mk, xg, w_kloc, D, None, ev_kc, f"{tag}_kc", cs_row=cs_kc, nmr=nmr,
            bias_row=br_kc)
    khalo = slab.falloc(CH, 128, BF16, f"khalo_{tag}")
    ones128r = None
    for mj in range(CH):
        pairs = [(w_kloc.t(c, mj * 128, (mj + 1) * 128),
                  TAP(hh.t().ap.rearrange("p (c n) -> p c n", c=CH)[:, c, :],
                      hh.slab.name, hh.t().box)) for c in range(CH)]
        pairs.append((cs_kc.t(0, 1, mj * 128, (mj + 1) * 128), nmrh.t(0, 1)))
        pairs.append((br_kc.t(0, 1, mj * 128, (mj + 1) * 128),
                      TAP(mk.ones512().ap[:, 0:128], "ones512", (0, 1, 0, 512))))
        ps = mk.psF[mj % 4].t(0, 128, 0, 128)
        p.mmgroup(pairs, ps, tag=f"{tag}_kh{mj}")
        p.copy("s", khalo.t(mj), ps, tag=f"{tag}_evkh{mj}")
    w_kloc.free()
    cs_kc.free()
    br_kc.free()

    w_vloc = mk.load_weight_bf16(WD["l1a_wv_loc"], D, tag=f"{tag}_wvc")
    cs_vc = proj_cs_row(mk, w_vloc, D, gcol_bf=lng_bf, tag=f"{tag}_csvc")
    br_vc = bias_row_bf(mk, w_vloc, D, lnb_bf, WD["l1a_bv_loc"], f"{tag}_brvc")
    vext = slab.falloc(NB * 9, D, BF16, f"vext_{tag}")

    def ev_vx(tc, e0, e1, ps):
        b2, tl = tc // 8, tc % 8
        base = b2 * 9 + tl
        for rr in range(3):
            p.copy("s", vext.t(base, e0, e1, p0=32 + rr * 32, p1=64 + rr * 32),
                   TAP(ps.ap[rr * 32:(rr + 1) * 32, :], ps.name,
                       (rr * 32, (rr + 1) * 32, ps.box[2], ps.box[3])),
                   tag=f"{tag}_evvx{tc}a{rr}")
        p.copy("s", vext.t(base + 1, e0, e1, p0=0, p1=32),
               TAP(ps.ap[96:128, :], ps.name, (96, 128, ps.box[2], ps.box[3])),
               tag=f"{tag}_evvx{tc}b")
    proj_tm(mk, xg, w_vloc, br_vc, nmr, cs_vc, ev_vx, f"{tag}_vx")
    for nt in range(2):
        e0, e1 = nt * 512, (nt + 1) * 512
        pairs = [(TAP(hh.t().ap.rearrange("p (c n) -> p c n", c=CH)[:, c, :],
                      hh.slab.name, hh.t().box), w_vloc.t(c, e0, e1)) for c in range(CH)]
        pairs.append((mk.ones_row_bf(0, 128), br_vc.t(0, 1, e0, e1)))
        pairs.append((nmrh.t(0, 1), cs_vc.t(0, 1, e0, e1)))
        ps = mk.psF[4].t()
        p.mmgroup(pairs, ps, tag=f"{tag}_vh{nt}")
        for b2 in range(NB):
            for j in range(2):
                r0 = b2 * 64 + j * 32
                if j == 0:
                    dst = vext.t(b2 * 9 + 0, e0, e1, p0=0, p1=32)
                else:
                    dst = vext.t(b2 * 9 + 8, e0, e1, p0=32, p1=64)
                p.copy("s", dst, TAP(ps.ap[r0:r0 + 32, :], ps.name,
                                     (ps.box[0] + r0, ps.box[0] + r0 + 32, ps.box[2], ps.box[3])),
                       tag=f"{tag}_evvh{nt}{b2}{j}")
    w_vloc.free()
    cs_vc.free()
    br_vc.free()
    xg_spill = mk.dinternal("xg_spill", [D, NCOL], BF16)
    xgsp = xg_spill.handle.ap().rearrange("(c p) n -> p c n", p=128)
    for c in range(CH):
        p.add("y", (lambda e, c=c, a=xg.t(c).ap: e.dma_start(out=xgsp[:, c, :], in_=a)),
              reads=[xg.t(c)], writes=[xg_spill.box((0, 1, c, c + 1))], kind="d",
              tag=f"{tag}_xgsp{c}")
    xg.free()


    if W5_TRUNC <= 3:
        xnew = slab.falloc(CH, NCOL, F32, "xtr")
        for c in range(CH):
            restream_x_chunk(mk, st, c, _TruncView(xnew, c), "tr")
        st.x = xnew
        return
    btol = mk.din("btol", [128, 16 * 192])
    b8 = {}
    btbuf = slab.alloc(16 * 192, F32, f"btol_{tag}")
    p.add("y", (lambda e, o=btbuf.t().ap, a=btol.handle.ap()[:, :]: e.dma_start(out=o, in_=a)),
          reads=[btol.t()], writes=[btbuf.t()], kind="d", tag=f"{tag}_btld")
    for v in ("I", "F", "L"):
        mdram = mk.din(f"wmask{v}", [128, 192])
        mbuf = slab.alloc(192, F32, f"m{v}_{tag}")
        p.add("y", (lambda e, o=mbuf.t().ap, a=mdram.handle.ap()[:, :]: e.dma_start(out=o, in_=a)),
              reads=[mdram.t()], writes=[mbuf.t()], kind="d", tag=f"{tag}_m{v}ld")
        bb = slab.alloc(16 * 192, BF16, f"b8{v}_{tag}")
        tmpb = slab.alloc(4 * 192, F32, f"b8t_{tag}")
        m3 = mbuf.t().ap.unsqueeze(1).broadcast_to([128, 4, 192])
        for hq in range(4):
            s0, s1 = hq * 4 * 192, (hq + 1) * 4 * 192
            p.add("v", (lambda e, o=tmpb.t().ap.rearrange("p (h j) -> p h j", h=4),
                        a=btbuf.t(0, 128, s0, s1).ap.rearrange("p (h j) -> p h j", h=4),
                        b=m3: e.tensor_tensor(out=o, in0=a, in1=b, op=ALU.add)),
                  reads=[btbuf.t(0, 128, s0, s1), mbuf.t()], writes=[tmpb.t()],
                  tag=f"{tag}_b8a{v}{hq}")
            p.ts("v", bb.t(0, 128, s0, s1), tmpb.t(), 8.0, ALU.mult, tag=f"{tag}_b8m{v}{hq}")
        tmpb.free()
        mbuf.free()
        b8[v] = bb
    btbuf.free()

    win = slab.falloc(16, D, BF16, f"win_{tag}")
    pbf = slab.alloc(4 * 192, BF16, f"pbf_{tag}")
    nrm4 = slab.alloc(8, F32, f"nrm4_{tag}")
    rsum = slab.alloc(4, F32, f"rsum_{tag}")
    rrec = slab.alloc(4, F32, f"rrec_{tag}")
    atsb = slab.alloc(1024, BF16, f"atsb_{tag}")
    _bqctr = [0]
    for b2 in range(NB):
        for q in range(8):
            _bqctr[0] += 1
            if W5_NBQ is not None and _bqctr[0] > W5_NBQ:
                continue
            t0own = b2 * SL + q * 128
            bsel = b8["F"] if q == 0 else (b8["L"] if q == 7 else b8["I"])
            segs = []
            ke0 = q * 128
            if q == 0:
                segs.append((0, 32, lambda h, b2=b2: khalo.t(h // 2, b2 * 64, b2 * 64 + 32,
                                                             p0=(h % 2) * 64, p1=(h % 2) * 64 + 64)))
                segs.append((32, 192, lambda h, b2=b2: kloc.t(h // 2, b2 * SL, b2 * SL + 160,
                                                              p0=(h % 2) * 64, p1=(h % 2) * 64 + 64)))
            elif q == 7:
                segs.append((0, 160, lambda h, b2=b2: kloc.t(h // 2, b2 * SL + 864, b2 * SL + SL,
                                                             p0=(h % 2) * 64, p1=(h % 2) * 64 + 64)))
                segs.append((160, 192, lambda h, b2=b2: khalo.t(h // 2, b2 * 64 + 32, b2 * 64 + 64,
                                                                p0=(h % 2) * 64, p1=(h % 2) * 64 + 64)))
            else:
                o0 = b2 * SL + ke0 - 32
                segs.append((0, 192, lambda h, o0=o0: kloc.t(h // 2, o0, o0 + 192,
                                                             p0=(h % 2) * 64, p1=(h % 2) * 64 + 64)))
            for hg in range(4):
                psS = [mk.psF[2 * (hg % 2)], mk.psF[2 * (hg % 2) + 1]]
                psT = mk.psB[hg % 2]
                psO = mk.psF[4 + hg % 2]
                for hh_ in range(4):
                    h = hg * 4 + hh_
                    bank = psS[hh_ // 2]
                    colo = (hh_ % 2) * 192
                    pst = bank.t(0, 128, colo, colo + 192)
                    qtap = qloc.t(h // 2, t0own, t0own + 128,
                                  p0=(h % 2) * 64, p1=(h % 2) * 64 + 64)
                    def em_sc(e, pst=pst.ap, qtap=qtap.ap, segs=segs, h=h, bsel=bsel, colo=colo):
                        inst = None
                        first = True
                        for (s0, s1, kf) in segs:
                            inst = e.matmul(pst[:, s0:s1], qtap, kf(h).ap,
                                            start=first, stop=False)
                            first = False
                        inst = e.matmul(pst, mk.ident_tap.ap,
                                        bsel.t(0, 128, h * 192, (h + 1) * 192).ap,
                                        start=False, stop=True)
                        return inst
                    rd = [qloc.t(h // 2, t0own, t0own + 128), kloc.t(h // 2, 0, NCOL),
                          khalo.t(h // 2), bsel.t(0, 128, h * 192, (h + 1) * 192),
                          mk.ident_tap]
                    p.add("t", em_sc, reads=rd, writes=[pst], tag=f"{tag}_sc{b2}{q}{h}")
                for bi in range(2):
                    bank = psS[bi]
                    p.add("v", (lambda e, o=nrm4.t(0, 128, hg % 2 * 4 + bi * 2,
                                                   hg % 2 * 4 + bi * 2 + 2).ap,
                                a=bank.t(0, 128, 0, 384).ap.rearrange("p (h j) -> p h j", h=2):
                                e.tensor_reduce(out=o, in_=a, axis=AX, op=ALU.max, negate=True)),
                          reads=[bank.t(0, 128, 0, 384)],
                          writes=[nrm4.t(0, 128, hg % 2 * 4 + bi * 2, hg % 2 * 4 + bi * 2 + 2)],
                          tag=f"{tag}_nrm{b2}{q}{hg}{bi}")
                p.ts("v", nrm4.t(0, 128, hg % 2 * 4, hg % 2 * 4 + 4),
                     nrm4.t(0, 128, hg % 2 * 4, hg % 2 * 4 + 4), 0.125, ALU.mult,
                     tag=f"{tag}_nsc{b2}{q}{hg}")
                for hh_ in range(4):
                    h = hg * 4 + hh_
                    bank = psS[hh_ // 2]
                    colo = (hh_ % 2) * 192
                    p.act(pbf.t(0, 128, hh_ * 192, (hh_ + 1) * 192),
                          bank.t(0, 128, colo, colo + 192), AF.Exp,
                          bias=nrm4.t(0, 128, hg % 2 * 4 + hh_, hg % 2 * 4 + hh_ + 1),
                          scale=0.125,
                          accum=rsum.t(0, 128, hh_, hh_ + 1), tag=f"{tag}_exp{b2}{q}{h}")
                p.add("v", (lambda e, o=rrec.t().ap, a=rsum.t().ap: e.reciprocal(out=o, in_=a)),
                      reads=[rsum.t()], writes=[rrec.t()], tag=f"{tag}_rr{b2}{q}{hg}")
                for hh_ in range(4):
                    p.transpose(psT.t(0, 128, hh_ * 128, hh_ * 128 + 128),
                                pbf.t(0, 128, hh_ * 192, hh_ * 192 + 128), mk.ident_tap,
                                tag=f"{tag}_tp{b2}{q}{hg}{hh_}a")
                    p.transpose(psT.t(0, 64, 512 + hh_ * 128, 512 + hh_ * 128 + 128),
                                pbf.t(0, 128, hh_ * 192 + 128, hh_ * 192 + 192), mk.ident_tap,
                                tag=f"{tag}_tp{b2}{q}{hg}{hh_}b")
                p.copy("v", atsb.t(0, 128, 0, 512), psT.t(0, 128, 0, 512),
                       tag=f"{tag}_at{b2}{q}{hg}")
                p.copy("v", atsb.t(0, 64, 512, 1024), psT.t(0, 64, 512, 1024),
                       tag=f"{tag}_at{b2}{q}{hg}b")
                for hh_ in range(4):
                    h = hg * 4 + hh_
                    vc1 = vext.t(b2 * 9 + q, h * 64, (h + 1) * 64)
                    vc2 = vext.t(b2 * 9 + q + 1, h * 64, (h + 1) * 64, p0=0, p1=64)
                    pso = psO.t(0, 128, hh_ * 64, (hh_ + 1) * 64)
                    p.mmgroup([(atsb.t(0, 128, hh_ * 128, hh_ * 128 + 128), vc1),
                               (atsb.t(0, 64, 512 + hh_ * 128, 512 + hh_ * 128 + 128), vc2)],
                              pso, tag=f"{tag}_ao{b2}{q}{h}")
                    p.ts("v", win.t(b2 * 8 + q, h * 64, (h + 1) * 64), pso,
                         rrec.t(0, 128, hh_, hh_ + 1), ALU.mult, tag=f"{tag}_wo{b2}{q}{h}")
    qloc.free()
    kloc.free()
    khalo.free()
    vext.free()
    hh.free()
    nmrh.free()
    pbf.free()
    nrm4.free()
    rsum.free()
    rrec.free()
    atsb.free()
    for v in b8:
        b8[v].free()

    mk.tap("w5_win", win, 16, D, BF16)

    if W5_TRUNC <= 4:
        xnew = slab.falloc(CH, NCOL, F32, "xtr")
        for c in range(CH):
            restream_x_chunk(mk, st, c, _TruncView(xnew, c), "tr")
        st.x = xnew
        return
    winfm = slab.falloc(CH, NCOL, BF16, f"winfm_{tag}")
    for mj in range(CH):
        for grp in range(2):
            bank = mk.psB[(mj * 2 + grp) % 2]
            for k8 in range(8):
                tq = grp * 8 + k8
                p.transpose(bank.t(0, 128, k8 * 128, (k8 + 1) * 128),
                            win.t(tq, mj * 128, (mj + 1) * 128), mk.ident_tap,
                            tag=f"{tag}_wt{mj}{tq}")
            p.copy("v", winfm.t(mj, grp * 1024, (grp + 1) * 1024), bank.t(),
                   tag=f"{tag}_wtev{mj}{grp}")
    win.free()

    xg = slab.falloc(CH, NCOL, BF16, f"xg2_{tag}")
    for c in range(CH):
        p.add("y", (lambda e, c=c, o=xg.t(c).ap: e.dma_start(out=o, in_=xgsp[:, c, :])),
              reads=[xg_spill.box((0, 1, c, c + 1))], writes=[xg.t(c)], kind="d",
              tag=f"{tag}_xgld{c}")
    w_qlin = mk.load_weight_bf16(WD["l1a_wq_lin"], D, tag=f"{tag}_wqn")
    cs_qn = proj_cs_row(mk, w_qlin, D, gcol_bf=lng_bf, tag=f"{tag}_csqn")
    bc_qn = matcol(mk, w_qlin, lnb_bf, extra_col=mk.load_col(WD["l1a_bq_lin"], tag=f"{tag}_bqn"),
                   tag=f"{tag}_bcqn")
    phq = slab.falloc(CH, NCOL, BF16, f"phq_{tag}")
    proj_fm(mk, xg, w_qlin, D, bc_qn, phi_evac_fm(mk, phq, f"{tag}_phq"), f"{tag}_qn",
            cs_row=cs_qn, nmr=nmr)
    w_qlin.free()
    cs_qn.free()
    bc_qn.free()
    kvbf = slab.falloc(16, D, BF16, f"kvbf_{tag}")
    kvo = kv_out.handle.ap()[0:NB * D * D].rearrange("(b dc eh p n) -> b dc eh p n",
                                                     b=NB, dc=CH, eh=2, p=128)
    for b2 in range(NB):
        for dc in range(CH):
            for eh in range(2):
                p.add("g", (lambda e, o=kvbf.t(b2 * 8 + dc, eh * 512, (eh + 1) * 512).ap,
                            a=kvo[b2, dc, eh]: e.dma_start(out=o, in_=a)),
                      reads=[kv_out.t()], writes=[kvbf.t(b2 * 8 + dc, eh * 512, (eh + 1) * 512)],
                      kind="d", tag=f"{tag}_kvrb{b2}{dc}{eh}")
    kso = kv_out.handle.ap()[NB * D * D:]
    ksumc = slab.alloc(16, BF16, f"ksumc_{tag}")
    p.add("g", (lambda e, o=ksumc.t().ap,
                a=kso.rearrange("(p j) -> p j", p=128):
                e.dma_start(out=o, in_=a)),
          reads=[kv_out.t()], writes=[ksumc.t()], kind="d", tag=f"{tag}_ksrb")
    denrow = slab.alloc(NCOL, F32, f"den_{tag}")
    for b2 in range(NB):
        for nt in range(2):
            c0 = b2 * SL + nt * 512
            ps = mk.psF[5].t(0, 1, 0, 512)
            p.mmgroup([(ksumc.t(0, 128, b2 * 8 + dc, b2 * 8 + dc + 1),
                        phq.t(dc, c0, c0 + 512)) for dc in range(CH)], ps,
                      tag=f"{tag}_dn{b2}{nt}")
            p.copy("v", denrow.t(0, 1, c0, c0 + 512), ps, tag=f"{tag}_dnev{b2}{nt}")
    p.ts("v", denrow.t(0, 1), denrow.t(0, 1), 1e-6, ALU.add, tag=f"{tag}_deps")
    p.act(denrow.t(0, 1), denrow.t(0, 1), AF.Ln, tag=f"{tag}_dln")
    p.act(denrow.t(0, 1), denrow.t(0, 1), AF.Exp, scale=-1.0, tag=f"{tag}_drec")
    r2bc = slab.alloc(NCOL, F32, f"r2bc_{tag}")
    for nt, (c0, c1) in enumerate(NT512):
        ps = mk.psF[5].t()
        p.mmgroup([(mk.ones_row_f32(), denrow.t(0, 1, c0, c1))], ps, tag=f"{tag}_r2b{nt}")
        p.copy("v", r2bc.t(0, 128, c0, c1), ps, tag=f"{tag}_r2e{nt}")
    denrow.free()
    ksumc.free()
    linfm = slab.falloc(CH, NCOL, BF16, f"lin_{tag}")
    for b2 in range(NB):
        for mj in range(CH):
            for nt in range(2):
                c0 = b2 * SL + nt * 512
                pairs = [(kvbf.t(b2 * 8 + dc, mj * 128, (mj + 1) * 128),
                          phq.t(dc, c0, c0 + 512)) for dc in range(CH)]
                ps = mk.psF[(mj * 2 + nt) % 4].t()
                p.mmgroup(pairs, ps, tag=f"{tag}_nm{b2}{mj}{nt}")
                p.tt("v", linfm.t(mj, c0, c0 + 512), ps,
                     TAP(r2bc.t(0, 128, c0 - b2 * SL + b2 * SL, c0 + 512).ap, "slab",
                         r2bc.t().box) if False else r2bc.t(0, 128, c0, c0 + 512),
                     ALU.mult, tag=f"{tag}_lin{b2}{mj}{nt}")
    mk.tap("w5_lin", linfm, CH, NCOL, BF16)
    phq.free()
    kvbf.free()
    r2bc.free()


    if W5_TRUNC <= 5:
        xnew = slab.falloc(CH, NCOL, F32, "xtr")
        for c in range(CH):
            restream_x_chunk(mk, st, c, _TruncView(xnew, c), "tr")
        st.x = xnew
        return
    w_g1 = mk.load_weight_bf16(WD["l1a_wg1"], D, tag=f"{tag}_wg1a")
    w_g1b = mk.load_weight_bf16(WD["l1a_wg1b"], D, tag=f"{tag}_wg1b")
    w_g1c = mk.load_weight_bf16(WD["l1a_wg1c"], D, tag=f"{tag}_wg1c")
    cs_g1 = proj_cs_row(mk, w_g1c, D, gcol_bf=lng_bf, tag=f"{tag}_csg1")
    bc_g1 = matcol(mk, w_g1c, lnb_bf, extra_col=mk.load_col(WD["l1a_bg1"], tag=f"{tag}_bg1"),
                   tag=f"{tag}_bcg1")
    g1 = slab.falloc(CH, NCOL, BF16, f"g1_{tag}")
    for mj in range(CH):
        for nt, (c0, c1) in enumerate(NT512):
            pairs = [(w_g1.t(c, mj * 128, (mj + 1) * 128), linfm.t(c, c0, c1))
                     for c in range(CH)]
            pairs += [(w_g1b.t(c, mj * 128, (mj + 1) * 128), winfm.t(c, c0, c1))
                      for c in range(CH)]
            pairs += [(w_g1c.t(c, mj * 128, (mj + 1) * 128), xg.t(c, c0, c1))
                      for c in range(CH)]
            pairs.append((cs_g1.t(0, 1, mj * 128, (mj + 1) * 128), nmr.t(0, 1, c0, c1)))
            ps = mk.psF[(mj * 4 + nt) % 6].t()
            p.mmgroup(pairs, ps, tag=f"{tag}_g1{mj}{nt}")
            p.act(g1.t(mj, c0, c1), ps, AF.Relu, bias=bc_g1.t(0, 128, mj, mj + 1),
                  tag=f"{tag}_g1ev{mj}{nt}")
    w_g1.free()
    w_g1b.free()
    w_g1c.free()
    cs_g1.free()
    bc_g1.free()
    xg.free()
    nmr.free()

    mk.tap("w5_g1", g1, CH, NCOL, BF16)
    w_g2 = mk.load_weight_bf16(WD["l1a_wg2"], D, tag=f"{tag}_wg2")
    bg2_col = mk.load_col(WD["l1a_bg2"], tag=f"{tag}_bg2")
    oa = slab.falloc(CH, NCOL, BF16, f"oa_{tag}")
    for mj in range(CH):
        for nt, (c0, c1) in enumerate(NT512):
            pairs = [(w_g2.t(c, mj * 128, (mj + 1) * 128), g1.t(c, c0, c1))
                     for c in range(CH)]
            ps = mk.psF[(mj * 4 + nt) % 6].t()
            p.mmgroup(pairs, ps, tag=f"{tag}_g2{mj}{nt}")
            gt = slab.alloc(512, F32, "gt2")
            dd = slab.alloc(512, F32, "dd2")
            p.act(gt.t(), ps, AF.Sigmoid, bias=bg2_col.t(0, 128, mj, mj + 1),
                  tag=f"{tag}_g2s{mj}{nt}")
            p.tt("v", dd.t(), linfm.t(mj, c0, c1), winfm.t(mj, c0, c1), ALU.subtract,
                 tag=f"{tag}_d{mj}{nt}")
            p.tt("v", dd.t(), gt.t(), dd.t(), ALU.mult, tag=f"{tag}_gd{mj}{nt}")
            p.tt("v", oa.t(mj, c0, c1), dd.t(), winfm.t(mj, c0, c1), ALU.add,
                 tag=f"{tag}_oa{mj}{nt}")
            gt.free()
            dd.free()
    w_g2.free()
    bg2_col.free()
    linfm.free()
    winfm.free()
    g1.free()

    mk.tap("w5_oa", oa, CH, NCOL, BF16)
    w_o = mk.load_weight_bf16(WD["l1a_wo"], D, tag=f"{tag}_wo")
    bo_col = mk.load_col(WD["l1a_bo"], tag=f"{tag}_bo")
    xnew = slab.falloc(CH, NCOL, F32, f"xnew_{tag}")
    for mj in range(CH):
        xold = slab.alloc(NCOL, F32, f"xoldw{mj}")
        restream_x_chunk(mk, st, mj, xold, tag)
        xlold = slab.alloc(NCOL, F32, f"xlold{mj}")
        p.add("y", (lambda e, mj=mj, o=xlold.t().ap: e.dma_start(out=o, in_=xlsp[:, mj, :])),
              reads=[xln_spill.box((0, 1, mj, mj + 1))], writes=[xlold.t()], kind="d",
              tag=f"{tag}_xlld{mj}")
        for nt, (c0, c1) in enumerate(NT512):
            pairs = [(w_o.t(c, mj * 128, (mj + 1) * 128), oa.t(c, c0, c1))
                     for c in range(CH)]
            ps = mk.psF[(mj * 4 + nt) % 6].t()
            p.mmgroup(pairs, ps, tag=f"{tag}_wo{mj}{nt}")
            p.stt("v", xnew.t(mj, c0, c1), ps, bo_col.t(0, 128, mj, mj + 1),
                  xold.t(0, 128, c0, c1), ALU.add, ALU.add, tag=f"{tag}_wres{mj}{nt}")
            p.tt("v", xnew.t(mj, c0, c1), xnew.t(mj, c0, c1), xlold.t(0, 128, c0, c1),
                 ALU.add, tag=f"{tag}_wres2{mj}{nt}")
        xold.free()
        xlold.free()
    w_o.free()
    bo_col.free()
    oa.free()
    for b in (lng_col, lnb_col, lng_bf, lnb_bf):
        b.free()
    st.x = xnew



def build_nc(n_steps=None):
    if n_steps is None:
        n_steps = N_STEPS
    es = ExitStack()
    nc = bass.Bass("TRN2", target_bir_lowering=False)
    mk = MK(nc, es, n_steps)
    p = mk.p

    x_in = mk.din("x_in", [D, NCOL])
    y_out = mk.dout("y_out", [D, NCOL])

    WD = {}
    for nm in ["l0f_wf", "l0f_wi", "l0f_wc", "l0f_wo",
               "l0f_uf", "l0f_ui", "l0f_uc", "l0f_uo",
               "l0a_wk", "l0a_wka", "l0a_wv"]:
        WD[nm] = mk.din(nm, [D, D])
    for nm in ["l0f_bf", "l0f_bi", "l0f_bc", "l0f_bo", "l0f_inc",
               "l0a_bk", "l0a_bka", "l0a_bv", "l0a_gamma", "l0a_beta"]:
        WD[nm] = mk.din(nm, [D])
    WD["l1f_w_up"] = mk.din("l1f_w_up", [3072, D])
    WD["l1f_w_gate"] = mk.din("l1f_w_gate", [3072, D])
    WD["l1f_w_down"] = mk.din("l1f_w_down", [D, 3072])
    for nm in ["l1a_wq_lin", "l1a_wk_lin", "l1a_wv_lin", "l1a_wq_loc", "l1a_wk_loc",
               "l1a_wv_loc", "l1a_wo", "l1a_wg1", "l1a_wg1b", "l1a_wg1c", "l1a_wg2"]:
        WD[nm] = mk.din(nm, [D, D])
    for nm in ["l1a_bq_lin", "l1a_bk_lin", "l1a_bv_lin", "l1a_bq_loc", "l1a_bk_loc",
               "l1a_bv_loc", "l1a_bo", "l1a_bg1", "l1a_bg2", "l1a_ln_g", "l1a_ln_b"]:
        WD[nm] = mk.din(nm, [D])
    ident_in = mk.din("identity_in", [128, 128])
    p.add("g", (lambda e, o=mk.ident_tap.ap, a=ident_in.handle.ap()[:, :]:
                e.dma_start(out=o, in_=a)),
          reads=[ident_in.t()], writes=[mk.ident_tap], kind="d", tag="ldident")
    rank_in = mk.din("rank_ids", [2], I32)
    p.add("y", (lambda e, o=mk.rank_t.ap()[0:1, :], a=rank_in.handle.ap()[None, :]:
                e.dma_start(out=o, in_=a)),
          reads=[rank_in.t()], writes=[TAP(None, "rankt", (0, 1, 0, 2))], kind="d",
          tag="ldrank")

    pfx = mk.din("pfx_mask", [8])
    pfx_buf = mk.slab.alloc(8, F32, "pfx")
    mk.pfx_tap = TAP(mk.slab.handle.ap()[0:8, pfx_buf.boff:pfx_buf.boff + 4].bitcast(F32),
                     "slab", (0, 8, pfx_buf.boff, pfx_buf.boff + 4))
    p.add("y", (lambda e, o=mk.pfx_tap.ap, a=pfx.handle.ap()[:, None]: e.dma_start(out=o, in_=a)),
          reads=[pfx.t()], writes=[mk.pfx_tap], kind="d", tag="ldpfx")

    xch = mk.slab.falloc(CH, NCOL, F32, "x0")
    xin_r = x_in.handle.ap().rearrange("(c p) n -> p c n", p=128)
    for c in range(CH):
        p.add("y", (lambda e, c=c, o=xch.t(c).ap: e.dma_start(out=o, in_=xin_r[:, c, :])),
              reads=[x_in.t()], writes=[xch.t(c)], kind="d", tag=f"ldx{c}")

    st = State(mk, xch)
    st.x_spill = mk.dinternal("x_spill", [D, NCOL])
    if n_steps >= 4:
        gen = sub_l1_prep_gen(mk, st, WD)
        mk.pump = lambda n=2: [next(gen, None) for _ in range(n)]

    steps = [
        lambda: sub_l0_ffn(mk, st, WD, True),
        lambda: sub_l0_attn(mk, st, WD),
        lambda: sub_l0_ffn(mk, st, WD, False),
        lambda: sub_l1_ffn(mk, st, WD, True),
        lambda: sub_l1_attn(mk, st, WD),
        lambda: sub_l1_ffn(mk, st, WD, False),
    ]
    for i in range(n_steps):
        if i == 3:
            mk.pump(10000)
            for b in mk._dq_transients:
                b.free()
        steps[i]()

    yout_r = y_out.handle.ap().rearrange("(c p) n -> p c n", p=128)
    for c in range(CH):
        p.add("y", (lambda e, c=c, a=st.x.t(c).ap: e.dma_start(out=yout_r[:, c, :], in_=a)),
              reads=[st.x.t(c)], writes=[y_out.t()], kind="d", tag=f"sty{c}")

    p.finalize()
    with nc.Block() as block:
        p.emit_all(block)
    es.close()
    return nc



def stage_inputs(inputs):
    f32 = np.float32

    def T(a):
        return np.ascontiguousarray(np.asarray(a, f32).T)

    def V(a):
        return np.ascontiguousarray(np.asarray(a, f32))

    x = np.asarray(inputs["L0_attn_hidden_states"], f32)
    shared = {
        "l0f_wf": T(inputs["L0_ffn_wf"]), "l0f_wi": T(inputs["L0_ffn_wi"]),
        "l0f_wc": T(inputs["L0_ffn_wc"]), "l0f_wo": T(inputs["L0_ffn_wo"]),
        "l0f_uf": T(inputs["L0_ffn_uf"]), "l0f_ui": T(inputs["L0_ffn_ui"]),
        "l0f_uc": T(inputs["L0_ffn_uc"]), "l0f_uo": T(inputs["L0_ffn_uo"]),
        "l0f_bf": V(inputs["L0_ffn_bf"]), "l0f_bi": V(inputs["L0_ffn_bi"]),
        "l0f_bc": V(inputs["L0_ffn_bc"]), "l0f_bo": V(inputs["L0_ffn_bo"]),
        "l0f_inc": V(inputs["L0_ffn_in_c"]),
        "l0a_wk": T(inputs["L0_attn_Wk"]), "l0a_wka": T(inputs["L0_attn_Wka"]),
        "l0a_wv": T(inputs["L0_attn_Wv"]),
        "l0a_bk": V(inputs["L0_attn_bk"]), "l0a_bka": V(inputs["L0_attn_bka"]),
        "l0a_bv": V(inputs["L0_attn_bv"]),
        "l0a_gamma": V(inputs["L0_attn_gamma"]), "l0a_beta": V(inputs["L0_attn_beta"]),
        "l1f_w_up": V(inputs["L1_ffn_w_up"]), "l1f_w_gate": V(inputs["L1_ffn_w_gate"]),
        "l1f_w_down": V(inputs["L1_ffn_w_down"]),
        "l1a_wq_lin": V(inputs["L1_attn_wq_lin"]), "l1a_wk_lin": V(inputs["L1_attn_wk_lin"]),
        "l1a_wv_lin": V(inputs["L1_attn_wv_lin"]), "l1a_wq_loc": V(inputs["L1_attn_wq_loc"]),
        "l1a_wk_loc": V(inputs["L1_attn_wk_loc"]), "l1a_wv_loc": V(inputs["L1_attn_wv_loc"]),
        "l1a_wo": V(inputs["L1_attn_wo"]),
        "l1a_wg1": V(inputs["L1_attn_wg1"][0:1024]),
        "l1a_wg1b": V(inputs["L1_attn_wg1"][1024:2048]),
        "l1a_wg1c": V(inputs["L1_attn_wg1"][2048:3072]),
        "l1a_wg2": V(inputs["L1_attn_wg2"]),
        "l1a_bq_lin": V(inputs["L1_attn_bq_lin"]), "l1a_bk_lin": V(inputs["L1_attn_bk_lin"]),
        "l1a_bv_lin": V(inputs["L1_attn_bv_lin"]), "l1a_bq_loc": V(inputs["L1_attn_bq_loc"]),
        "l1a_bk_loc": V(inputs["L1_attn_bk_loc"]), "l1a_bv_loc": V(inputs["L1_attn_bv_loc"]),
        "l1a_bo": V(inputs["L1_attn_bo"]), "l1a_bg1": V(inputs["L1_attn_bg1"]),
        "l1a_bg2": V(inputs["L1_attn_bg2"]),
        "l1a_ln_g": V(inputs["L1_attn_ln_g"]), "l1a_ln_b": V(inputs["L1_attn_ln_b"]),
        "identity_in": np.eye(128, dtype=f32),
    }
    rel = np.asarray(inputs["L1_attn_rel_bias"], f32)
    pp, jj = np.meshgrid(np.arange(128), np.arange(192), indexing="ij")
    idx = np.clip(jj - pp + 32, 0, 128)
    btol = rel[idx, :]
    btol2 = np.ascontiguousarray(btol.transpose(0, 2, 1).reshape(128, 16 * 192))
    band = (jj - pp >= 0) & (jj - pp <= 64)
    maskI = np.where(band, 0.0, -1e30).astype(f32)
    shared["btol"] = btol2
    shared["wmaskI"] = maskI
    in_maps = []
    for k in range(NCORES):
        sl = slice(k * SL, (k + 1) * SL)
        xk = np.ascontiguousarray(x[:, sl, :].transpose(2, 0, 1).reshape(D, NCOL))
        m = dict(shared)
        m["x_in"] = xk
        m["pfx_mask"] = (np.arange(8) < k).astype(np.float32)
        mF = maskI.copy()
        if k == 0:
            mF = np.where(jj < 32, np.float32(-1e30), mF).astype(f32)
        mL = maskI.copy()
        if k == NCORES - 1:
            mL = np.where(jj >= 160, np.float32(-1e30), mL).astype(f32)
        m["wmaskF"] = mF
        m["wmaskL"] = mL
        m["rank_ids"] = np.array([(k - 1) % NCORES, (k + 1) % NCORES], np.int32)
        in_maps.append(m)
    return in_maps


def gather(results):
    out = np.empty((NB, SEQ, D), dtype=np.float32)
    for k in range(NCORES):
        yk = np.asarray(results[k]["y_out"])
        out[:, k * SL:(k + 1) * SL, :] = yk.reshape(D, NB, SL).transpose(1, 2, 0)
    return out


def kernel(**inputs):
    nc = build_nc()
    in_maps = stage_inputs(inputs)
    res = run_bass_kernel_spmd(nc, in_maps, core_ids=list(range(NCORES)))
    return gather(res.results)


# version 12 (final)
# speedup vs baseline: 1.0188x; 1.0188x over previous
import numpy as np
from contextlib import ExitStack

import concourse.bass as bass
import concourse.mybir as mybir
from concourse.bass_utils import run_bass_kernel_spmd
from concourse.replica_groups import maybe_share_collective_output_space

F32 = mybir.dt.float32
BF16 = mybir.dt.bfloat16
I32 = mybir.dt.int32
AF = mybir.ActivationFunctionType
ALU = mybir.AluOpType
AX = mybir.AxisListType.X

NCORES = 8
D = 1024
SEQ = 8192
NB = 2
SL = SEQ // NCORES
NCOL = NB * SL
CH = D // 128
HALO = 32
LN_EPS = 1e-5
NT512 = [(i * 512, (i + 1) * 512) for i in range(NCOL // 512)]

N_STEPS = 6
SLAB_BYTES = 204800
DEBUG_TAPS = []
W5_TRUNC = 99
W5_NBQ = None



def boxes_overlap(a, b):
    return not (a[1] <= b[0] or b[1] <= a[0] or a[3] <= b[2] or b[3] <= a[2])


def box_covers(o, i):
    return o[0] <= i[0] and o[1] >= i[1] and o[2] <= i[2] and o[3] >= i[3]


class TAP:
    __slots__ = ("ap", "name", "box")

    def __init__(self, ap, name, box):
        self.ap = ap
        self.name = name
        self.box = box


class Op:
    __slots__ = ("idx", "eng", "emit", "kind", "deps", "sem", "val",
                 "pre_waits", "drain_before", "tag")

    def __init__(self, idx, eng, emit, kind, tag):
        self.idx = idx
        self.eng = eng
        self.emit = emit
        self.kind = kind
        self.deps = set()
        self.sem = None
        self.val = 0
        self.pre_waits = []
        self.drain_before = False
        self.tag = tag


class Prog:
    ENGS = ("t", "v", "s", "g", "y")

    def __init__(self, nc, es, n_dma_sems=12):
        self.nc = nc
        self.es = es
        self.ops = []
        self.writers = {}
        self.readers = {}
        self.eng_sem = {e: es.enter_context(nc.semaphore(f"sem_{e}")) for e in self.ENGS}
        self.dma_pools = {e: [es.enter_context(nc.semaphore(f"dma_{e}{i}"))
                              for i in range(n_dma_sems)] for e in ("y", "g")}
        self.dma_rr = {e: 0 for e in ("y", "g")}
        self.dma_tot = {}
        self.cc_sems = []

    def _dep_read(self, op, taps):
        for t in taps:
            for (b, oi) in self.writers.get(t.name, ()):
                if oi != op.idx and boxes_overlap(b, t.box):
                    op.deps.add(oi)
            self.readers.setdefault(t.name, []).append((t.box, op.idx))

    def _dep_write(self, op, taps):
        for t in taps:
            ws = self.writers.setdefault(t.name, [])
            for (b, oi) in ws:
                if oi != op.idx and boxes_overlap(b, t.box):
                    op.deps.add(oi)
            rs = self.readers.setdefault(t.name, [])
            keep = []
            for (b, oi) in rs:
                if oi == op.idx:
                    keep.append((b, oi))
                    continue
                if boxes_overlap(b, t.box):
                    op.deps.add(oi)
                    if not box_covers(t.box, b):
                        keep.append((b, oi))
                else:
                    keep.append((b, oi))
            self.readers[t.name] = keep
            ws2 = [(b, oi) for (b, oi) in ws if not box_covers(t.box, b)]
            ws2.append((t.box, op.idx))
            self.writers[t.name] = ws2

    def add(self, eng, emit, reads=(), writes=(), kind="c", tag=""):
        op = Op(len(self.ops), eng, emit, kind, tag)
        self.ops.append(op)
        self._dep_read(op, reads)
        self._dep_write(op, writes)
        return op

    def tt(self, eng, out, in0, in1, op, tag=""):
        return self.add(eng, (lambda e, o=out.ap, a=in0.ap, b=in1.ap, alu=op:
                              e.tensor_tensor(out=o, in0=a, in1=b, op=alu)),
                        reads=[in0, in1], writes=[out], tag=tag or "tt")

    def ts(self, eng, out, in0, s1, op0, s2=None, op1=None, tag=""):
        s1ap = s1.ap if isinstance(s1, TAP) else s1
        rd = [in0] + ([s1] if isinstance(s1, TAP) else [])
        if op1 is None:
            return self.add(eng, (lambda e, o=out.ap, a=in0.ap, s=s1ap, alu=op0:
                                  e.tensor_scalar(out=o, in0=a, scalar1=s, scalar2=None, op0=alu)),
                            reads=rd, writes=[out], tag=tag or "ts")
        s2ap = s2.ap if isinstance(s2, TAP) else s2
        if isinstance(s2, TAP):
            rd.append(s2)
        return self.add(eng, (lambda e, o=out.ap, a=in0.ap, s=s1ap, s2_=s2ap, a0=op0, a1=op1:
                              e.tensor_scalar(out=o, in0=a, scalar1=s, scalar2=s2_, op0=a0, op1=a1)),
                        reads=rd, writes=[out], tag=tag or "ts2")

    def stt(self, eng, out, in0, scalar, in1, op0, op1, tag=""):
        sap = scalar.ap if isinstance(scalar, TAP) else scalar
        rd = [in0, in1] + ([scalar] if isinstance(scalar, TAP) else [])
        return self.add(eng, (lambda e, o=out.ap, a=in0.ap, s=sap, b=in1.ap, a0=op0, a1=op1:
                              e.scalar_tensor_tensor(out=o, in0=a, scalar=s, in1=b, op0=a0, op1=a1)),
                        reads=rd, writes=[out], tag=tag or "stt")

    def act(self, out, in_, func, bias=None, scale=None, accum=None, tag=""):
        rd = [in_]
        bap = bias.ap if isinstance(bias, TAP) else bias
        sap = scale.ap if isinstance(scale, TAP) else scale
        if isinstance(bias, TAP):
            rd.append(bias)
        if isinstance(scale, TAP):
            rd.append(scale)
        wr = [out] + ([accum] if accum is not None else [])

        def em(e, o=out.ap, a=in_.ap, f=func, b=bap, s=sap, acc=accum):
            kw = {}
            if b is not None:
                kw["bias"] = b
            if s is not None:
                kw["scale"] = s
            if acc is not None:
                kw["accum_out"] = acc.ap
            return e.activation(out=o, in_=a, func=f, **kw)
        return self.add("s", em, reads=rd, writes=wr, tag=tag or "act")

    def copy(self, eng, out, in_, tag=""):
        if eng == "s":
            return self.add(eng, (lambda e, o=out.ap, a=in_.ap: e.copy(out=o, in_=a)),
                            reads=[in_], writes=[out], tag=tag or "scopy")
        return self.add(eng, (lambda e, o=out.ap, a=in_.ap: e.tensor_copy(out=o, in_=a)),
                        reads=[in_], writes=[out], tag=tag or "copy")

    def red(self, eng, out, in_, op, absval=False, negate=False, tag=""):
        return self.add(eng, (lambda e, o=out.ap, a=in_.ap, al=op, av=absval, ng=negate:
                              e.tensor_reduce(out=o, in_=a, axis=AX, op=al,
                                              apply_absolute_value=av, negate=ng)),
                        reads=[in_], writes=[out], tag=tag or "red")

    def scan_add(self, eng, out, data, initial=0.0, tag=""):
        iap = initial.ap if isinstance(initial, TAP) else initial
        rd = [data] + ([initial] if isinstance(initial, TAP) else [])
        return self.add(eng, (lambda e, o=out.ap, d=data.ap, i=iap:
                              e.tensor_tensor_scan(out=o, data0=d, data1=d, initial=i,
                                                   op0=ALU.add, op1=ALU.bypass)),
                        reads=rd, writes=[out], tag=tag or "scan")

    def mmgroup(self, pairs, out, tag=""):
        rd = []
        for l, r in pairs:
            rd += [l, r]
        n = len(pairs)

        def em(e, ps=out.ap, pr=[(l.ap, r.ap) for l, r in pairs], n=n):
            inst = None
            for i, (la, ra) in enumerate(pr):
                inst = e.matmul(ps, la, ra, start=(i == 0), stop=(i == n - 1))
            return inst
        return self.add("t", em, reads=rd, writes=[out], tag=tag or "mm")

    def transpose(self, out, in_, ident, tag=""):
        return self.add("t", (lambda e, o=out.ap, a=in_.ap, i=ident.ap:
                              e.transpose(o, a, i)),
                        reads=[in_, ident], writes=[out], tag=tag or "tr")

    def coll(self, kind, op, cin, cout, rg, tag=""):
        return self.add("g", (lambda e, k=kind, o=op, a=cin.ap, b=cout.ap, g=rg:
                              e.collective_compute(k, o, ins=[a], outs=[b], replica_groups=g)),
                        reads=[cin], writes=[cout], kind="cc", tag=tag or kind)

    def finalize(self):
        eng_ord = {e: 0 for e in self.ENGS}
        for op in self.ops:
            if op.kind == "c":
                eng_ord[op.eng] += 1
                op.sem = self.eng_sem[op.eng]
                op.val = eng_ord[op.eng]
            elif op.kind == "d":
                pool = self.dma_pools[op.eng]
                sem = pool[self.dma_rr[op.eng] % len(pool)]
                self.dma_rr[op.eng] += 1
                prev = self.dma_tot.get(id(sem), 0)
                if prev:
                    op.pre_waits.append((sem, prev))
                self.dma_tot[id(sem)] = prev + 16
                op.sem = sem
                op.val = prev + 16
            elif op.kind == "cc":
                sem = self.es.enter_context(self.nc.semaphore(f"cc{len(self.cc_sems)}"))
                self.cc_sems.append(sem)
                op.sem = sem
                op.val = 1
        watermark = {}
        for op in self.ops:
            per_sem = {}
            for di in sorted(op.deps):
                dop = self.ops[di]
                if dop.sem is None:
                    continue
                k = id(dop.sem)
                if k not in per_sem or per_sem[k][1] < dop.val:
                    per_sem[k] = (dop.sem, dop.val)
            for k, (sem, val) in per_sem.items():
                wk = (op.eng, k)
                if watermark.get(wk, -1) >= val:
                    continue
                watermark[wk] = val
                op.pre_waits.append((sem, val))
        finals = []
        for e in self.ENGS:
            if eng_ord[e]:
                finals.append((self.eng_sem[e], eng_ord[e]))
        for e in ("y", "g"):
            for sem in self.dma_pools[e]:
                tot = self.dma_tot.get(id(sem), 0)
                if tot:
                    finals.append((sem, tot))
        for sem in self.cc_sems:
            finals.append((sem, 1))
        self.final_waits = finals

    def emit_all(self, block):
        per_eng = {e: [] for e in self.ENGS}
        for op in self.ops:
            per_eng[op.eng].append(op)
        prog = self
        attr = {"t": "tensor", "v": "vector", "s": "scalar", "g": "gpsimd", "y": "sync"}

        def mk_body(engname):
            ops = per_eng[engname]

            def body(eng):
                for op in ops:
                    for (sem, val) in op.pre_waits:
                        eng.wait_ge(sem, val)
                    if op.drain_before and engname != "t":
                        eng.drain()
                    inst = op.emit(eng)
                    if op.sem is not None:
                        assert inst is not None, f"op {op.idx} [{op.tag}] returned no inst"
                        inst.then_inc(op.sem, 16 if op.kind == "d" else 1)
                if engname == "y":
                    for (sem, val) in prog.final_waits:
                        eng.wait_ge(sem, val)
            body.__name__ = f"_{engname}"
            getattr(block, attr[engname])(body)
        for e in self.ENGS:
            mk_body(e)



DT_SIZE = {F32: 4, BF16: 2, I32: 4}


class Buf:
    __slots__ = ("slab", "name", "boff", "cols", "dtype")

    def __init__(self, slab, name, boff, cols, dtype):
        self.slab = slab
        self.name = name
        self.boff = boff
        self.cols = cols
        self.dtype = dtype

    def t(self, p0=0, p1=128, c0=0, c1=None):
        if c1 is None:
            c1 = self.cols
        sz = DT_SIZE[self.dtype]
        b0 = self.boff + c0 * sz
        b1 = self.boff + c1 * sz
        ap = self.slab.handle.ap()[p0:p1, b0:b1].bitcast(self.dtype)
        return TAP(ap, self.slab.name, (p0, p1, b0, b1))

    def free(self):
        self.slab.free(self)


class CBuf:
    def __init__(self, buf, nch, width):
        self.buf = buf
        self.nch = nch
        self.width = width

    def t(self, c, c0=0, c1=None, p0=0, p1=128):
        if c1 is None:
            c1 = self.width
        return self.buf.t(p0, p1, c * self.width + c0, c * self.width + c1)

    def free(self):
        self.buf.free()


class FBuf:

    def __init__(self, bufs, width):
        self.bufs = bufs
        self.width = width

    def t(self, c, c0=0, c1=None, p0=0, p1=128):
        if c1 is None:
            c1 = self.width
        return self.bufs[c].t(p0, p1, c0, c1)

    def free(self):
        for b in self.bufs:
            b.free()


class Slab:
    def __init__(self, nc, es, name, nbytes):
        self.handle = es.enter_context(nc.sbuf_tensor(name, [128, nbytes], mybir.dt.uint8))
        self.name = name
        self.free_list = [(0, nbytes)]
        self.peak = 0
        self.used = 0

    def alloc(self, cols, dtype, tag=""):
        nb = (cols * DT_SIZE[dtype] + 63) // 64 * 64
        cur = getattr(self, "cursor", 0)
        if nb >= 16384:
            order = list(range(len(self.free_list)))
        else:
            order = [i for i, (o, n) in enumerate(self.free_list) if o >= cur] + \
                    [i for i, (o, n) in enumerate(self.free_list) if o < cur]
        for i in order:
            o, n = self.free_list[i]
            if n >= nb:
                if n == nb:
                    self.free_list.pop(i)
                else:
                    self.free_list[i] = (o + nb, n - nb)
                self.used += nb
                self.peak = max(self.peak, self.used)
                if nb < 16384:
                    self.cursor = o + nb
                return Buf(self, tag, o, cols, dtype)
        raise MemoryError(f"slab alloc {nb}B failed (tag={tag}, used={self.used}, free={self.free_list})")

    def challoc(self, nch, width, dtype, tag=""):
        b = self.alloc(nch * width, dtype, tag)
        return CBuf(b, nch, width)

    def falloc(self, nch, width, dtype, tag=""):
        return FBuf([self.alloc(width, dtype, f"{tag}{c}") for c in range(nch)], width)

    def free(self, buf):
        nb = (buf.cols * DT_SIZE[buf.dtype] + 63) // 64 * 64
        self.used -= nb
        self.free_list.append((buf.boff, nb))
        self.free_list.sort()
        merged = []
        for o, n in self.free_list:
            if merged and merged[-1][0] + merged[-1][1] == o:
                merged[-1] = (merged[-1][0], merged[-1][1] + n)
            else:
                merged.append((o, n))
        self.free_list = merged


class DBuf:
    def __init__(self, handle, name):
        self.handle = handle
        self.name = name

    def t(self, box=None):
        return TAP(self.handle.ap(), self.name, box or (0, 1, 0, 1 << 50))

    def box(self, b):
        return TAP(None, self.name, b)


class MK:
    def __init__(self, nc, es, n_steps):
        self.nc = nc
        self.es = es
        self.n_steps = n_steps
        self.p = Prog(nc, es)
        self.slab = Slab(nc, es, "slab", SLAB_BYTES)
        self.rg = [list(range(NCORES))]
        self.inputs = {}
        self._cc_ctr = 0
        self.pump = lambda n=2: None
        self._warm_ctr = 0
        self.psF = []
        for i in range(6):
            h = es.enter_context(nc.psum_tensor(f"psF{i}", [128, 512], F32))
            self.psF.append(_PsBuf(h, f"psF{i}"))
        self.psB = []
        for i in range(2):
            h = es.enter_context(nc.psum_tensor(f"psB{i}", [128, 1024], BF16))
            self.psB.append(_PsBuf(h, f"psB{i}"))
        self.ones_col_bf = es.enter_context(nc.sbuf_tensor("ones_col", [128, 2], BF16))
        self.ones128_bf = es.enter_context(nc.sbuf_tensor("ones128", [128, 128], BF16))
        self.p.add("v", lambda e: e.memset(self.ones_col_bf.ap()[:, :], 1.0),
                   writes=[TAP(None, "ones_col", (0, 128, 0, 2))], tag="ones_c")
        self.p.add("v", lambda e: e.memset(self.ones128_bf.ap()[:, :], 1.0),
                   writes=[TAP(None, "ones128", (0, 128, 0, 128))], tag="ones_m")
        self.ones_row_f32_t = es.enter_context(nc.sbuf_tensor("ones_rf", [1, 128], F32))
        self.p.add("v", lambda e: e.memset(self.ones_row_f32_t.ap()[:, :], 1.0),
                   writes=[TAP(None, "ones_rf", (0, 1, 0, 128))], tag="ones_rf")
        self.ones_row_bf_t = es.enter_context(nc.sbuf_tensor("ones_rb", [1, 128], BF16))
        self.p.add("v", lambda e: e.memset(self.ones_row_bf_t.ap()[:, :], 1.0),
                   writes=[TAP(None, "ones_rb", (0, 1, 0, 128))], tag="ones_rb")
        self.ones512_t = es.enter_context(nc.sbuf_tensor("ones512", [1, 512], BF16))
        self.p.add("v", lambda e: e.memset(self.ones512_t.ap()[:, :], 1.0),
                   writes=[TAP(None, "ones512", (0, 1, 0, 512))], tag="ones512")
        self.ident_t = es.enter_context(nc.sbuf_tensor("ident", [128, 128], BF16))
        self.ident_tap = TAP(self.ident_t.ap()[:, :], "ident", (0, 128, 0, 128))
        self.rank_t = es.enter_context(nc.sbuf_tensor("rankt", [1, 2], I32))

    def warm(self):
        ps = self.psF[5].t(0, 64, 0, 64)
        self.p.add("t", (lambda e, o=ps.ap, a=self.ident_tap.ap:
                         e.matmul(o, a[:, 0:64], a[:, 0:64], start=True, stop=True)),
                   reads=[self.ident_tap], writes=[ps], tag="warm")

    def ones_col(self):
        return TAP(self.ones_col_bf.ap()[:, 0:1], "ones_col", (0, 128, 0, 1))

    def ones_row_f32(self):
        return TAP(self.ones_row_f32_t.ap()[:, :], "ones_rf", (0, 1, 0, 128))

    def ones512(self):
        return TAP(self.ones512_t.ap()[:, :], "ones512", (0, 1, 0, 512))

    def ones_row_bf(self, *a):
        return TAP(self.ones_row_bf_t.ap()[:, :], "ones_rb", (0, 1, 0, 128))

    def rank_tap(self, side):
        return TAP(self.rank_t.ap()[0:1, side:side + 1], "rankt", (0, 1, side, side + 1))

    def ones128(self):
        return TAP(self.ones128_bf.ap()[:, :], "ones128", (0, 128, 0, 128))

    def din(self, name, shape, dtype=F32):
        h = self.nc.dram_tensor(name, list(shape), dtype, kind="ExternalInput")
        b = DBuf(h, name)
        self.inputs[name] = b
        return b

    def dinternal(self, name, shape, dtype=F32, shared_for=None):
        kw = {}
        if shared_for is not None:
            kw["addr_space"] = maybe_share_collective_output_space(shared_for, self.rg,
                                                                   prefer_local=True)
        h = self.nc.dram_tensor(name, list(shape), dtype, kind="Internal", **kw)
        return DBuf(h, name)

    def dout(self, name, shape, dtype=F32):
        h = self.nc.dram_tensor(name, list(shape), dtype, kind="ExternalOutput")
        return DBuf(h, name)

    def tap(self, name, fb, nch, width, dtype=F32):
        if name not in DEBUG_TAPS:
            return
        d = self.dout(f"dbg_{name}", [nch * 128, width], dtype)
        dr = d.handle.ap().rearrange("(c p) n -> p c n", p=128)
        for c in range(nch):
            self.p.add("y", (lambda e, c=c, a=fb.t(c).ap: e.dma_start(out=dr[:, c, :], in_=a)),
                       reads=[fb.t(c)], writes=[d.box((0, 1, c, c + 1))], kind="d",
                       tag=f"tap_{name}{c}")

    def load_weight_bf16(self, dram, E, e0=0, e1=None, tag=""):
        if e1 is None:
            e1 = E
        w = e1 - e0
        cb = self.slab.challoc(CH, w, BF16, tag or dram.name)
        dst = cb.buf.t()
        dstap = dst.ap.rearrange("p (c n) -> p c n", c=CH)
        src = dram.handle.ap().rearrange("(c p) n -> p c n", p=128)[:, :, e0:e1]
        self.p.add("g", (lambda e, o=dstap, a=src: e.dma_start(out=o, in_=a)),
                   reads=[dram.t()], writes=[dst], kind="d", tag=f"ldw_{dram.name}")
        return cb

    def load_col(self, dram, tag=""):
        b = self.slab.alloc(CH, F32, tag or ("col_" + dram.name))
        dst = b.t()
        src = dram.handle.ap().rearrange("(c p) -> p c", p=128)
        self.p.add("y", (lambda e, o=dst.ap, a=src:
                         e.dma_start(out=o, in_=a, allow_slow_non_contiguous=True)),
                   reads=[dram.t()], writes=[dst], kind="d", tag=f"ldc_{dram.name}")
        return b


class _PsBuf:

    def __init__(self, handle, name):
        self.handle = handle
        self.name = name

    def t(self, p0=0, p1=128, c0=0, c1=None):
        if c1 is None:
            c1 = self.handle.shape[1]
        ap = self.handle.ap()[p0:p1, c0:c1]
        return TAP(ap, self.name, (0, 128, 0, self.handle.shape[1]))



class State:
    def __init__(self, mk, xch):
        self.mk = mk
        self.x = xch
        self.x_spill = None
        self.l0f_cache = None


def ln_stats(mk, st, tag, want_nmr_bc=False):
    p, slab = mk.p, mk.slab
    x = st.x
    xbf = slab.falloc(CH, NCOL, BF16, f"xbf_{tag}")
    for c in range(CH):
        p.copy("v", xbf.t(c), x.t(c), tag=f"{tag}_xbf{c}")
    R_bc = slab.alloc(NCOL, F32, f"Rbc_{tag}")
    nmr_row = slab.alloc(NCOL, BF16, f"nmrrow_{tag}")
    nmr_bc = slab.alloc(NCOL, F32, f"nmrbc_{tag}") if want_nmr_bc else None
    x2 = slab.falloc(CH, 512, BF16, f"x2_{tag}")
    cinv = 1.0 / D
    for nt, (c0, c1) in enumerate(NT512):
        pa = mk.psF[0].t()
        pb = mk.psF[1].t()
        for c in range(CH):
            p.act(x2.t(c), xbf.t(c, c0, c1), AF.Square, tag=f"{tag}_x2_{nt}{c}")
        p.mmgroup([(mk.ones128(), xbf.t(c, c0, c1)) for c in range(CH)], pa,
                  tag=f"{tag}_Sx{nt}")
        p.mmgroup([(mk.ones128(), x2.t(c)) for c in range(CH)], pb,
                  tag=f"{tag}_Sxx{nt}")
        mu = slab.alloc(512, F32, "mu")
        te = slab.alloc(512, F32, "te")
        vpe = slab.alloc(512, F32, "vpe")
        r0 = slab.alloc(512, F32, "r0")
        p.ts("v", mu.t(), pa, cinv, ALU.mult, tag=f"{tag}_mu{nt}")
        p.tt("g", te.t(), mu.t(), mu.t(), ALU.mult, tag=f"{tag}_mu2{nt}")
        p.stt("v", vpe.t(), pb, cinv, te.t(), ALU.mult, ALU.subtract, tag=f"{tag}_var{nt}")
        p.ts("g", vpe.t(), vpe.t(), LN_EPS, ALU.add, tag=f"{tag}_vpe{nt}")
        p.act(te.t(), vpe.t(), AF.Ln, tag=f"{tag}_ln{nt}")
        p.act(r0.t(), te.t(), AF.Exp, scale=-0.5, tag=f"{tag}_rsq0{nt}")
        p.tt("g", te.t(), r0.t(), r0.t(), ALU.mult, tag=f"{tag}_nwa{nt}")
        p.tt("g", te.t(), te.t(), vpe.t(), ALU.mult, tag=f"{tag}_nwb{nt}")
        p.ts("v", te.t(), te.t(), -0.5, ALU.mult, 1.5, ALU.add, tag=f"{tag}_nwc{nt}")
        p.tt("v", R_bc.t(0, 128, c0, c1), r0.t(), te.t(), ALU.mult, tag=f"{tag}_R{nt}")
        p.stt("v", nmr_row.t(0, 1, c0, c1), mu.t(0, 1), -1.0, R_bc.t(0, 1, c0, c1),
              ALU.mult, ALU.mult, tag=f"{tag}_nmr{nt}")
        mk.warm()
        if want_nmr_bc:
            p.stt("v", nmr_bc.t(0, 128, c0, c1), mu.t(), -1.0, R_bc.t(0, 128, c0, c1),
                  ALU.mult, ALU.mult, tag=f"{tag}_nmrb{nt}")
        for b in (mu, te, vpe, r0):
            b.free()
    x2.free()
    return {"R_bc": R_bc, "nmr_row": nmr_row, "xbf": xbf, "nmr_bc": nmr_bc}


def spill_x(mk, st, tag):
    p = mk.p
    sp = st.x_spill
    spr = sp.handle.ap().rearrange("(c p) n -> p c n", p=128)
    for c in range(CH):
        p.add("y", (lambda e, c=c, a=st.x.t(c).ap: e.dma_start(out=spr[:, c, :], in_=a)),
              reads=[st.x.t(c)], writes=[sp.box((0, 1, c, c + 1))], kind="d",
              tag=f"{tag}_spill{c}")


def restream_x_chunk(mk, st, c, dstbuf, tag):
    p = mk.p
    spr = st.x_spill.handle.ap().rearrange("(c p) n -> p c n", p=128)
    p.add("y", (lambda e, o=dstbuf.t().ap, a=spr[:, c, :]: e.dma_start(out=o, in_=a)),
          reads=[st.x_spill.box((0, 1, c, c + 1))], writes=[dstbuf.t()],
          kind="d", tag=f"{tag}_restr{c}")


def load_row_bf(mk, dram, E, tag):
    p, slab = mk.p, mk.slab
    r32 = slab.alloc(E, F32, f"lr_{tag}")
    p.add("y", (lambda e, o=r32.t(0, 1).ap, a=dram.handle.ap()[None, :]:
                e.dma_start(out=o, in_=a)),
          reads=[dram.t()], writes=[r32.t(0, 1)], kind="d", tag=f"{tag}_ld")
    rb = slab.alloc(E, BF16, f"lrb_{tag}")
    p.copy("v", rb.t(0, 1), r32.t(0, 1), tag=f"{tag}_cast")
    r32.free()
    return rb


def proj_cs_row(mk, wcb, E, gcol_bf=None, tag=""):
    p, slab = mk.p, mk.slab
    csrow = slab.alloc(E, BF16, f"cs_{tag}")
    for s in range(E // 512):
        ps = mk.psF[2].t(0, 1, 0, 512)
        if gcol_bf is None:
            pairs = [(mk.ones_col(), wcb.t(c, s * 512, (s + 1) * 512)) for c in range(CH)]
        else:
            pairs = [(gcol_bf.t(0, 128, c, c + 1), wcb.t(c, s * 512, (s + 1) * 512))
                     for c in range(CH)]
        p.mmgroup(pairs, ps, tag=f"{tag}_cs{s}")
        p.copy("v", csrow.t(0, 1, s * 512, (s + 1) * 512), ps, tag=f"{tag}_csev{s}")
    return csrow



def sub_l0_ffn(mk, st, WD, first):
    p, slab = mk.p, mk.slab
    tag = "f1" if first else "f2"
    w_w = {g: mk.load_weight_bf16(WD[f"l0f_w{g}"], D, tag=f"{tag}_w{g}") for g in "fic"}
    ln = ln_stats(mk, st, tag)
    ln["xbf"].free()
    xs = slab.falloc(CH, NCOL, BF16, f"xs_{tag}")
    for c in range(CH):
        p.stt("v", xs.t(c), st.x.t(c), 1.0, ln["R_bc"].t(), ALU.mult, ALU.mult,
              tag=f"{tag}_xs{c}")
    mk.tap(f"{tag}_xs", xs, CH, NCOL, BF16)
    spill_x(mk, st, tag)
    st.x.free()
    ln["R_bc"].free()
    nmr = ln["nmr_row"]

    w_u = {g: mk.load_weight_bf16(WD[f"l0f_u{g}"], D, tag=f"{tag}_u{g}") for g in "fic"}
    cs = {g: proj_cs_row(mk, w_w[g], D, tag=f"{tag}_cs{g}") for g in "fic"}
    bcol = {g: mk.load_col(WD[f"l0f_b{g}"], tag=f"{tag}_b{g}") for g in "fico"}
    inc_col = mk.load_col(WD["l0f_inc"], tag=f"{tag}_inc")
    inc_bf = slab.alloc(CH, BF16, f"incbf_{tag}")
    p.copy("v", inc_bf.t(), inc_col.t(), tag=f"{tag}_incbf")

    g0b = {}
    for g in "fic":
        col = slab.alloc(CH, F32, f"{tag}_g0b{g}")
        for mj in range(CH):
            ps = mk.psF[3].t(0, 128, 0, 1)
            p.mmgroup([(w_u[g].t(c, mj * 128, (mj + 1) * 128), inc_bf.t(0, 128, c, c + 1))
                       for c in range(CH)], ps, tag=f"{tag}_g0b{g}{mj}")
            p.copy("v", col.t(0, 128, mj, mj + 1), ps, tag=f"{tag}_g0bev{g}{mj}")
        p.tt("v", col.t(), col.t(), bcol[g].t(), ALU.add, tag=f"{tag}_g0bb{g}")
        g0b[g] = col

    cbf = slab.falloc(CH, SL, BF16, f"cbf_{tag}")
    cbf2 = slab.falloc(CH, SL, BF16, f"cbf2_{tag}")

    for tstep in range(2):
        base = tstep * SL
        for mj in range(CH):
            for half in range(SL // 512):
                c0, c1 = half * 512, (half + 1) * 512
                banks = [mk.psF[i] for i in ((0, 1, 2) if (mj + half) % 2 == 0 else (3, 4, 5))]
                gps = {}
                for gi, g in enumerate("fic"):
                    pairs = [(w_w[g].t(c, mj * 128, (mj + 1) * 128), xs.t(c, base + c0, base + c1))
                             for c in range(CH)]
                    if tstep == 1:
                        pairs += [(w_u[g].t(c, mj * 128, (mj + 1) * 128), cbf.t(c, c0, c1))
                                  for c in range(CH)]
                    pairs.append((cs[g].t(0, 1, mj * 128, (mj + 1) * 128),
                                  nmr.t(0, 1, base + c0, base + c1)))
                    ps = banks[gi].t(0, 128, 0, 512)
                    p.mmgroup(pairs, ps, tag=f"{tag}_t{tstep}{g}m{mj}h{half}")
                    gps[g] = ps
                bsrc = g0b if tstep == 0 else bcol
                tf = slab.alloc(512, F32, "tf")
                ti = slab.alloc(512, F32, "ti")
                tg = slab.alloc(512, F32, "tg")
                p.act(tf.t(), gps["f"], AF.Sigmoid, bias=bsrc["f"].t(0, 128, mj, mj + 1),
                      tag=f"{tag}_sf{tstep}{mj}{half}")
                p.act(ti.t(), gps["i"], AF.Sigmoid, bias=bsrc["i"].t(0, 128, mj, mj + 1),
                      tag=f"{tag}_si{tstep}{mj}{half}")
                p.act(tg.t(), gps["c"], AF.Tanh, bias=bsrc["c"].t(0, 128, mj, mj + 1),
                      tag=f"{tag}_tg{tstep}{mj}{half}")
                pr = slab.alloc(512, F32, "pr")
                p.tt("v", pr.t(), ti.t(), tg.t(), ALU.mult, tag=f"{tag}_ig{tstep}{mj}{half}")
                if tstep == 0:
                    p.stt("v", cbf.t(mj, c0, c1), tf.t(), inc_col.t(0, 128, mj, mj + 1), pr.t(),
                          ALU.mult, ALU.add, tag=f"{tag}_c1{mj}{half}")
                else:
                    p.tt("v", tf.t(), tf.t(), cbf.t(mj, c0, c1), ALU.mult, tag=f"{tag}_fc{mj}{half}")
                    p.tt("v", cbf2.t(mj, c0, c1), tf.t(), pr.t(), ALU.add, tag=f"{tag}_c2{mj}{half}")
                for b in (tf, ti, tg, pr):
                    b.free()
                mk.pump()
                mk.warm()
        if tstep == 0:
            mk.tap(f"{tag}_c1", cbf, CH, SL, BF16)

    mk.tap(f"{tag}_cbf", cbf2, CH, SL, BF16)
    cbf.free()
    cbf = cbf2
    for g in "fic":
        w_u[g].free()
        cs[g].free()
        g0b[g].free()
    w_w["f"].free()
    w_w["i"].free()
    w_w["c"].free()

    w_wo = mk.load_weight_bf16(WD["l0f_wo"], D, tag=f"{tag}_wo")
    w_uo = mk.load_weight_bf16(WD["l0f_uo"], D, tag=f"{tag}_uo")
    cs_o = proj_cs_row(mk, w_wo, D, tag=f"{tag}_cso")
    xnew = slab.falloc(CH, NCOL, F32, f"xnew_{tag}")
    for mj in range(CH):
        xold = slab.alloc(NCOL, F32, f"xold{mj}")
        restream_x_chunk(mk, st, mj, xold, tag)
        for nt, (c0, c1) in enumerate(NT512):
            pairs = [(w_wo.t(c, mj * 128, (mj + 1) * 128), xs.t(c, c0, c1)) for c in range(CH)]
            cc0 = c0 % SL
            pairs += [(w_uo.t(c, mj * 128, (mj + 1) * 128), cbf.t(c, cc0, cc0 + 512))
                      for c in range(CH)]
            pairs.append((cs_o.t(0, 1, mj * 128, (mj + 1) * 128), nmr.t(0, 1, c0, c1)))
            ps = mk.psF[(2 * mj + nt) % 6].t(0, 128, 0, 512)
            p.mmgroup(pairs, ps, tag=f"{tag}_o{mj}{nt}")
            th = slab.alloc(512, F32, "th")
            p.act(th.t(), ps, AF.Tanh, bias=bcol["o"].t(0, 128, mj, mj + 1),
                  tag=f"{tag}_oth{mj}{nt}")
            p.stt("v", xnew.t(mj, c0, c1), th.t(), 0.5, xold.t(0, 128, c0, c1),
                  ALU.mult, ALU.add, tag=f"{tag}_res{mj}{nt}")
            th.free()
            mk.pump()
            mk.warm()
        xold.free()
    for b in (w_wo, w_uo, cs_o, xs, cbf, nmr, inc_col, inc_bf):
        b.free()
    for g in "fico":
        bcol[g].free()
    st.x = xnew


def matcol(mk, wcb, rhs_col_bf, extra_col=None, tag=""):
    p, slab = mk.p, mk.slab
    col = slab.alloc(CH, F32, f"mc_{tag}")
    ps = mk.psF[2]
    for mj in range(CH):
        pt = ps.t(0, 128, mj, mj + 1)
        p.mmgroup([(wcb.t(c, mj * 128, (mj + 1) * 128), rhs_col_bf.t(0, 128, c, c + 1))
                   for c in range(CH)], pt, tag=f"{tag}_mc{mj}")
    p.copy("v", col.t(), ps.t(0, 128, 0, CH), tag=f"{tag}_mcev")
    if extra_col is not None:
        p.tt("v", col.t(), col.t(), extra_col.t(), ALU.add, tag=f"{tag}_mcb")
    return col


def proj_fm(mk, src_bf, wcb, E, bias_col, evac, tag, cs_row=None, nmr=None,
            ncols=NCOL, extra_pairs=None, bias_row=None):
    p = mk.p
    nnt = ncols // 512
    for mj in range(E // 128):
        for nt in range(nnt):
            c0, c1 = nt * 512, (nt + 1) * 512
            pairs = [(wcb.t(c, mj * 128, (mj + 1) * 128), src_bf.t(c, c0, c1))
                     for c in range(CH)]
            if extra_pairs is not None:
                pairs += extra_pairs(mj, c0, c1)
            if cs_row is not None:
                pairs.append((cs_row.t(0, 1, mj * 128, (mj + 1) * 128), nmr.t(0, 1, c0, c1)))
            if bias_row is not None:
                pairs.append((bias_row.t(0, 1, mj * 128, (mj + 1) * 128), mk.ones512()))
            ps = mk.psF[(mj * nnt + nt) % 5].t()
            p.mmgroup(pairs, ps, tag=f"{tag}_p{mj}_{nt}")
            bias = bias_col.t(0, 128, mj, mj + 1) if bias_col is not None else None
            evac(mj, c0, c1, ps, bias)
            mk.pump()
            mk.warm()


def sub_l0_attn(mk, st, WD):
    p, slab = mk.p, mk.slab
    tag = "a0"
    w_k = mk.load_weight_bf16(WD["l0a_wk"], D, tag=f"{tag}_wk")
    gamma_col = mk.load_col(WD["l0a_gamma"], tag=f"{tag}_g")
    beta_col = mk.load_col(WD["l0a_beta"], tag=f"{tag}_b")
    bk_col = mk.load_col(WD["l0a_bk"], tag=f"{tag}_bk")
    bka_col = mk.load_col(WD["l0a_bka"], tag=f"{tag}_bka")
    bv_col = mk.load_col(WD["l0a_bv"], tag=f"{tag}_bv")
    ln = ln_stats(mk, st, tag)
    ln["xbf"].free()
    gbf = slab.alloc(CH, BF16, f"{tag}_gbf")
    bbf = slab.alloc(CH, BF16, f"{tag}_bbf")
    p.copy("v", gbf.t(), gamma_col.t(), tag=f"{tag}_gbf")
    p.copy("v", bbf.t(), beta_col.t(), tag=f"{tag}_bbf")
    xg = slab.falloc(CH, NCOL, BF16, f"xg_{tag}")
    for c in range(CH):
        p.stt("v", xg.t(c), st.x.t(c), gamma_col.t(0, 128, c, c + 1), ln["R_bc"].t(),
              ALU.mult, ALU.mult, tag=f"{tag}_xg{c}")
    spill_x(mk, st, tag)
    st.x.free()
    ln["R_bc"].free()
    nmr = ln["nmr_row"]

    cs_k = proj_cs_row(mk, w_k, D, gcol_bf=gbf, tag=f"{tag}_csk")
    br_k = bias_row_bf(mk, w_k, D, bbf, WD["l0a_bk"], f"{tag}_brk")
    keys = slab.falloc(CH, NCOL, BF16, f"keys_{tag}")

    def ev_keys(mj, c0, c1, ps, bias):
        p.copy("s", keys.t(mj, c0, c1), ps, tag=f"{tag}_evk{mj}")
    proj_fm(mk, xg, w_k, D, None, ev_keys, f"{tag}_k", cs_row=cs_k, nmr=nmr,
            bias_row=br_k)
    w_k.free()
    cs_k.free()
    br_k.free()

    w_v = mk.load_weight_bf16(WD["l0a_wv"], D, tag=f"{tag}_wv")
    cs_v = proj_cs_row(mk, w_v, D, gcol_bf=gbf, tag=f"{tag}_csv")
    br_v = bias_row_bf(mk, w_v, D, bbf, WD["l0a_bv"], f"{tag}_brv")
    values = slab.falloc(CH, NCOL, BF16, f"vals_{tag}")

    def ev_vals(mj, c0, c1, ps, bias):
        p.copy("s", values.t(mj, c0, c1), ps, tag=f"{tag}_evv{mj}")
    proj_fm(mk, xg, w_v, D, None, ev_vals, f"{tag}_v", cs_row=cs_v, nmr=nmr,
            bias_row=br_v)
    w_v.free()
    cs_v.free()
    br_v.free()
    xg.free()
    nmr.free()
    v_spill = mk.dinternal(f"v_spill", [D, NCOL], BF16)
    vsp = v_spill.handle.ap().rearrange("(c p) n -> p c n", p=128)
    for c in range(CH):
        p.add("y", (lambda e, c=c, a=values.t(c).ap: e.dma_start(out=vsp[:, c, :], in_=a)),
              reads=[values.t(c)], writes=[v_spill.box((0, 1, c, c + 1))], kind="d",
              tag=f"{tag}_vsp{c}")
    values.free()

    w_ka = mk.load_weight_bf16(WD["l0a_wka"], D, tag=f"{tag}_wka")
    br_ka = load_row_bf(mk, WD["l0a_bka"], D, f"{tag}_brka")
    katt = slab.falloc(CH, NCOL, BF16, f"katt_{tag}")

    def ev_katt(mj, c0, c1, ps, bias):
        p.copy("s", katt.t(mj, c0, c1), ps, tag=f"{tag}_evka{mj}")
    proj_fm(mk, keys, w_ka, D, None, ev_katt, f"{tag}_ka", bias_row=br_ka)
    w_ka.free()
    br_ka.free()

    A = slab.falloc(CH, NCOL, F32, f"A_{tag}")
    T1 = slab.alloc(16, F32, f"T1_{tag}")
    tmp2 = [slab.alloc(NCOL, F32, f"tmpa{i}") for i in range(2)]
    for c in range(CH):
        tm = tmp2[c % 2]
        p.act(tm.t(), katt.t(c), AF.Exp, scale=1.0 / 32.0, tag=f"{tag}_kw1_{c}")
        for b in range(NB):
            p.scan_add("v", A.t(c, b * SL, (b + 1) * SL), tm.t(0, 128, b * SL, (b + 1) * SL),
                       tag=f"{tag}_sc1_{c}{b}")
            p.copy("v", T1.t(0, 128, b * CH + c, b * CH + c + 1),
                   A.t(c, (b + 1) * SL - 1, (b + 1) * SL), tag=f"{tag}_t1_{c}{b}")
            mk.pump()
            mk.warm()

    carry1 = carry_exchange(mk, T1, f"{tag}_x1")
    T1.free()

    T2 = slab.alloc(16, F32, f"T2_{tag}")
    for c in range(CH):
        tm = tmp2[c % 2]
        zt = slab.alloc(NCOL, F32, "zshared")
        for b in range(NB):
            p.ts("v", A.t(c, b * SL, (b + 1) * SL), A.t(c, b * SL, (b + 1) * SL),
                 carry1.t(0, 128, b * CH + c, b * CH + c + 1), ALU.add, tag=f"{tag}_Ac{c}{b}")
        p.act(tm.t(), A.t(c), AF.Ln, tag=f"{tag}_lnA{c}")
        p.stt("v", tm.t(), katt.t(c), 1.0 / 32.0, tm.t(), ALU.mult, ALU.subtract,
              tag=f"{tag}_dv{c}")
        p.act(tm.t(), tm.t(), AF.Exp, tag=f"{tag}_kw2_{c}")
        p.tt("v", zt.t(), tm.t(), keys.t(c), ALU.mult, tag=f"{tag}_z{c}")
        for b in range(NB):
            p.scan_add("v", A.t(c, b * SL, (b + 1) * SL), zt.t(0, 128, b * SL, (b + 1) * SL),
                       tag=f"{tag}_sc2_{c}{b}")
            p.copy("v", T2.t(0, 128, b * CH + c, b * CH + c + 1),
                   A.t(c, (b + 1) * SL - 1, (b + 1) * SL), tag=f"{tag}_t2_{c}{b}")
            mk.pump()
            mk.warm()
        zt.free()
    katt.free()
    keys.free()
    for tmq in tmp2:
        tmq.free()

    carry2 = carry_exchange(mk, T2, f"{tag}_x2")
    T2.free()

    xnew = slab.falloc(CH, NCOL, F32, f"xnew_{tag}")
    values = slab.falloc(CH, NCOL, BF16, f"vals2_{tag}")
    for c in range(CH):
        p.add("y", (lambda e, c=c, o=values.t(c).ap: e.dma_start(out=o, in_=vsp[:, c, :])),
              reads=[v_spill.box((0, 1, c, c + 1))], writes=[values.t(c)], kind="d",
              tag=f"{tag}_vld{c}")
        xold = slab.alloc(NCOL, F32, f"xoldA{c}")
        restream_x_chunk(mk, st, c, xold, tag)
        for b in range(NB):
            p.ts("v", A.t(c, b * SL, (b + 1) * SL), A.t(c, b * SL, (b + 1) * SL),
                 carry2.t(0, 128, b * CH + c, b * CH + c + 1), ALU.add, tag=f"{tag}_Pc{c}{b}")
        p.tt("v", A.t(c), A.t(c), values.t(c), ALU.mult, tag=f"{tag}_pv{c}")
        p.tt("v", xnew.t(c), A.t(c), xold.t(), ALU.add, tag=f"{tag}_out{c}")
        mk.pump()
        mk.warm()
        xold.free()
    A.free()
    values.free()
    carry1.free()
    carry2.free()
    for b in (gamma_col, beta_col, bk_col, bka_col, bv_col, gbf, bbf):
        b.free()
    st.x = xnew


def carry_exchange(mk, Tcol, tag):
    p, slab = mk.p, mk.slab
    cin = mk.dinternal(f"cc{mk._cc_ctr}_in", [128 * 16], F32)
    cout = mk.dinternal(f"cc{mk._cc_ctr}_out", [NCORES * 128 * 16], F32,
                        shared_for="AllGather")
    mk._cc_ctr += 1
    cin_ap = cin.handle.ap().rearrange("(p j) -> p j", p=128)
    p.add("g", (lambda e, o=cin_ap, a=Tcol.t().ap: e.dma_start(out=o, in_=a)),
          reads=[Tcol.t()], writes=[cin.t()], kind="d", tag=f"{tag}_st")
    p.coll("AllGather", ALU.bypass, cin.t(), cout.t(), mk.rg, tag=f"{tag}_ag")
    tall = slab.alloc(NCORES * 16 * 16, F32, f"tall_{tag}")
    tall_tap = TAP(tall.slab.handle.ap()[0:8, tall.boff:tall.boff + 2048 * 4].bitcast(F32),
                   tall.slab.name, (0, 8, tall.boff, tall.boff + 2048 * 4))
    cout_ap = cout.handle.ap().rearrange("(r n) -> r n", r=NCORES)
    p.add("g", (lambda e, o=tall_tap.ap, a=cout_ap: e.dma_start(out=o, in_=a)),
          reads=[cout.t()], writes=[tall_tap], kind="d", tag=f"{tag}_ld")
    carry = slab.alloc(16, F32, f"carry_{tag}")
    ps = mk.psF[3]
    tall3 = tall_tap.ap.rearrange("r (pp j) -> r j pp", j=16)
    for j in range(16):
        p.mmgroup([(TAP(tall3[:, j, :], tall_tap.name, tall_tap.box), mk.pfx_tap)],
                  ps.t(0, 128, j, j + 1), tag=f"{tag}_car{j}")
    p.copy("v", carry.t(), ps.t(0, 128, 0, 16), tag=f"{tag}_carev")
    tall.free()
    return carry



def gmax_exchange(mk, col_tap, k, tag):
    p, slab = mk.p, mk.slab
    cin = mk.dinternal(f"cc{mk._cc_ctr}_in", [128 * k], F32)
    cout = mk.dinternal(f"cc{mk._cc_ctr}_out", [NCORES * 128 * k], F32,
                        shared_for="AllGather")
    mk._cc_ctr += 1
    cin_ap = cin.handle.ap().rearrange("(p j) -> p j", p=128)
    p.add("g", (lambda e, o=cin_ap, a=col_tap.ap: e.dma_start(out=o, in_=a)),
          reads=[col_tap], writes=[cin.t()], kind="d", tag=f"{tag}_st")
    p.coll("AllGather", ALU.bypass, cin.t(), cout.t(), mk.rg, tag=f"{tag}_ag")
    row = slab.alloc(NCORES * 128 * k, F32, f"gmr_{tag}")
    p.add("g", (lambda e, o=row.t(0, 1).ap, a=cout.handle.ap()[None, :]:
                e.dma_start(out=o, in_=a)),
          reads=[cout.t()], writes=[row.t(0, 1)], kind="d", tag=f"{tag}_ld")
    out = slab.alloc(k, F32, f"gmx_{tag}")
    r3 = row.t(0, 1).ap.rearrange("o (r p j) -> o j (r p)", r=NCORES, j=k)
    p.add("v", (lambda e, o=out.t(0, 1).ap, a=r3:
                e.tensor_reduce(out=o, in_=a, axis=AX, op=ALU.max)),
          reads=[row.t(0, 1)], writes=[out.t(0, 1)], tag=f"{tag}_red")
    row.free()
    return out


def scale_cols(mk, gmax_row, k, qmax, tag):
    p, slab = mk.p, mk.slab
    srow = slab.alloc(2 * k, F32, f"srow_{tag}")
    p.ts("v", srow.t(0, 1, 0, k), gmax_row.t(0, 1), 1.0 / qmax, ALU.mult, 1e-12, ALU.max,
         tag=f"{tag}_s")
    p.add("v", (lambda e, o=srow.t(0, 1, k, 2 * k).ap, a=srow.t(0, 1, 0, k).ap:
                e.reciprocal(out=o, in_=a)),
          reads=[srow.t(0, 1, 0, k)], writes=[srow.t(0, 1, k, 2 * k)], tag=f"{tag}_rs")
    ps = mk.psF[4].t(0, 128, 0, 2 * k)
    p.mmgroup([(mk.ones_row_f32(), srow.t(0, 1))], ps, tag=f"{tag}_bc")
    scol = slab.alloc(2 * k, F32, f"scol_{tag}")
    p.copy("v", scol.t(), ps, tag=f"{tag}_scev")
    srow.free()
    return scol


RK = 32768.0

def roundq(mk, out, in_, rs, s, tag, ibuf=None):
    p, slab = mk.p, mk.slab
    w = in_.box[3] - in_.box[2]
    own = ibuf is None
    if own:
        ibuf = slab.alloc((in_.box[3] - in_.box[2]) // 4 + 8, I32, f"ri_{tag}")
    it = ibuf.t(0, 128, 0, (in_.box[3] - in_.box[2]) // DT_SIZE[F32])
    p.ts("v", it, in_, rs, ALU.mult, RK + 0.5, ALU.add, tag=f"{tag}_ri")
    p.ts("v", out, it, -RK, ALU.add, s, ALU.mult, tag=f"{tag}_rq")
    if own:
        ibuf.free()


def sub_l1_prep_gen(mk, st, WD):
    p, slab = mk.p, mk.slab
    mk.dq = {}
    bufs = {
        "up": (WD["l1f_w_up"], 3072, 1024),
        "gate": (WD["l1f_w_gate"], 3072, 1024),
        "down": (WD["l1f_w_down"], 1024, 3072),
    }
    W = 512
    NBW = W // 32
    wtile = [slab.alloc(W, F32, f"dqw{i}") for i in range(2)]
    t1 = [slab.alloc(W, F32, "dqt0")] * 2
    fb = [slab.alloc(W, I32, "dqf0")] * 2
    ob = [slab.alloc(W, BF16, "dqo0")] * 2
    mxb = slab.alloc(2 * NBW, F32, "dqmx")
    sb = slab.alloc(2 * NBW, F32, "dqs")
    rb = slab.alloc(2 * NBW, F32, "dqr")
    mk._dq_transients = wtile + [t1[0], fb[0], ob[0], mxb, sb, rb]
    for nm, (dram, O, I) in bufs.items():
        dq = mk.dinternal(f"dq_{nm}", [O, I], BF16)
        mk.dq[nm] = dq
        src_ap = dram.handle.ap().rearrange("(c p) n -> p c n", p=128)
        dst_ap = dq.handle.ap().rearrange("(c p) n -> p c n", p=128)
        idx = 0
        for oc in range(O // 128):
            for isl in range(I // W):
                j = idx % 2
                idx += 1
                wt, tt1, tfb, tob = wtile[j], t1[j], fb[j], ob[j]
                mxs = mxb.t(0, 128, j * NBW, (j + 1) * NBW)
                sbs = sb.t(0, 128, j * NBW, (j + 1) * NBW)
                rbs = rb.t(0, 128, j * NBW, (j + 1) * NBW)
                i0 = isl * W
                p.add("y", (lambda e, o=wt.t().ap, a=src_ap[:, oc, i0:i0 + W]:
                            e.dma_start(out=o, in_=a)),
                      reads=[dram.t()], writes=[wt.t()], kind="d",
                      tag=f"dq_{nm}_ld{oc}_{isl}")
                w3 = wt.t().ap.rearrange("p (g b) -> p g b", b=32)
                p.add("v", (lambda e, o=mxs.ap, a=w3:
                            e.tensor_reduce(out=o, in_=a, axis=AX, op=ALU.max,
                                            apply_absolute_value=True)),
                      reads=[wt.t()], writes=[mxs], tag=f"dq_{nm}_mx{oc}_{isl}")
                p.ts("g", sbs, mxs, 1.0 / 7.0, ALU.mult, 1e-12, ALU.max,
                     tag=f"dq_{nm}_s{oc}_{isl}")
                p.add("v", (lambda e, o=rbs.ap, a=sbs.ap: e.reciprocal(out=o, in_=a)),
                      reads=[sbs], writes=[rbs], tag=f"dq_{nm}_r{oc}_{isl}")
                r3 = rbs.ap.unsqueeze(2).broadcast_to([128, NBW, 32])
                s3 = sbs.ap.unsqueeze(2).broadcast_to([128, NBW, 32])
                t13 = tt1.t().ap.rearrange("p (g b) -> p g b", b=32)
                p.add("g", (lambda e, o=t13, a=w3, b=r3:
                            e.tensor_tensor(out=o, in0=a, in1=b, op=ALU.mult)),
                      reads=[wt.t(), rbs], writes=[tt1.t()], tag=f"dq_{nm}_t{oc}_{isl}")
                p.ts("g", tt1.t(), tt1.t(), 32768.5, ALU.add, tag=f"dq_{nm}_z{oc}_{isl}")
                p.copy("v", tfb.t(), tt1.t(), tag=f"dq_{nm}_i{oc}_{isl}")
                p.tt("v", wt.t(), tfb.t(), tt1.t(), ALU.is_gt, tag=f"dq_{nm}_g{oc}_{isl}")
                p.tt("v", tt1.t(), tfb.t(), wt.t(), ALU.subtract, tag=f"dq_{nm}_s{oc}_{isl}")
                p.ts("g", tt1.t(), tt1.t(), -32768.0, ALU.add, tag=f"dq_{nm}_q{oc}_{isl}")
                ob3 = tob.t().ap.rearrange("p (g b) -> p g b", b=32)
                p.add("g", (lambda e, o=ob3, a=t13, b=s3:
                            e.tensor_tensor(out=o, in0=a, in1=b, op=ALU.mult)),
                      reads=[tt1.t(), sbs], writes=[tob.t()], tag=f"dq_{nm}_o{oc}_{isl}")
                p.add("g", (lambda e, o=dst_ap[:, oc, i0:i0 + W], a=tob.t().ap:
                            e.dma_start(out=o, in_=a)),
                      reads=[tob.t()], writes=[dq.box((0, 1, oc * 8 + isl, oc * 8 + isl + 1))],
                      kind="d", tag=f"dq_{nm}_st{oc}_{isl}")
                yield


def load_wT_slice(mk, dq, rows0, rows1, kchunk, tag):
    p, slab = mk.p, mk.slab
    w = rows1 - rows0
    b = slab.alloc(w, BF16, f"wt_{tag}")
    srcap = dq.handle.ap()[rows0:rows1, kchunk * 128:(kchunk + 1) * 128]
    p.add("y", (lambda e, o=b.t().ap, a=srcap: e.dma_start(out=o, in_=a, transpose=True)),
          reads=[dq.t()], writes=[b.t()], kind="d", tag=f"ldT_{tag}")
    return b


def sub_l1_ffn(mk, st, WD, first):
    p, slab = mk.p, mk.slab
    tag = "q1" if first else "q2"
    ln = ln_stats(mk, st, tag, want_nmr_bc=True)
    ln["xbf"].free()
    nmr_bc = ln["nmr_bc"]
    xn = slab.falloc(CH, NCOL, F32, f"xn_{tag}")
    mxc = slab.alloc(CH, F32, f"mxc_{tag}")
    for c in range(CH):
        p.stt("v", xn.t(c), st.x.t(c), 1.0, ln["R_bc"].t(), ALU.mult, ALU.mult,
              tag=f"{tag}_xr{c}")
        p.tt("g", xn.t(c), xn.t(c), nmr_bc.t(), ALU.add, tag=f"{tag}_xn{c}")
        p.red("v", mxc.t(0, 128, c, c + 1), xn.t(c), ALU.max, absval=True,
              tag=f"{tag}_mx{c}")
        mk.warm()
    spill_x(mk, st, tag)
    st.x.free()
    ln["R_bc"].free()
    nmr_bc.free()
    ln["nmr_row"].free()
    mxall = slab.alloc(1, F32, f"mxall_{tag}")
    p.red("v", mxall.t(), mxc.t(), ALU.max, tag=f"{tag}_mxall")
    gmx = gmax_exchange(mk, mxall.t(), 1, f"{tag}_b0")
    scol_x = scale_cols(mk, gmx, 1, 32767.0, f"{tag}_sx")
    gmx.free()
    mxall.free()
    mxc.free()
    qx = slab.falloc(CH, NCOL, BF16, f"qx_{tag}")
    for c in range(CH):
        qi = slab.alloc(NCOL, I32, "qi")
        p.ts("v", qi.t(), xn.t(c), scol_x.t(0, 128, 1, 2), ALU.mult, RK + 0.5, ALU.add,
             tag=f"{tag}_qa{c}")
        p.ts("v", qx.t(c), qi.t(), -RK, ALU.add, scol_x.t(0, 128, 0, 1), ALU.mult,
             tag=f"{tag}_qx{c}")
        mk.warm()
        qi.free()
        xn.bufs[c].free()
    xn.bufs = []
    scol_x.free()

    E = 3072
    gate = slab.falloc(24, NCOL, BF16, f"gate_{tag}")
    up_spill = mk.dinternal(f"up_spill_{tag}", [E, NCOL], BF16)
    ups_ap = up_spill.handle.ap().rearrange("(c p) n -> p c n", p=128)
    upmx = slab.alloc(96, F32, f"upmx_{tag}")
    gmxc = slab.alloc(96, F32, f"gmxc_{tag}")
    uptmp = [slab.alloc(NCOL, BF16, f"uptmp{i}") for i in range(2)]
    for nm, mxt in (("up", upmx), ("gate", gmxc)):
        dq = mk.dq[nm]
        for mq in range(4):
            r0, r1 = mq * 768, (mq + 1) * 768
            wts = [load_wT_slice(mk, dq, r0, r1, c, f"{tag}_{nm}{mq}_{c}") for c in range(CH)]
            for mloc in range(6):
                mj = mq * 6 + mloc
                dstt = gate.t if nm == "gate" else uptmp[mj % 2].t
                dst = (lambda c0, c1: gate.t(mj, c0, c1)) if nm == "gate" else                       (lambda c0, c1: uptmp[mj % 2].t(0, 128, c0, c1))
                for nt, (c0, c1) in enumerate(NT512):
                    pairs = [(wts[c].t(0, 128, mloc * 128, (mloc + 1) * 128),
                              qx.t(c, c0, c1)) for c in range(CH)]
                    ps = mk.psF[(mj * 4 + nt) % 5].t()
                    p.mmgroup(pairs, ps, tag=f"{tag}_{nm}mm{mj}_{nt}")
                    p.copy("s", dst(c0, c1), ps, tag=f"{tag}_{nm}ev{mj}_{nt}")
                    p.red("v", mxt.t(0, 128, mj * 4 + nt, mj * 4 + nt + 1),
                          dst(c0, c1), ALU.max, absval=True,
                          tag=f"{tag}_{nm}mx{mj}_{nt}")
                if nm == "up":
                    p.add("y", (lambda e, mj=mj, a=uptmp[mj % 2].t().ap:
                                e.dma_start(out=ups_ap[:, mj, :], in_=a)),
                          reads=[uptmp[mj % 2].t()],
                          writes=[up_spill.box((0, 1, mj, mj + 1))], kind="d",
                          tag=f"{tag}_upsp{mj}")
            for w in wts:
                w.free()
    for b in uptmp:
        b.free()
    qx.free()
    mx2 = slab.alloc(2, F32, f"mx2_{tag}")
    p.red("v", mx2.t(0, 128, 0, 1), upmx.t(), ALU.max, tag=f"{tag}_mxu")
    p.red("v", mx2.t(0, 128, 1, 2), gmxc.t(), ALU.max, tag=f"{tag}_mxg")
    upmx.free()
    gmxc.free()
    gmx1 = gmax_exchange(mk, mx2.t(), 2, f"{tag}_b1")
    scol1 = scale_cols(mk, gmx1, 2, 32767.0, f"{tag}_s1")
    gmx1.free()
    mx2.free()
    s_up = lambda: scol1.t(0, 128, 0, 1)
    s_g = lambda: scol1.t(0, 128, 1, 2)
    rs_up = lambda: scol1.t(0, 128, 2, 3)
    rs_g = lambda: scol1.t(0, 128, 3, 4)
    csg = slab.alloc(1, F32, f"csg_{tag}")
    p.ts("v", csg.t(), s_g(), 1.0 / 65536.0, ALU.mult, tag=f"{tag}_csg")
    m32sg = slab.alloc(1, F32, f"m32sg_{tag}")
    p.ts("v", m32sg.t(), s_g(), -RK, ALU.mult, tag=f"{tag}_m32sg")

    tmxc = slab.alloc(24, F32, f"tmx_{tag}")
    for mj0 in range(0, 24, 2):
        prs = []
        for mj in (mj0, mj0 + 1):
            bi1 = slab.alloc(NCOL, I32, "bi1")
            bi2 = slab.alloc(NCOL, I32, "bi2")
            bd1 = slab.alloc(NCOL, F32, "bd1")
            bd2 = slab.alloc(NCOL, F32, "bd2")
            prs.append((mj, bi1, bi2, bd1, bd2))
        for mj, bi1, bi2, bd1, bd2 in prs:
            p.ts("v", bi1.t(), gate.t(mj), rs_g(), ALU.mult, RK + 0.5, ALU.add,
                 tag=f"{tag}_Ba{mj}")
        for mj, bi1, bi2, bd1, bd2 in prs:
            p.act(bd2.t(), bi1.t(), AF.Sigmoid, scale=s_g(), bias=m32sg.t(),
                  tag=f"{tag}_Bsig{mj}")
        for mj, bi1, bi2, bd1, bd2 in prs:
            p.act(bi2.t(), bd2.t(), AF.Copy, scale=65536.0, bias=RK + 0.5,
                  tag=f"{tag}_Bq{mj}")
        for mj, bi1, bi2, bd1, bd2 in prs:
            p.act(bd1.t(), bi1.t(), AF.Copy, bias=-RK, tag=f"{tag}_Bd1{mj}")
        for mj, bi1, bi2, bd1, bd2 in prs:
            p.act(bd2.t(), bi2.t(), AF.Copy, bias=-RK, tag=f"{tag}_Bd2{mj}")
        for mj, bi1, bi2, bd1, bd2 in prs:
            p.tt("v", bd1.t(), bd1.t(), bd2.t(), ALU.mult, tag=f"{tag}_Bt{mj}")
        for mj, bi1, bi2, bd1, bd2 in prs:
            p.red("v", tmxc.t(0, 128, mj, mj + 1), bd1.t(), ALU.max, absval=True,
                  tag=f"{tag}_Bmx{mj}")
        for mj, bi1, bi2, bd1, bd2 in prs:
            p.ts("v", gate.t(mj), bd1.t(), csg.t(), ALU.mult, tag=f"{tag}_Btb{mj}")
        mk.warm()
        mk.warm()
        for mj, bi1, bi2, bd1, bd2 in prs:
            for b in (bi1, bi2, bd1, bd2):
                b.free()
    tmax = slab.alloc(1, F32, f"tmax_{tag}")
    p.red("v", tmax.t(), tmxc.t(), ALU.max, tag=f"{tag}_Btmax")
    p.tt("v", tmax.t(), tmax.t(), csg.t(0, 128, 0, 1), ALU.mult, tag=f"{tag}_Btsc")
    gmx2 = gmax_exchange(mk, tmax.t(), 1, f"{tag}_b2")
    scol2 = scale_cols(mk, gmx2, 1, 32767.0, f"{tag}_s2")
    gmx2.free()
    tmax.free()
    tmxc.free()
    s_t = lambda: scol2.t(0, 128, 0, 1)
    rs_t = lambda: scol2.t(0, 128, 1, 2)
    cstu = slab.alloc(1, F32, f"cstu_{tag}")
    p.tt("v", cstu.t(), s_t(), s_up(), ALU.mult, tag=f"{tag}_cstu")

    umxc = slab.alloc(24, F32, f"umx_{tag}")
    upld = [slab.alloc(NCOL, BF16, f"upld{i}") for i in range(4)]
    for mj0 in range(0, 24, 2):
        prs = []
        for mj in (mj0, mj0 + 1):
            upb = upld[mj % 4]
            p.add("y", (lambda e, mj=mj, o=upb.t().ap: e.dma_start(out=o, in_=ups_ap[:, mj, :])),
                  reads=[up_spill.box((0, 1, mj, mj + 1))], writes=[upb.t()], kind="d",
                  tag=f"{tag}_upld{mj}")
            bi1 = slab.alloc(NCOL, I32, "ci1")
            bi2 = slab.alloc(NCOL, I32, "ci2")
            bd1 = slab.alloc(NCOL, F32, "cd1")
            bd2 = slab.alloc(NCOL, F32, "cd2")
            prs.append((mj, upb, bi1, bi2, bd1, bd2))
        for mj, upb, bi1, bi2, bd1, bd2 in prs:
            p.ts("v", bi1.t(), gate.t(mj), rs_t(), ALU.mult, RK + 0.5, ALU.add,
                 tag=f"{tag}_Ca{mj}")
        for mj, upb, bi1, bi2, bd1, bd2 in prs:
            p.ts("v", bi2.t(), upb.t(), rs_up(), ALU.mult, RK + 0.5, ALU.add,
                 tag=f"{tag}_Cb{mj}")
        for mj, upb, bi1, bi2, bd1, bd2 in prs:
            p.act(bd1.t(), bi1.t(), AF.Copy, bias=-RK, tag=f"{tag}_Cd1{mj}")
        for mj, upb, bi1, bi2, bd1, bd2 in prs:
            p.act(bd2.t(), bi2.t(), AF.Copy, bias=-RK, tag=f"{tag}_Cd2{mj}")
        for mj, upb, bi1, bi2, bd1, bd2 in prs:
            p.tt("v", bd1.t(), bd1.t(), bd2.t(), ALU.mult, tag=f"{tag}_Cu{mj}")
        for mj, upb, bi1, bi2, bd1, bd2 in prs:
            p.red("v", umxc.t(0, 128, mj, mj + 1), bd1.t(), ALU.max, absval=True,
                  tag=f"{tag}_Cmx{mj}")
        for mj, upb, bi1, bi2, bd1, bd2 in prs:
            p.ts("v", gate.t(mj), bd1.t(), cstu.t(), ALU.mult, tag=f"{tag}_Cub{mj}")
        mk.warm()
        mk.warm()
        for mj, upb, bi1, bi2, bd1, bd2 in prs:
            for b in (bi1, bi2, bd1, bd2):
                b.free()
    for b in upld:
        b.free()
    umax = slab.alloc(1, F32, f"umax_{tag}")
    p.red("v", umax.t(), umxc.t(), ALU.max, tag=f"{tag}_Cumax")
    p.tt("v", umax.t(), umax.t(), cstu.t(0, 128, 0, 1), ALU.mult, tag=f"{tag}_Cusc")
    gmx3 = gmax_exchange(mk, umax.t(), 1, f"{tag}_b3")
    scol3 = scale_cols(mk, gmx3, 1, 32767.0, f"{tag}_s3")
    gmx3.free()
    umax.free()
    umxc.free()
    s_u = lambda: scol3.t(0, 128, 0, 1)
    rs_u = lambda: scol3.t(0, 128, 1, 2)

    for mj in range(24):
        bi1 = slab.alloc(NCOL, I32, "di1")
        p.ts("v", bi1.t(), gate.t(mj), rs_u(), ALU.mult, RK + 0.5, ALU.add,
             tag=f"{tag}_Da{mj}")
        p.ts("v", gate.t(mj), bi1.t(), -RK, ALU.add, s_u(), ALU.mult,
             tag=f"{tag}_Dq{mj}")
        mk.warm()
        bi1.free()
    m32sg.free()

    xnew = slab.falloc(CH, NCOL, F32, f"xnew_{tag}")
    dqd = mk.dq["down"]
    for half in range(4):
        o0 = half * 256
        wts = [load_wT_slice(mk, dqd, o0, o0 + 256, c, f"{tag}_dn{half}_{c}")
               for c in range(24)]
        for mloc in range(2):
            mj = half * 2 + mloc
            xold = slab.alloc(NCOL, F32, f"xoldq{mj}")
            restream_x_chunk(mk, st, mj, xold, tag)
            for nt, (c0, c1) in enumerate(NT512):
                pairs = [(wts[c].t(0, 128, mloc * 128, (mloc + 1) * 128),
                          gate.t(c, c0, c1)) for c in range(24)]
                ps = mk.psF[(mj * 4 + nt) % 5].t()
                p.mmgroup(pairs, ps, tag=f"{tag}_dn{mj}_{nt}")
                p.stt("v", xnew.t(mj, c0, c1), ps, 0.5, xold.t(0, 128, c0, c1),
                      ALU.mult, ALU.add, tag=f"{tag}_dres{mj}_{nt}")
            xold.free()
        for w in wts:
            w.free()
    gate.free()
    for b in (scol1, scol2, scol3, csg, cstu):
        b.free()
    st.x = xnew


class _TruncView:
    def __init__(self, fb, c):
        self.fb = fb
        self.c = c

    def t(self):
        return self.fb.t(self.c)


def bias_row_bf(mk, wcb, E, lnb_bf, bvec_dram, tag):
    p, slab = mk.p, mk.slab
    row = slab.alloc(E, F32, f"br_{tag}")
    for s in range(E // 512):
        ps = mk.psF[2].t(0, 1, 0, 512)
        p.mmgroup([(lnb_bf.t(0, 128, c, c + 1), wcb.t(c, s * 512, (s + 1) * 512))
                   for c in range(CH)], ps, tag=f"{tag}_br{s}")
        p.copy("v", row.t(0, 1, s * 512, (s + 1) * 512), ps, tag=f"{tag}_brev{s}")
    bro = slab.alloc(E, F32, f"brb_{tag}")
    p.add("y", (lambda e, o=bro.t(0, 1).ap, a=bvec_dram.handle.ap()[None, :]:
                e.dma_start(out=o, in_=a)),
          reads=[bvec_dram.t()], writes=[bro.t(0, 1)], kind="d", tag=f"{tag}_brld")
    out = slab.alloc(E, BF16, f"brf_{tag}")
    p.tt("v", out.t(0, 1), row.t(0, 1), bro.t(0, 1), ALU.add, tag=f"{tag}_bradd")
    row.free()
    bro.free()
    return out


def proj_tm(mk, src_bf, wcb, bias_row, nmr_row, cs_row, evac, tag,
            tchunks=16, srcsl=None):
    p = mk.p
    for tc in range(tchunks):
        t0 = tc * 128
        for nt in range(2):
            e0, e1 = nt * 512, (nt + 1) * 512
            pairs = [(src_bf.t(c, t0, t0 + 128), wcb.t(c, e0, e1)) for c in range(CH)]
            pairs.append((mk.ones_row_bf(0, 128), bias_row.t(0, 1, e0, e1)))
            if cs_row is not None:
                pairs.append((nmr_row.t(0, 1, t0, t0 + 128), cs_row.t(0, 1, e0, e1)))
            ps = mk.psF[(tc * 2 + nt) % 5].t()
            p.mmgroup(pairs, ps, tag=f"{tag}_tm{tc}_{nt}")
            evac(tc, e0, e1, ps)


def phi_evac_fm(mk, dst, tag):
    p, slab = mk.p, mk.slab

    def ev(mj, c0, c1, ps, bias):
        tmn = slab.alloc(512, F32, "phn")
        tmx = slab.alloc(512, F32, "phx")
        p.ts("v", tmn.t(), ps, bias, ALU.add, 0.0, ALU.min, tag=f"{tag}_mn{mj}")
        p.ts("v", tmx.t(), ps, bias, ALU.add, 0.0, ALU.max, tag=f"{tag}_mx{mj}")
        p.act(tmn.t(), tmn.t(), AF.Exp, tag=f"{tag}_ex{mj}")
        p.tt("v", dst.t(mj, c0, c1), tmn.t(), tmx.t(), ALU.add, tag=f"{tag}_ph{mj}")
        tmn.free()
        tmx.free()
    return ev


def phi_evac_tm(mk, dst, tag):
    p, slab = mk.p, mk.slab

    def ev(tc, e0, e1, ps):
        tmn = slab.alloc(512, F32, "phn")
        tmx = slab.alloc(512, F32, "phx")
        p.ts("v", tmn.t(), ps, 0.0, ALU.min, tag=f"{tag}_mn{tc}")
        p.ts("v", tmx.t(), ps, 0.0, ALU.max, tag=f"{tag}_mx{tc}")
        p.act(tmn.t(), tmn.t(), AF.Exp, tag=f"{tag}_ex{tc}")
        p.tt("v", dst.t(tc, e0, e1), tmn.t(), tmx.t(), ALU.add, tag=f"{tag}_ph{tc}")
        tmn.free()
        tmx.free()
    return ev


def sub_l1_attn(mk, st, WD):
    p, slab = mk.p, mk.slab
    tag = "w5"
    ln = ln_stats(mk, st, tag, want_nmr_bc=True)
    ln["xbf"].free()
    lng_col = mk.load_col(WD["l1a_ln_g"], tag=f"{tag}_lng")
    lnb_col = mk.load_col(WD["l1a_ln_b"], tag=f"{tag}_lnb")
    lng_bf = slab.alloc(CH, BF16, f"{tag}_lngbf")
    lnb_bf = slab.alloc(CH, BF16, f"{tag}_lnbbf")
    p.copy("v", lng_bf.t(), lng_col.t(), tag=f"{tag}_lngc")
    p.copy("v", lnb_bf.t(), lnb_col.t(), tag=f"{tag}_lnbc")
    xg = slab.falloc(CH, NCOL, BF16, f"xg_{tag}")
    for c in range(CH):
        p.stt("v", xg.t(c), st.x.t(c), lng_col.t(0, 128, c, c + 1), ln["R_bc"].t(),
              ALU.mult, ALU.mult, tag=f"{tag}_xg{c}")
    mk.tap("w5_xg", xg, CH, NCOL, BF16)
    xln_spill = mk.dinternal("xln_spill", [D, NCOL], F32)
    xlsp = xln_spill.handle.ap().rearrange("(c p) n -> p c n", p=128)
    xlt = slab.alloc(NCOL, F32, f"xlt_{tag}")
    for c in range(CH):
        p.stt("v", xlt.t(), st.x.t(c), 1.0, ln["R_bc"].t(), ALU.mult, ALU.mult,
              tag=f"{tag}_xl{c}")
        p.tt("g", xlt.t(), xlt.t(), ln["nmr_bc"].t(), ALU.add, tag=f"{tag}_xlb{c}")
        p.add("y", (lambda e, c=c, a=xlt.t().ap: e.dma_start(out=xlsp[:, c, :], in_=a)),
              reads=[xlt.t()], writes=[xln_spill.box((0, 1, c, c + 1))], kind="d",
              tag=f"{tag}_xlsp{c}")
    xlt.free()
    ln["nmr_bc"].free()
    spill_x(mk, st, tag)
    st.x.free()
    ln["R_bc"].free()
    nmr = ln["nmr_row"]

    HW = CH * 128
    hstage = slab.alloc(HW, BF16, f"hst_{tag}")
    for c in range(CH):
        for b2 in range(NB):
            for j in range(2):
                csrc = b2 * SL + (0 if j == 0 else SL - HALO)
                p.copy("v", hstage.t(0, 128, c * 128 + b2 * 64 + j * 32,
                                     c * 128 + b2 * 64 + j * 32 + 32),
                       xg.t(c, csrc, csrc + HALO), tag=f"{tag}_hs{c}{b2}{j}")
    nstage = slab.alloc(128, BF16, f"nst_{tag}")
    for b2 in range(NB):
        for j in range(2):
            csrc = b2 * SL + (0 if j == 0 else SL - HALO)
            p.copy("v", nstage.t(0, 1, b2 * 64 + j * 32, b2 * 64 + j * 32 + 32),
                   nmr.t(0, 1, csrc, csrc + HALO), tag=f"{tag}_hsn{b2}{j}")
    TOTW = 128 * HW + 128
    hcin = mk.dinternal(f"cc{mk._cc_ctr}_in", [TOTW], BF16)
    hcout = mk.dinternal(f"cc{mk._cc_ctr}_out", [NCORES * TOTW], BF16,
                         shared_for="AllGather")
    mk._cc_ctr += 1
    p.add("g", (lambda e, o=hcin.handle.ap()[0:128 * HW].rearrange("(p n) -> p n", p=128),
                a=hstage.t().ap: e.dma_start(out=o, in_=a)),
          reads=[hstage.t()], writes=[hcin.box((0, 1, 0, 1))], kind="d", tag=f"{tag}_hciw")
    p.add("g", (lambda e, o=hcin.handle.ap()[None, 128 * HW:TOTW],
                a=nstage.t(0, 1).ap: e.dma_start(out=o, in_=a)),
          reads=[nstage.t(0, 1)], writes=[hcin.box((0, 1, 1, 2))], kind="d",
          tag=f"{tag}_hcin")
    p.coll("AllGather", ALU.bypass, hcin.t(), hcout.t(), mk.rg, tag=f"{tag}_hag")
    hstage.free()
    nstage.free()
    hh = slab.alloc(CH * 128, BF16, f"hh_{tag}")
    nmrh = slab.alloc(128, BF16, f"nmrh_{tag}")
    hco = hcout.handle.ap().rearrange("(r q) -> r q", r=NCORES)

    def _halo_dma(e, side):
        with e.register(f"rk{side}") as rk:
            e.reg_load(rk, mk.rank_tap(side).ap)
            ro = e.snap(rk)
            src = hco[bass.ds(ro, 1), 0:128 * HW].rearrange(
                "r (p c b2 j i) -> r p c b2 j i", p=128, c=CH, b2=NB, j=2)[0, :, :, :, 1 - side, :]
            dst = hh.t().ap.rearrange("p (c b2 j i) -> p c b2 j i", c=CH, b2=NB, j=2)[
                :, :, :, side, :]
            return e.dma_start(out=dst, in_=src)

    for side in range(2):
        sd = side
        p.add("g", (lambda e, sd=sd: _halo_dma(e, sd)),
              reads=[hcout.t(), mk.rank_tap(sd)], writes=[hh.t()], kind="d",
              tag=f"{tag}_hld{sd}")

    def _halo_nmr_dma(e, side):
        with e.register(f"rn{side}") as rk:
            e.reg_load(rk, mk.rank_tap(side).ap)
            ro = e.snap(rk)
            src = hco[bass.ds(ro, 1), 128 * HW:TOTW].rearrange(
                "r (b2 j i) -> r b2 j i", b2=NB, j=2)[:, :, 1 - side, :]
            dst = nmrh.t(0, 1).ap.rearrange("p (b2 j i) -> p b2 j i", b2=NB, j=2)[
                :, :, side, :]
            return e.dma_start(out=dst, in_=src)

    for side in range(2):
        sd = side
        p.add("g", (lambda e, sd=sd: _halo_nmr_dma(e, sd)),
              reads=[hcout.t(), mk.rank_tap(sd)], writes=[nmrh.t(0, 1)], kind="d",
              tag=f"{tag}_hldn{sd}")


    if W5_TRUNC <= 1:
        xnew = slab.falloc(CH, NCOL, F32, "xtr")
        for c in range(CH):
            restream_x_chunk(mk, st, c, _TruncView(xnew, c), "tr")
        st.x = xnew
        return
    w_klin = mk.load_weight_bf16(WD["l1a_wk_lin"], D, tag=f"{tag}_wkl")
    w_vlin = mk.load_weight_bf16(WD["l1a_wv_lin"], D, tag=f"{tag}_wvl")
    cs_kl = proj_cs_row(mk, w_klin, D, gcol_bf=lng_bf, tag=f"{tag}_cskl")
    br_kl = bias_row_bf(mk, w_klin, D, lnb_bf, WD["l1a_bk_lin"], f"{tag}_brkl")
    phk = slab.falloc(16, D, BF16, f"phk_{tag}")
    proj_tm(mk, xg, w_klin, br_kl, nmr, cs_kl, phi_evac_tm(mk, phk, f"{tag}_phk"),
            f"{tag}_kl")
    w_klin.free()
    cs_kl.free()
    br_kl.free()
    cs_vl = proj_cs_row(mk, w_vlin, D, gcol_bf=lng_bf, tag=f"{tag}_csvl")
    br_vl = bias_row_bf(mk, w_vlin, D, lnb_bf, WD["l1a_bv_lin"], f"{tag}_brvl")
    vtm = slab.falloc(16, D, BF16, f"vtm_{tag}")

    def ev_v(tc, e0, e1, ps):
        p.copy("s", vtm.t(tc, e0, e1), ps, tag=f"{tag}_evv{tc}_{e0}")
    proj_tm(mk, xg, w_vlin, br_vl, nmr, cs_vl, ev_v, f"{tag}_vl")
    w_vlin.free()
    cs_vl.free()
    br_vl.free()
    kv_in = mk.dinternal(f"cc{mk._cc_ctr}_in", [NB * D * D + NB * D], F32)
    kv_out = mk.dinternal(f"cc{mk._cc_ctr}_out", [NB * D * D + NB * D], F32,
                          shared_for="AllReduce")
    mk._cc_ctr += 1
    kvr = kv_in.handle.ap()[0:NB * D * D].rearrange("(b dc eh p n) -> b dc eh p n",
                                                    b=NB, dc=CH, eh=2, p=128)
    kstage = slab.alloc(512, F32, f"kvst_{tag}")
    for b2 in range(NB):
        for dc in range(CH):
            for eh in range(2):
                ps = mk.psF[(dc * 2 + eh) % 4].t()
                p.mmgroup([(phk.t(b2 * 8 + tc, dc * 128, (dc + 1) * 128),
                            vtm.t(b2 * 8 + tc, eh * 512, (eh + 1) * 512))
                           for tc in range(8)], ps, tag=f"{tag}_kv{b2}{dc}{eh}")
                p.copy("v", kstage.t(), ps, tag=f"{tag}_kvev{b2}{dc}{eh}")
                p.add("y", (lambda e, o=kvr[b2, dc, eh], a=kstage.t().ap:
                            e.dma_start(out=o, in_=a)),
                      reads=[kstage.t()], writes=[kv_in.box((0, 1, b2 * 16 + dc * 2 + eh,
                                                             b2 * 16 + dc * 2 + eh + 1))],
                      kind="d", tag=f"{tag}_kvw{b2}{dc}{eh}")
    ksr = kv_in.handle.ap()[NB * D * D:].rearrange("(p j) -> p j", p=128)
    kss = slab.alloc(16, F32, f"kss_{tag}")
    for b2 in range(NB):
        for dc in range(CH):
            ps = mk.psF[4].t(0, 128, 0, 1)
            p.mmgroup([(phk.t(b2 * 8 + tc, dc * 128, (dc + 1) * 128),
                        mk.ones_col()) for tc in range(8)], ps,
                      tag=f"{tag}_ks{b2}{dc}")
            p.copy("v", kss.t(0, 128, b2 * 8 + dc, b2 * 8 + dc + 1), ps,
                   tag=f"{tag}_ksev{b2}{dc}")
    p.add("y", (lambda e, o=ksr, a=kss.t().ap: e.dma_start(out=o, in_=a)),
          reads=[kss.t()], writes=[kv_in.box((0, 1, 100, 101))], kind="d",
          tag=f"{tag}_ksw")
    kss.free()
    kstage.free()
    phk.free()
    vtm.free()
    p.coll("AllReduce", ALU.add, kv_in.t(), kv_out.t(), mk.rg, tag=f"{tag}_kvar")


    if W5_TRUNC <= 2:
        xnew = slab.falloc(CH, NCOL, F32, "xtr")
        for c in range(CH):
            restream_x_chunk(mk, st, c, _TruncView(xnew, c), "tr")
        st.x = xnew
        return
    w_qloc = mk.load_weight_bf16(WD["l1a_wq_loc"], D, tag=f"{tag}_wql")
    cs_ql = proj_cs_row(mk, w_qloc, D, gcol_bf=lng_bf, tag=f"{tag}_csql")
    br_ql = bias_row_bf(mk, w_qloc, D, lnb_bf, WD["l1a_bq_loc"], f"{tag}_brql")
    qloc = slab.falloc(CH, NCOL, BF16, f"qloc_{tag}")

    def ev_ql(mj, c0, c1, ps, bias):
        p.copy("s", qloc.t(mj, c0, c1), ps, tag=f"{tag}_evql{mj}")
    proj_fm(mk, xg, w_qloc, D, None, ev_ql, f"{tag}_ql", cs_row=cs_ql, nmr=nmr,
            bias_row=br_ql)
    w_qloc.free()
    cs_ql.free()
    br_ql.free()

    w_kloc = mk.load_weight_bf16(WD["l1a_wk_loc"], D, tag=f"{tag}_wkc")
    cs_kc = proj_cs_row(mk, w_kloc, D, gcol_bf=lng_bf, tag=f"{tag}_cskc")
    br_kc = bias_row_bf(mk, w_kloc, D, lnb_bf, WD["l1a_bk_loc"], f"{tag}_brkc")
    kloc = slab.falloc(CH, NCOL, BF16, f"kloc_{tag}")

    def ev_kc(mj, c0, c1, ps, bias):
        p.copy("s", kloc.t(mj, c0, c1), ps, tag=f"{tag}_evkc{mj}")
    proj_fm(mk, xg, w_kloc, D, None, ev_kc, f"{tag}_kc", cs_row=cs_kc, nmr=nmr,
            bias_row=br_kc)
    khalo = slab.falloc(CH, 128, BF16, f"khalo_{tag}")
    ones128r = None
    for mj in range(CH):
        pairs = [(w_kloc.t(c, mj * 128, (mj + 1) * 128),
                  TAP(hh.t().ap.rearrange("p (c n) -> p c n", c=CH)[:, c, :],
                      hh.slab.name, hh.t().box)) for c in range(CH)]
        pairs.append((cs_kc.t(0, 1, mj * 128, (mj + 1) * 128), nmrh.t(0, 1)))
        pairs.append((br_kc.t(0, 1, mj * 128, (mj + 1) * 128),
                      TAP(mk.ones512().ap[:, 0:128], "ones512", (0, 1, 0, 512))))
        ps = mk.psF[mj % 4].t(0, 128, 0, 128)
        p.mmgroup(pairs, ps, tag=f"{tag}_kh{mj}")
        p.copy("s", khalo.t(mj), ps, tag=f"{tag}_evkh{mj}")
    w_kloc.free()
    cs_kc.free()
    br_kc.free()

    w_vloc = mk.load_weight_bf16(WD["l1a_wv_loc"], D, tag=f"{tag}_wvc")
    cs_vc = proj_cs_row(mk, w_vloc, D, gcol_bf=lng_bf, tag=f"{tag}_csvc")
    br_vc = bias_row_bf(mk, w_vloc, D, lnb_bf, WD["l1a_bv_loc"], f"{tag}_brvc")
    vext = slab.falloc(NB * 9, D, BF16, f"vext_{tag}")

    def ev_vx(tc, e0, e1, ps):
        b2, tl = tc // 8, tc % 8
        base = b2 * 9 + tl
        for rr in range(3):
            p.copy("s", vext.t(base, e0, e1, p0=32 + rr * 32, p1=64 + rr * 32),
                   TAP(ps.ap[rr * 32:(rr + 1) * 32, :], ps.name,
                       (rr * 32, (rr + 1) * 32, ps.box[2], ps.box[3])),
                   tag=f"{tag}_evvx{tc}a{rr}")
        p.copy("s", vext.t(base + 1, e0, e1, p0=0, p1=32),
               TAP(ps.ap[96:128, :], ps.name, (96, 128, ps.box[2], ps.box[3])),
               tag=f"{tag}_evvx{tc}b")
    proj_tm(mk, xg, w_vloc, br_vc, nmr, cs_vc, ev_vx, f"{tag}_vx")
    for nt in range(2):
        e0, e1 = nt * 512, (nt + 1) * 512
        pairs = [(TAP(hh.t().ap.rearrange("p (c n) -> p c n", c=CH)[:, c, :],
                      hh.slab.name, hh.t().box), w_vloc.t(c, e0, e1)) for c in range(CH)]
        pairs.append((mk.ones_row_bf(0, 128), br_vc.t(0, 1, e0, e1)))
        pairs.append((nmrh.t(0, 1), cs_vc.t(0, 1, e0, e1)))
        ps = mk.psF[4].t()
        p.mmgroup(pairs, ps, tag=f"{tag}_vh{nt}")
        for b2 in range(NB):
            for j in range(2):
                r0 = b2 * 64 + j * 32
                if j == 0:
                    dst = vext.t(b2 * 9 + 0, e0, e1, p0=0, p1=32)
                else:
                    dst = vext.t(b2 * 9 + 8, e0, e1, p0=32, p1=64)
                p.copy("s", dst, TAP(ps.ap[r0:r0 + 32, :], ps.name,
                                     (ps.box[0] + r0, ps.box[0] + r0 + 32, ps.box[2], ps.box[3])),
                       tag=f"{tag}_evvh{nt}{b2}{j}")
    w_vloc.free()
    cs_vc.free()
    br_vc.free()
    xg_spill = mk.dinternal("xg_spill", [D, NCOL], BF16)
    xgsp = xg_spill.handle.ap().rearrange("(c p) n -> p c n", p=128)
    for c in range(CH):
        p.add("y", (lambda e, c=c, a=xg.t(c).ap: e.dma_start(out=xgsp[:, c, :], in_=a)),
              reads=[xg.t(c)], writes=[xg_spill.box((0, 1, c, c + 1))], kind="d",
              tag=f"{tag}_xgsp{c}")
    xg.free()


    if W5_TRUNC <= 3:
        xnew = slab.falloc(CH, NCOL, F32, "xtr")
        for c in range(CH):
            restream_x_chunk(mk, st, c, _TruncView(xnew, c), "tr")
        st.x = xnew
        return
    btol = mk.din("btol", [128, 16 * 192])
    b8 = {}
    btbuf = slab.alloc(16 * 192, F32, f"btol_{tag}")
    p.add("y", (lambda e, o=btbuf.t().ap, a=btol.handle.ap()[:, :]: e.dma_start(out=o, in_=a)),
          reads=[btol.t()], writes=[btbuf.t()], kind="d", tag=f"{tag}_btld")
    for v in ("I", "F", "L"):
        mdram = mk.din(f"wmask{v}", [128, 192])
        mbuf = slab.alloc(192, F32, f"m{v}_{tag}")
        p.add("y", (lambda e, o=mbuf.t().ap, a=mdram.handle.ap()[:, :]: e.dma_start(out=o, in_=a)),
              reads=[mdram.t()], writes=[mbuf.t()], kind="d", tag=f"{tag}_m{v}ld")
        bb = slab.alloc(16 * 192, BF16, f"b8{v}_{tag}")
        tmpb = slab.alloc(4 * 192, F32, f"b8t_{tag}")
        m3 = mbuf.t().ap.unsqueeze(1).broadcast_to([128, 4, 192])
        for hq in range(4):
            s0, s1 = hq * 4 * 192, (hq + 1) * 4 * 192
            p.add("v", (lambda e, o=tmpb.t().ap.rearrange("p (h j) -> p h j", h=4),
                        a=btbuf.t(0, 128, s0, s1).ap.rearrange("p (h j) -> p h j", h=4),
                        b=m3: e.tensor_tensor(out=o, in0=a, in1=b, op=ALU.add)),
                  reads=[btbuf.t(0, 128, s0, s1), mbuf.t()], writes=[tmpb.t()],
                  tag=f"{tag}_b8a{v}{hq}")
            p.ts("v", bb.t(0, 128, s0, s1), tmpb.t(), 8.0, ALU.mult, tag=f"{tag}_b8m{v}{hq}")
        tmpb.free()
        mbuf.free()
        b8[v] = bb
    btbuf.free()

    win = slab.falloc(16, D, BF16, f"win_{tag}")
    pbf = slab.alloc(4 * 192, BF16, f"pbf_{tag}")
    nrm4 = slab.alloc(8, F32, f"nrm4_{tag}")
    rsum = slab.alloc(4, F32, f"rsum_{tag}")
    rrec = slab.alloc(4, F32, f"rrec_{tag}")
    atsb = slab.alloc(1024, BF16, f"atsb_{tag}")
    _bqctr = [0]
    for b2 in range(NB):
        for q in range(8):
            _bqctr[0] += 1
            if W5_NBQ is not None and _bqctr[0] > W5_NBQ:
                continue
            t0own = b2 * SL + q * 128
            bsel = b8["F"] if q == 0 else (b8["L"] if q == 7 else b8["I"])
            segs = []
            ke0 = q * 128
            if q == 0:
                segs.append((0, 32, lambda h, b2=b2: khalo.t(h // 2, b2 * 64, b2 * 64 + 32,
                                                             p0=(h % 2) * 64, p1=(h % 2) * 64 + 64)))
                segs.append((32, 192, lambda h, b2=b2: kloc.t(h // 2, b2 * SL, b2 * SL + 160,
                                                              p0=(h % 2) * 64, p1=(h % 2) * 64 + 64)))
            elif q == 7:
                segs.append((0, 160, lambda h, b2=b2: kloc.t(h // 2, b2 * SL + 864, b2 * SL + SL,
                                                             p0=(h % 2) * 64, p1=(h % 2) * 64 + 64)))
                segs.append((160, 192, lambda h, b2=b2: khalo.t(h // 2, b2 * 64 + 32, b2 * 64 + 64,
                                                                p0=(h % 2) * 64, p1=(h % 2) * 64 + 64)))
            else:
                o0 = b2 * SL + ke0 - 32
                segs.append((0, 192, lambda h, o0=o0: kloc.t(h // 2, o0, o0 + 192,
                                                             p0=(h % 2) * 64, p1=(h % 2) * 64 + 64)))
            for hg in range(4):
                psS = [mk.psF[2 * (hg % 2)], mk.psF[2 * (hg % 2) + 1]]
                psT = mk.psB[hg % 2]
                psO = mk.psF[4 + hg % 2]
                for hh_ in range(4):
                    h = hg * 4 + hh_
                    bank = psS[hh_ // 2]
                    colo = (hh_ % 2) * 192
                    pst = bank.t(0, 128, colo, colo + 192)
                    qtap = qloc.t(h // 2, t0own, t0own + 128,
                                  p0=(h % 2) * 64, p1=(h % 2) * 64 + 64)
                    def em_sc(e, pst=pst.ap, qtap=qtap.ap, segs=segs, h=h, bsel=bsel, colo=colo):
                        inst = None
                        first = True
                        for (s0, s1, kf) in segs:
                            inst = e.matmul(pst[:, s0:s1], qtap, kf(h).ap,
                                            start=first, stop=False)
                            first = False
                        inst = e.matmul(pst, mk.ident_tap.ap,
                                        bsel.t(0, 128, h * 192, (h + 1) * 192).ap,
                                        start=False, stop=True)
                        return inst
                    rd = [qloc.t(h // 2, t0own, t0own + 128), kloc.t(h // 2, 0, NCOL),
                          khalo.t(h // 2), bsel.t(0, 128, h * 192, (h + 1) * 192),
                          mk.ident_tap]
                    p.add("t", em_sc, reads=rd, writes=[pst], tag=f"{tag}_sc{b2}{q}{h}")
                for bi in range(2):
                    bank = psS[bi]
                    p.add("v", (lambda e, o=nrm4.t(0, 128, hg % 2 * 4 + bi * 2,
                                                   hg % 2 * 4 + bi * 2 + 2).ap,
                                a=bank.t(0, 128, 0, 384).ap.rearrange("p (h j) -> p h j", h=2):
                                e.tensor_reduce(out=o, in_=a, axis=AX, op=ALU.max, negate=True)),
                          reads=[bank.t(0, 128, 0, 384)],
                          writes=[nrm4.t(0, 128, hg % 2 * 4 + bi * 2, hg % 2 * 4 + bi * 2 + 2)],
                          tag=f"{tag}_nrm{b2}{q}{hg}{bi}")
                p.ts("v", nrm4.t(0, 128, hg % 2 * 4, hg % 2 * 4 + 4),
                     nrm4.t(0, 128, hg % 2 * 4, hg % 2 * 4 + 4), 0.125, ALU.mult,
                     tag=f"{tag}_nsc{b2}{q}{hg}")
                for hh_ in range(4):
                    h = hg * 4 + hh_
                    bank = psS[hh_ // 2]
                    colo = (hh_ % 2) * 192
                    p.act(pbf.t(0, 128, hh_ * 192, (hh_ + 1) * 192),
                          bank.t(0, 128, colo, colo + 192), AF.Exp,
                          bias=nrm4.t(0, 128, hg % 2 * 4 + hh_, hg % 2 * 4 + hh_ + 1),
                          scale=0.125,
                          accum=rsum.t(0, 128, hh_, hh_ + 1), tag=f"{tag}_exp{b2}{q}{h}")
                p.add("v", (lambda e, o=rrec.t().ap, a=rsum.t().ap: e.reciprocal(out=o, in_=a)),
                      reads=[rsum.t()], writes=[rrec.t()], tag=f"{tag}_rr{b2}{q}{hg}")
                for hh_ in range(4):
                    p.transpose(psT.t(0, 128, hh_ * 128, hh_ * 128 + 128),
                                pbf.t(0, 128, hh_ * 192, hh_ * 192 + 128), mk.ident_tap,
                                tag=f"{tag}_tp{b2}{q}{hg}{hh_}a")
                    p.transpose(psT.t(0, 64, 512 + hh_ * 128, 512 + hh_ * 128 + 128),
                                pbf.t(0, 128, hh_ * 192 + 128, hh_ * 192 + 192), mk.ident_tap,
                                tag=f"{tag}_tp{b2}{q}{hg}{hh_}b")
                p.copy("v", atsb.t(0, 128, 0, 512), psT.t(0, 128, 0, 512),
                       tag=f"{tag}_at{b2}{q}{hg}")
                p.copy("v", atsb.t(0, 64, 512, 1024), psT.t(0, 64, 512, 1024),
                       tag=f"{tag}_at{b2}{q}{hg}b")
                for hh_ in range(4):
                    h = hg * 4 + hh_
                    vc1 = vext.t(b2 * 9 + q, h * 64, (h + 1) * 64)
                    vc2 = vext.t(b2 * 9 + q + 1, h * 64, (h + 1) * 64, p0=0, p1=64)
                    pso = psO.t(0, 128, hh_ * 64, (hh_ + 1) * 64)
                    p.mmgroup([(atsb.t(0, 128, hh_ * 128, hh_ * 128 + 128), vc1),
                               (atsb.t(0, 64, 512 + hh_ * 128, 512 + hh_ * 128 + 128), vc2)],
                              pso, tag=f"{tag}_ao{b2}{q}{h}")
                    p.ts("v", win.t(b2 * 8 + q, h * 64, (h + 1) * 64), pso,
                         rrec.t(0, 128, hh_, hh_ + 1), ALU.mult, tag=f"{tag}_wo{b2}{q}{h}")
    qloc.free()
    kloc.free()
    khalo.free()
    vext.free()
    hh.free()
    nmrh.free()
    pbf.free()
    nrm4.free()
    rsum.free()
    rrec.free()
    atsb.free()
    for v in b8:
        b8[v].free()

    mk.tap("w5_win", win, 16, D, BF16)

    if W5_TRUNC <= 4:
        xnew = slab.falloc(CH, NCOL, F32, "xtr")
        for c in range(CH):
            restream_x_chunk(mk, st, c, _TruncView(xnew, c), "tr")
        st.x = xnew
        return
    winfm = slab.falloc(CH, NCOL, BF16, f"winfm_{tag}")
    for mj in range(CH):
        for grp in range(2):
            bank = mk.psB[(mj * 2 + grp) % 2]
            for k8 in range(8):
                tq = grp * 8 + k8
                p.transpose(bank.t(0, 128, k8 * 128, (k8 + 1) * 128),
                            win.t(tq, mj * 128, (mj + 1) * 128), mk.ident_tap,
                            tag=f"{tag}_wt{mj}{tq}")
            p.copy("v", winfm.t(mj, grp * 1024, (grp + 1) * 1024), bank.t(),
                   tag=f"{tag}_wtev{mj}{grp}")
    win.free()

    xg = slab.falloc(CH, NCOL, BF16, f"xg2_{tag}")
    for c in range(CH):
        p.add("y", (lambda e, c=c, o=xg.t(c).ap: e.dma_start(out=o, in_=xgsp[:, c, :])),
              reads=[xg_spill.box((0, 1, c, c + 1))], writes=[xg.t(c)], kind="d",
              tag=f"{tag}_xgld{c}")
    w_qlin = mk.load_weight_bf16(WD["l1a_wq_lin"], D, tag=f"{tag}_wqn")
    cs_qn = proj_cs_row(mk, w_qlin, D, gcol_bf=lng_bf, tag=f"{tag}_csqn")
    bc_qn = matcol(mk, w_qlin, lnb_bf, extra_col=mk.load_col(WD["l1a_bq_lin"], tag=f"{tag}_bqn"),
                   tag=f"{tag}_bcqn")
    phq = slab.falloc(CH, NCOL, BF16, f"phq_{tag}")
    proj_fm(mk, xg, w_qlin, D, bc_qn, phi_evac_fm(mk, phq, f"{tag}_phq"), f"{tag}_qn",
            cs_row=cs_qn, nmr=nmr)
    w_qlin.free()
    cs_qn.free()
    bc_qn.free()
    kvbf = slab.falloc(16, D, BF16, f"kvbf_{tag}")
    kvo = kv_out.handle.ap()[0:NB * D * D].rearrange("(b dc eh p n) -> b dc eh p n",
                                                     b=NB, dc=CH, eh=2, p=128)
    for b2 in range(NB):
        for dc in range(CH):
            for eh in range(2):
                p.add("g", (lambda e, o=kvbf.t(b2 * 8 + dc, eh * 512, (eh + 1) * 512).ap,
                            a=kvo[b2, dc, eh]: e.dma_start(out=o, in_=a)),
                      reads=[kv_out.t()], writes=[kvbf.t(b2 * 8 + dc, eh * 512, (eh + 1) * 512)],
                      kind="d", tag=f"{tag}_kvrb{b2}{dc}{eh}")
    kso = kv_out.handle.ap()[NB * D * D:]
    ksumc = slab.alloc(16, BF16, f"ksumc_{tag}")
    p.add("g", (lambda e, o=ksumc.t().ap,
                a=kso.rearrange("(p j) -> p j", p=128):
                e.dma_start(out=o, in_=a)),
          reads=[kv_out.t()], writes=[ksumc.t()], kind="d", tag=f"{tag}_ksrb")
    denrow = slab.alloc(NCOL, F32, f"den_{tag}")
    for b2 in range(NB):
        for nt in range(2):
            c0 = b2 * SL + nt * 512
            ps = mk.psF[5].t(0, 1, 0, 512)
            p.mmgroup([(ksumc.t(0, 128, b2 * 8 + dc, b2 * 8 + dc + 1),
                        phq.t(dc, c0, c0 + 512)) for dc in range(CH)], ps,
                      tag=f"{tag}_dn{b2}{nt}")
            p.copy("v", denrow.t(0, 1, c0, c0 + 512), ps, tag=f"{tag}_dnev{b2}{nt}")
    p.ts("v", denrow.t(0, 1), denrow.t(0, 1), 1e-6, ALU.add, tag=f"{tag}_deps")
    p.act(denrow.t(0, 1), denrow.t(0, 1), AF.Ln, tag=f"{tag}_dln")
    p.act(denrow.t(0, 1), denrow.t(0, 1), AF.Exp, scale=-1.0, tag=f"{tag}_drec")
    r2bc = slab.alloc(NCOL, F32, f"r2bc_{tag}")
    for nt, (c0, c1) in enumerate(NT512):
        ps = mk.psF[5].t()
        p.mmgroup([(mk.ones_row_f32(), denrow.t(0, 1, c0, c1))], ps, tag=f"{tag}_r2b{nt}")
        p.copy("v", r2bc.t(0, 128, c0, c1), ps, tag=f"{tag}_r2e{nt}")
    denrow.free()
    ksumc.free()
    linfm = slab.falloc(CH, NCOL, BF16, f"lin_{tag}")
    for b2 in range(NB):
        for mj in range(CH):
            for nt in range(2):
                c0 = b2 * SL + nt * 512
                pairs = [(kvbf.t(b2 * 8 + dc, mj * 128, (mj + 1) * 128),
                          phq.t(dc, c0, c0 + 512)) for dc in range(CH)]
                ps = mk.psF[(mj * 2 + nt) % 4].t()
                p.mmgroup(pairs, ps, tag=f"{tag}_nm{b2}{mj}{nt}")
                p.tt("v", linfm.t(mj, c0, c0 + 512), ps,
                     TAP(r2bc.t(0, 128, c0 - b2 * SL + b2 * SL, c0 + 512).ap, "slab",
                         r2bc.t().box) if False else r2bc.t(0, 128, c0, c0 + 512),
                     ALU.mult, tag=f"{tag}_lin{b2}{mj}{nt}")
    mk.tap("w5_lin", linfm, CH, NCOL, BF16)
    phq.free()
    kvbf.free()
    r2bc.free()


    if W5_TRUNC <= 5:
        xnew = slab.falloc(CH, NCOL, F32, "xtr")
        for c in range(CH):
            restream_x_chunk(mk, st, c, _TruncView(xnew, c), "tr")
        st.x = xnew
        return
    w_g1 = mk.load_weight_bf16(WD["l1a_wg1"], D, tag=f"{tag}_wg1a")
    w_g1b = mk.load_weight_bf16(WD["l1a_wg1b"], D, tag=f"{tag}_wg1b")
    w_g1c = mk.load_weight_bf16(WD["l1a_wg1c"], D, tag=f"{tag}_wg1c")
    cs_g1 = proj_cs_row(mk, w_g1c, D, gcol_bf=lng_bf, tag=f"{tag}_csg1")
    bc_g1 = matcol(mk, w_g1c, lnb_bf, extra_col=mk.load_col(WD["l1a_bg1"], tag=f"{tag}_bg1"),
                   tag=f"{tag}_bcg1")
    g1 = slab.falloc(CH, NCOL, BF16, f"g1_{tag}")
    for mj in range(CH):
        for nt, (c0, c1) in enumerate(NT512):
            pairs = [(w_g1.t(c, mj * 128, (mj + 1) * 128), linfm.t(c, c0, c1))
                     for c in range(CH)]
            pairs += [(w_g1b.t(c, mj * 128, (mj + 1) * 128), winfm.t(c, c0, c1))
                      for c in range(CH)]
            pairs += [(w_g1c.t(c, mj * 128, (mj + 1) * 128), xg.t(c, c0, c1))
                      for c in range(CH)]
            pairs.append((cs_g1.t(0, 1, mj * 128, (mj + 1) * 128), nmr.t(0, 1, c0, c1)))
            ps = mk.psF[(mj * 4 + nt) % 5].t()
            p.mmgroup(pairs, ps, tag=f"{tag}_g1{mj}{nt}")
            p.act(g1.t(mj, c0, c1), ps, AF.Relu, bias=bc_g1.t(0, 128, mj, mj + 1),
                  tag=f"{tag}_g1ev{mj}{nt}")
    w_g1.free()
    w_g1b.free()
    w_g1c.free()
    cs_g1.free()
    bc_g1.free()
    xg.free()
    nmr.free()

    mk.tap("w5_g1", g1, CH, NCOL, BF16)
    w_g2 = mk.load_weight_bf16(WD["l1a_wg2"], D, tag=f"{tag}_wg2")
    bg2_col = mk.load_col(WD["l1a_bg2"], tag=f"{tag}_bg2")
    oa = slab.falloc(CH, NCOL, BF16, f"oa_{tag}")
    for mj in range(CH):
        for nt, (c0, c1) in enumerate(NT512):
            pairs = [(w_g2.t(c, mj * 128, (mj + 1) * 128), g1.t(c, c0, c1))
                     for c in range(CH)]
            ps = mk.psF[(mj * 4 + nt) % 5].t()
            p.mmgroup(pairs, ps, tag=f"{tag}_g2{mj}{nt}")
            gt = slab.alloc(512, F32, "gt2")
            dd = slab.alloc(512, F32, "dd2")
            p.act(gt.t(), ps, AF.Sigmoid, bias=bg2_col.t(0, 128, mj, mj + 1),
                  tag=f"{tag}_g2s{mj}{nt}")
            p.tt("v", dd.t(), linfm.t(mj, c0, c1), winfm.t(mj, c0, c1), ALU.subtract,
                 tag=f"{tag}_d{mj}{nt}")
            p.tt("v", dd.t(), gt.t(), dd.t(), ALU.mult, tag=f"{tag}_gd{mj}{nt}")
            p.tt("v", oa.t(mj, c0, c1), dd.t(), winfm.t(mj, c0, c1), ALU.add,
                 tag=f"{tag}_oa{mj}{nt}")
            gt.free()
            dd.free()
    w_g2.free()
    bg2_col.free()
    linfm.free()
    winfm.free()
    g1.free()

    mk.tap("w5_oa", oa, CH, NCOL, BF16)
    w_o = mk.load_weight_bf16(WD["l1a_wo"], D, tag=f"{tag}_wo")
    bo_col = mk.load_col(WD["l1a_bo"], tag=f"{tag}_bo")
    xnew = slab.falloc(CH, NCOL, F32, f"xnew_{tag}")
    for mj in range(CH):
        xold = slab.alloc(NCOL, F32, f"xoldw{mj}")
        restream_x_chunk(mk, st, mj, xold, tag)
        xlold = slab.alloc(NCOL, F32, f"xlold{mj}")
        p.add("y", (lambda e, mj=mj, o=xlold.t().ap: e.dma_start(out=o, in_=xlsp[:, mj, :])),
              reads=[xln_spill.box((0, 1, mj, mj + 1))], writes=[xlold.t()], kind="d",
              tag=f"{tag}_xlld{mj}")
        for nt, (c0, c1) in enumerate(NT512):
            pairs = [(w_o.t(c, mj * 128, (mj + 1) * 128), oa.t(c, c0, c1))
                     for c in range(CH)]
            ps = mk.psF[(mj * 4 + nt) % 5].t()
            p.mmgroup(pairs, ps, tag=f"{tag}_wo{mj}{nt}")
            p.stt("v", xnew.t(mj, c0, c1), ps, bo_col.t(0, 128, mj, mj + 1),
                  xold.t(0, 128, c0, c1), ALU.add, ALU.add, tag=f"{tag}_wres{mj}{nt}")
            p.tt("v", xnew.t(mj, c0, c1), xnew.t(mj, c0, c1), xlold.t(0, 128, c0, c1),
                 ALU.add, tag=f"{tag}_wres2{mj}{nt}")
        xold.free()
        xlold.free()
    w_o.free()
    bo_col.free()
    oa.free()
    for b in (lng_col, lnb_col, lng_bf, lnb_bf):
        b.free()
    st.x = xnew



def build_nc(n_steps=None):
    if n_steps is None:
        n_steps = N_STEPS
    es = ExitStack()
    nc = bass.Bass("TRN2", target_bir_lowering=False)
    mk = MK(nc, es, n_steps)
    p = mk.p

    x_in = mk.din("x_in", [D, NCOL])
    y_out = mk.dout("y_out", [D, NCOL])

    WD = {}
    for nm in ["l0f_wf", "l0f_wi", "l0f_wc", "l0f_wo",
               "l0f_uf", "l0f_ui", "l0f_uc", "l0f_uo",
               "l0a_wk", "l0a_wka", "l0a_wv"]:
        WD[nm] = mk.din(nm, [D, D])
    for nm in ["l0f_bf", "l0f_bi", "l0f_bc", "l0f_bo", "l0f_inc",
               "l0a_bk", "l0a_bka", "l0a_bv", "l0a_gamma", "l0a_beta"]:
        WD[nm] = mk.din(nm, [D])
    WD["l1f_w_up"] = mk.din("l1f_w_up", [3072, D])
    WD["l1f_w_gate"] = mk.din("l1f_w_gate", [3072, D])
    WD["l1f_w_down"] = mk.din("l1f_w_down", [D, 3072])
    for nm in ["l1a_wq_lin", "l1a_wk_lin", "l1a_wv_lin", "l1a_wq_loc", "l1a_wk_loc",
               "l1a_wv_loc", "l1a_wo", "l1a_wg1", "l1a_wg1b", "l1a_wg1c", "l1a_wg2"]:
        WD[nm] = mk.din(nm, [D, D])
    for nm in ["l1a_bq_lin", "l1a_bk_lin", "l1a_bv_lin", "l1a_bq_loc", "l1a_bk_loc",
               "l1a_bv_loc", "l1a_bo", "l1a_bg1", "l1a_bg2", "l1a_ln_g", "l1a_ln_b"]:
        WD[nm] = mk.din(nm, [D])
    ident_in = mk.din("identity_in", [128, 128])
    p.add("g", (lambda e, o=mk.ident_tap.ap, a=ident_in.handle.ap()[:, :]:
                e.dma_start(out=o, in_=a)),
          reads=[ident_in.t()], writes=[mk.ident_tap], kind="d", tag="ldident")
    rank_in = mk.din("rank_ids", [2], I32)
    p.add("y", (lambda e, o=mk.rank_t.ap()[0:1, :], a=rank_in.handle.ap()[None, :]:
                e.dma_start(out=o, in_=a)),
          reads=[rank_in.t()], writes=[TAP(None, "rankt", (0, 1, 0, 2))], kind="d",
          tag="ldrank")

    pfx = mk.din("pfx_mask", [8])
    pfx_buf = mk.slab.alloc(8, F32, "pfx")
    mk.pfx_tap = TAP(mk.slab.handle.ap()[0:8, pfx_buf.boff:pfx_buf.boff + 4].bitcast(F32),
                     "slab", (0, 8, pfx_buf.boff, pfx_buf.boff + 4))
    p.add("y", (lambda e, o=mk.pfx_tap.ap, a=pfx.handle.ap()[:, None]: e.dma_start(out=o, in_=a)),
          reads=[pfx.t()], writes=[mk.pfx_tap], kind="d", tag="ldpfx")

    xch = mk.slab.falloc(CH, NCOL, F32, "x0")
    xin_r = x_in.handle.ap().rearrange("(c p) n -> p c n", p=128)
    for c in range(CH):
        p.add("y", (lambda e, c=c, o=xch.t(c).ap: e.dma_start(out=o, in_=xin_r[:, c, :])),
              reads=[x_in.t()], writes=[xch.t(c)], kind="d", tag=f"ldx{c}")

    st = State(mk, xch)
    st.x_spill = mk.dinternal("x_spill", [D, NCOL])
    if n_steps >= 4:
        gen = sub_l1_prep_gen(mk, st, WD)
        mk.pump = lambda n=2: [next(gen, None) for _ in range(n)]

    steps = [
        lambda: sub_l0_ffn(mk, st, WD, True),
        lambda: sub_l0_attn(mk, st, WD),
        lambda: sub_l0_ffn(mk, st, WD, False),
        lambda: sub_l1_ffn(mk, st, WD, True),
        lambda: sub_l1_attn(mk, st, WD),
        lambda: sub_l1_ffn(mk, st, WD, False),
    ]
    for i in range(n_steps):
        if i == 3:
            mk.pump(10000)
            for b in mk._dq_transients:
                b.free()
        steps[i]()

    yout_r = y_out.handle.ap().rearrange("(c p) n -> p c n", p=128)
    for c in range(CH):
        p.add("y", (lambda e, c=c, a=st.x.t(c).ap: e.dma_start(out=yout_r[:, c, :], in_=a)),
              reads=[st.x.t(c)], writes=[y_out.t()], kind="d", tag=f"sty{c}")

    p.finalize()
    with nc.Block() as block:
        p.emit_all(block)
    es.close()
    return nc



def stage_inputs(inputs):
    f32 = np.float32

    def T(a):
        return np.ascontiguousarray(np.asarray(a, f32).T)

    def V(a):
        return np.ascontiguousarray(np.asarray(a, f32))

    x = np.asarray(inputs["L0_attn_hidden_states"], f32)
    shared = {
        "l0f_wf": T(inputs["L0_ffn_wf"]), "l0f_wi": T(inputs["L0_ffn_wi"]),
        "l0f_wc": T(inputs["L0_ffn_wc"]), "l0f_wo": T(inputs["L0_ffn_wo"]),
        "l0f_uf": T(inputs["L0_ffn_uf"]), "l0f_ui": T(inputs["L0_ffn_ui"]),
        "l0f_uc": T(inputs["L0_ffn_uc"]), "l0f_uo": T(inputs["L0_ffn_uo"]),
        "l0f_bf": V(inputs["L0_ffn_bf"]), "l0f_bi": V(inputs["L0_ffn_bi"]),
        "l0f_bc": V(inputs["L0_ffn_bc"]), "l0f_bo": V(inputs["L0_ffn_bo"]),
        "l0f_inc": V(inputs["L0_ffn_in_c"]),
        "l0a_wk": T(inputs["L0_attn_Wk"]), "l0a_wka": T(inputs["L0_attn_Wka"]),
        "l0a_wv": T(inputs["L0_attn_Wv"]),
        "l0a_bk": V(inputs["L0_attn_bk"]), "l0a_bka": V(inputs["L0_attn_bka"]),
        "l0a_bv": V(inputs["L0_attn_bv"]),
        "l0a_gamma": V(inputs["L0_attn_gamma"]), "l0a_beta": V(inputs["L0_attn_beta"]),
        "l1f_w_up": V(inputs["L1_ffn_w_up"]), "l1f_w_gate": V(inputs["L1_ffn_w_gate"]),
        "l1f_w_down": V(inputs["L1_ffn_w_down"]),
        "l1a_wq_lin": V(inputs["L1_attn_wq_lin"]), "l1a_wk_lin": V(inputs["L1_attn_wk_lin"]),
        "l1a_wv_lin": V(inputs["L1_attn_wv_lin"]), "l1a_wq_loc": V(inputs["L1_attn_wq_loc"]),
        "l1a_wk_loc": V(inputs["L1_attn_wk_loc"]), "l1a_wv_loc": V(inputs["L1_attn_wv_loc"]),
        "l1a_wo": V(inputs["L1_attn_wo"]),
        "l1a_wg1": V(inputs["L1_attn_wg1"][0:1024]),
        "l1a_wg1b": V(inputs["L1_attn_wg1"][1024:2048]),
        "l1a_wg1c": V(inputs["L1_attn_wg1"][2048:3072]),
        "l1a_wg2": V(inputs["L1_attn_wg2"]),
        "l1a_bq_lin": V(inputs["L1_attn_bq_lin"]), "l1a_bk_lin": V(inputs["L1_attn_bk_lin"]),
        "l1a_bv_lin": V(inputs["L1_attn_bv_lin"]), "l1a_bq_loc": V(inputs["L1_attn_bq_loc"]),
        "l1a_bk_loc": V(inputs["L1_attn_bk_loc"]), "l1a_bv_loc": V(inputs["L1_attn_bv_loc"]),
        "l1a_bo": V(inputs["L1_attn_bo"]), "l1a_bg1": V(inputs["L1_attn_bg1"]),
        "l1a_bg2": V(inputs["L1_attn_bg2"]),
        "l1a_ln_g": V(inputs["L1_attn_ln_g"]), "l1a_ln_b": V(inputs["L1_attn_ln_b"]),
        "identity_in": np.eye(128, dtype=f32),
    }
    rel = np.asarray(inputs["L1_attn_rel_bias"], f32)
    pp, jj = np.meshgrid(np.arange(128), np.arange(192), indexing="ij")
    idx = np.clip(jj - pp + 32, 0, 128)
    btol = rel[idx, :]
    btol2 = np.ascontiguousarray(btol.transpose(0, 2, 1).reshape(128, 16 * 192))
    band = (jj - pp >= 0) & (jj - pp <= 64)
    maskI = np.where(band, 0.0, -1e30).astype(f32)
    shared["btol"] = btol2
    shared["wmaskI"] = maskI
    in_maps = []
    for k in range(NCORES):
        sl = slice(k * SL, (k + 1) * SL)
        xk = np.ascontiguousarray(x[:, sl, :].transpose(2, 0, 1).reshape(D, NCOL))
        m = dict(shared)
        m["x_in"] = xk
        m["pfx_mask"] = (np.arange(8) < k).astype(np.float32)
        mF = maskI.copy()
        if k == 0:
            mF = np.where(jj < 32, np.float32(-1e30), mF).astype(f32)
        mL = maskI.copy()
        if k == NCORES - 1:
            mL = np.where(jj >= 160, np.float32(-1e30), mL).astype(f32)
        m["wmaskF"] = mF
        m["wmaskL"] = mL
        m["rank_ids"] = np.array([(k - 1) % NCORES, (k + 1) % NCORES], np.int32)
        in_maps.append(m)
    return in_maps


def gather(results):
    out = np.empty((NB, SEQ, D), dtype=np.float32)
    for k in range(NCORES):
        yk = np.asarray(results[k]["y_out"])
        out[:, k * SL:(k + 1) * SL, :] = yk.reshape(D, NB, SL).transpose(1, 2, 0)
    return out


def kernel(**inputs):
    nc = build_nc()
    in_maps = stage_inputs(inputs)
    res = run_bass_kernel_spmd(nc, in_maps, core_ids=list(range(NCORES)))
    return gather(res.results)


# version 13
# speedup vs baseline: 1.0214x; 1.0026x over previous
import numpy as np
from contextlib import ExitStack

import concourse.bass as bass
import concourse.mybir as mybir
from concourse.bass_utils import run_bass_kernel_spmd
from concourse.replica_groups import maybe_share_collective_output_space

F32 = mybir.dt.float32
BF16 = mybir.dt.bfloat16
I32 = mybir.dt.int32
AF = mybir.ActivationFunctionType
ALU = mybir.AluOpType
AX = mybir.AxisListType.X

NCORES = 8
D = 1024
SEQ = 8192
NB = 2
SL = SEQ // NCORES
NCOL = NB * SL
CH = D // 128
HALO = 32
LN_EPS = 1e-5
NT512 = [(i * 512, (i + 1) * 512) for i in range(NCOL // 512)]

N_STEPS = 6
SLAB_BYTES = 204800
DEBUG_TAPS = []
W5_TRUNC = 99
W5_NBQ = None



def boxes_overlap(a, b):
    return not (a[1] <= b[0] or b[1] <= a[0] or a[3] <= b[2] or b[3] <= a[2])


def box_covers(o, i):
    return o[0] <= i[0] and o[1] >= i[1] and o[2] <= i[2] and o[3] >= i[3]


class TAP:
    __slots__ = ("ap", "name", "box")

    def __init__(self, ap, name, box):
        self.ap = ap
        self.name = name
        self.box = box


class Op:
    __slots__ = ("idx", "eng", "emit", "kind", "deps", "sem", "val",
                 "pre_waits", "drain_before", "tag")

    def __init__(self, idx, eng, emit, kind, tag):
        self.idx = idx
        self.eng = eng
        self.emit = emit
        self.kind = kind
        self.deps = set()
        self.sem = None
        self.val = 0
        self.pre_waits = []
        self.drain_before = False
        self.tag = tag


class Prog:
    ENGS = ("t", "v", "s", "g", "y")

    def __init__(self, nc, es, n_dma_sems=12):
        self.nc = nc
        self.es = es
        self.ops = []
        self.writers = {}
        self.readers = {}
        self.eng_sem = {e: es.enter_context(nc.semaphore(f"sem_{e}")) for e in self.ENGS}
        self.dma_pools = {e: [es.enter_context(nc.semaphore(f"dma_{e}{i}"))
                              for i in range(n_dma_sems)] for e in ("y", "g")}
        self.dma_rr = {e: 0 for e in ("y", "g")}
        self.dma_tot = {}
        self.cc_sems = []

    def _dep_read(self, op, taps):
        for t in taps:
            for (b, oi) in self.writers.get(t.name, ()):
                if oi != op.idx and boxes_overlap(b, t.box):
                    op.deps.add(oi)
            self.readers.setdefault(t.name, []).append((t.box, op.idx))

    def _dep_write(self, op, taps):
        for t in taps:
            ws = self.writers.setdefault(t.name, [])
            for (b, oi) in ws:
                if oi != op.idx and boxes_overlap(b, t.box):
                    op.deps.add(oi)
            rs = self.readers.setdefault(t.name, [])
            keep = []
            for (b, oi) in rs:
                if oi == op.idx:
                    keep.append((b, oi))
                    continue
                if boxes_overlap(b, t.box):
                    op.deps.add(oi)
                    if not box_covers(t.box, b):
                        keep.append((b, oi))
                else:
                    keep.append((b, oi))
            self.readers[t.name] = keep
            ws2 = [(b, oi) for (b, oi) in ws if not box_covers(t.box, b)]
            ws2.append((t.box, op.idx))
            self.writers[t.name] = ws2

    def add(self, eng, emit, reads=(), writes=(), kind="c", tag=""):
        op = Op(len(self.ops), eng, emit, kind, tag)
        self.ops.append(op)
        self._dep_read(op, reads)
        self._dep_write(op, writes)
        return op

    def tt(self, eng, out, in0, in1, op, tag=""):
        return self.add(eng, (lambda e, o=out.ap, a=in0.ap, b=in1.ap, alu=op:
                              e.tensor_tensor(out=o, in0=a, in1=b, op=alu)),
                        reads=[in0, in1], writes=[out], tag=tag or "tt")

    def ts(self, eng, out, in0, s1, op0, s2=None, op1=None, tag=""):
        s1ap = s1.ap if isinstance(s1, TAP) else s1
        rd = [in0] + ([s1] if isinstance(s1, TAP) else [])
        if op1 is None:
            return self.add(eng, (lambda e, o=out.ap, a=in0.ap, s=s1ap, alu=op0:
                                  e.tensor_scalar(out=o, in0=a, scalar1=s, scalar2=None, op0=alu)),
                            reads=rd, writes=[out], tag=tag or "ts")
        s2ap = s2.ap if isinstance(s2, TAP) else s2
        if isinstance(s2, TAP):
            rd.append(s2)
        return self.add(eng, (lambda e, o=out.ap, a=in0.ap, s=s1ap, s2_=s2ap, a0=op0, a1=op1:
                              e.tensor_scalar(out=o, in0=a, scalar1=s, scalar2=s2_, op0=a0, op1=a1)),
                        reads=rd, writes=[out], tag=tag or "ts2")

    def stt(self, eng, out, in0, scalar, in1, op0, op1, tag=""):
        sap = scalar.ap if isinstance(scalar, TAP) else scalar
        rd = [in0, in1] + ([scalar] if isinstance(scalar, TAP) else [])
        return self.add(eng, (lambda e, o=out.ap, a=in0.ap, s=sap, b=in1.ap, a0=op0, a1=op1:
                              e.scalar_tensor_tensor(out=o, in0=a, scalar=s, in1=b, op0=a0, op1=a1)),
                        reads=rd, writes=[out], tag=tag or "stt")

    def act(self, out, in_, func, bias=None, scale=None, accum=None, tag=""):
        rd = [in_]
        bap = bias.ap if isinstance(bias, TAP) else bias
        sap = scale.ap if isinstance(scale, TAP) else scale
        if isinstance(bias, TAP):
            rd.append(bias)
        if isinstance(scale, TAP):
            rd.append(scale)
        wr = [out] + ([accum] if accum is not None else [])

        def em(e, o=out.ap, a=in_.ap, f=func, b=bap, s=sap, acc=accum):
            kw = {}
            if b is not None:
                kw["bias"] = b
            if s is not None:
                kw["scale"] = s
            if acc is not None:
                kw["accum_out"] = acc.ap
            return e.activation(out=o, in_=a, func=f, **kw)
        return self.add("s", em, reads=rd, writes=wr, tag=tag or "act")

    def copy(self, eng, out, in_, tag=""):
        if eng == "s":
            return self.add(eng, (lambda e, o=out.ap, a=in_.ap: e.copy(out=o, in_=a)),
                            reads=[in_], writes=[out], tag=tag or "scopy")
        return self.add(eng, (lambda e, o=out.ap, a=in_.ap: e.tensor_copy(out=o, in_=a)),
                        reads=[in_], writes=[out], tag=tag or "copy")

    def red(self, eng, out, in_, op, absval=False, negate=False, tag=""):
        return self.add(eng, (lambda e, o=out.ap, a=in_.ap, al=op, av=absval, ng=negate:
                              e.tensor_reduce(out=o, in_=a, axis=AX, op=al,
                                              apply_absolute_value=av, negate=ng)),
                        reads=[in_], writes=[out], tag=tag or "red")

    def scan_add(self, eng, out, data, initial=0.0, tag=""):
        iap = initial.ap if isinstance(initial, TAP) else initial
        rd = [data] + ([initial] if isinstance(initial, TAP) else [])
        return self.add(eng, (lambda e, o=out.ap, d=data.ap, i=iap:
                              e.tensor_tensor_scan(out=o, data0=d, data1=d, initial=i,
                                                   op0=ALU.add, op1=ALU.bypass)),
                        reads=rd, writes=[out], tag=tag or "scan")

    def mmgroup(self, pairs, out, tag=""):
        rd = []
        for l, r in pairs:
            rd += [l, r]
        n = len(pairs)

        def em(e, ps=out.ap, pr=[(l.ap, r.ap) for l, r in pairs], n=n):
            inst = None
            for i, (la, ra) in enumerate(pr):
                inst = e.matmul(ps, la, ra, start=(i == 0), stop=(i == n - 1))
            return inst
        return self.add("t", em, reads=rd, writes=[out], tag=tag or "mm")

    def transpose(self, out, in_, ident, tag=""):
        return self.add("t", (lambda e, o=out.ap, a=in_.ap, i=ident.ap:
                              e.transpose(o, a, i)),
                        reads=[in_, ident], writes=[out], tag=tag or "tr")

    def coll(self, kind, op, cin, cout, rg, tag=""):
        return self.add("g", (lambda e, k=kind, o=op, a=cin.ap, b=cout.ap, g=rg:
                              e.collective_compute(k, o, ins=[a], outs=[b], replica_groups=g)),
                        reads=[cin], writes=[cout], kind="cc", tag=tag or kind)

    def finalize(self):
        eng_ord = {e: 0 for e in self.ENGS}
        for op in self.ops:
            if op.kind == "c":
                eng_ord[op.eng] += 1
                op.sem = self.eng_sem[op.eng]
                op.val = eng_ord[op.eng]
            elif op.kind == "d":
                pool = self.dma_pools[op.eng]
                sem = pool[self.dma_rr[op.eng] % len(pool)]
                self.dma_rr[op.eng] += 1
                prev = self.dma_tot.get(id(sem), 0)
                if prev:
                    op.pre_waits.append((sem, prev))
                self.dma_tot[id(sem)] = prev + 16
                op.sem = sem
                op.val = prev + 16
            elif op.kind == "cc":
                sem = self.es.enter_context(self.nc.semaphore(f"cc{len(self.cc_sems)}"))
                self.cc_sems.append(sem)
                op.sem = sem
                op.val = 1
        watermark = {}
        for op in self.ops:
            per_sem = {}
            for di in sorted(op.deps):
                dop = self.ops[di]
                if dop.sem is None:
                    continue
                k = id(dop.sem)
                if k not in per_sem or per_sem[k][1] < dop.val:
                    per_sem[k] = (dop.sem, dop.val)
            for k, (sem, val) in per_sem.items():
                wk = (op.eng, k)
                if watermark.get(wk, -1) >= val:
                    continue
                watermark[wk] = val
                op.pre_waits.append((sem, val))
        finals = []
        for e in self.ENGS:
            if eng_ord[e]:
                finals.append((self.eng_sem[e], eng_ord[e]))
        for e in ("y", "g"):
            for sem in self.dma_pools[e]:
                tot = self.dma_tot.get(id(sem), 0)
                if tot:
                    finals.append((sem, tot))
        for sem in self.cc_sems:
            finals.append((sem, 1))
        self.final_waits = finals

    def emit_all(self, block):
        per_eng = {e: [] for e in self.ENGS}
        for op in self.ops:
            per_eng[op.eng].append(op)
        prog = self
        attr = {"t": "tensor", "v": "vector", "s": "scalar", "g": "gpsimd", "y": "sync"}

        def mk_body(engname):
            ops = per_eng[engname]

            def body(eng):
                for op in ops:
                    for (sem, val) in op.pre_waits:
                        eng.wait_ge(sem, val)
                    if op.drain_before and engname != "t":
                        eng.drain()
                    inst = op.emit(eng)
                    if op.sem is not None:
                        assert inst is not None, f"op {op.idx} [{op.tag}] returned no inst"
                        inst.then_inc(op.sem, 16 if op.kind == "d" else 1)
                if engname == "y":
                    for (sem, val) in prog.final_waits:
                        eng.wait_ge(sem, val)
            body.__name__ = f"_{engname}"
            getattr(block, attr[engname])(body)
        for e in self.ENGS:
            mk_body(e)



DT_SIZE = {F32: 4, BF16: 2, I32: 4}


class Buf:
    __slots__ = ("slab", "name", "boff", "cols", "dtype")

    def __init__(self, slab, name, boff, cols, dtype):
        self.slab = slab
        self.name = name
        self.boff = boff
        self.cols = cols
        self.dtype = dtype

    def t(self, p0=0, p1=128, c0=0, c1=None):
        if c1 is None:
            c1 = self.cols
        sz = DT_SIZE[self.dtype]
        b0 = self.boff + c0 * sz
        b1 = self.boff + c1 * sz
        ap = self.slab.handle.ap()[p0:p1, b0:b1].bitcast(self.dtype)
        return TAP(ap, self.slab.name, (p0, p1, b0, b1))

    def free(self):
        self.slab.free(self)


class CBuf:
    def __init__(self, buf, nch, width):
        self.buf = buf
        self.nch = nch
        self.width = width

    def t(self, c, c0=0, c1=None, p0=0, p1=128):
        if c1 is None:
            c1 = self.width
        return self.buf.t(p0, p1, c * self.width + c0, c * self.width + c1)

    def free(self):
        self.buf.free()


class FBuf:

    def __init__(self, bufs, width):
        self.bufs = bufs
        self.width = width

    def t(self, c, c0=0, c1=None, p0=0, p1=128):
        if c1 is None:
            c1 = self.width
        return self.bufs[c].t(p0, p1, c0, c1)

    def free(self):
        for b in self.bufs:
            b.free()


class Slab:
    def __init__(self, nc, es, name, nbytes):
        self.handle = es.enter_context(nc.sbuf_tensor(name, [128, nbytes], mybir.dt.uint8))
        self.name = name
        self.free_list = [(0, nbytes)]
        self.peak = 0
        self.used = 0

    def alloc(self, cols, dtype, tag=""):
        nb = (cols * DT_SIZE[dtype] + 63) // 64 * 64
        cur = getattr(self, "cursor", 0)
        if nb >= 16384:
            order = list(range(len(self.free_list)))
        else:
            order = [i for i, (o, n) in enumerate(self.free_list) if o >= cur] + \
                    [i for i, (o, n) in enumerate(self.free_list) if o < cur]
        for i in order:
            o, n = self.free_list[i]
            if n >= nb:
                if n == nb:
                    self.free_list.pop(i)
                else:
                    self.free_list[i] = (o + nb, n - nb)
                self.used += nb
                self.peak = max(self.peak, self.used)
                if nb < 16384:
                    self.cursor = o + nb
                return Buf(self, tag, o, cols, dtype)
        raise MemoryError(f"slab alloc {nb}B failed (tag={tag}, used={self.used}, free={self.free_list})")

    def challoc(self, nch, width, dtype, tag=""):
        b = self.alloc(nch * width, dtype, tag)
        return CBuf(b, nch, width)

    def falloc(self, nch, width, dtype, tag=""):
        return FBuf([self.alloc(width, dtype, f"{tag}{c}") for c in range(nch)], width)

    def free(self, buf):
        nb = (buf.cols * DT_SIZE[buf.dtype] + 63) // 64 * 64
        self.used -= nb
        self.free_list.append((buf.boff, nb))
        self.free_list.sort()
        merged = []
        for o, n in self.free_list:
            if merged and merged[-1][0] + merged[-1][1] == o:
                merged[-1] = (merged[-1][0], merged[-1][1] + n)
            else:
                merged.append((o, n))
        self.free_list = merged


class DBuf:
    def __init__(self, handle, name):
        self.handle = handle
        self.name = name

    def t(self, box=None):
        return TAP(self.handle.ap(), self.name, box or (0, 1, 0, 1 << 50))

    def box(self, b):
        return TAP(None, self.name, b)


class MK:
    def __init__(self, nc, es, n_steps):
        self.nc = nc
        self.es = es
        self.n_steps = n_steps
        self.p = Prog(nc, es)
        self.slab = Slab(nc, es, "slab", SLAB_BYTES)
        self.rg = [list(range(NCORES))]
        self.inputs = {}
        self._cc_ctr = 0
        self.pump = lambda n=2: None
        self._warm_ctr = 0
        self.psF = []
        for i in range(6):
            h = es.enter_context(nc.psum_tensor(f"psF{i}", [128, 512], F32))
            self.psF.append(_PsBuf(h, f"psF{i}"))
        self.psB = []
        for i in range(2):
            h = es.enter_context(nc.psum_tensor(f"psB{i}", [128, 1024], BF16))
            self.psB.append(_PsBuf(h, f"psB{i}"))
        self.ones_col_bf = es.enter_context(nc.sbuf_tensor("ones_col", [128, 2], BF16))
        self.ones128_bf = es.enter_context(nc.sbuf_tensor("ones128", [128, 128], BF16))
        self.p.add("v", lambda e: e.memset(self.ones_col_bf.ap()[:, :], 1.0),
                   writes=[TAP(None, "ones_col", (0, 128, 0, 2))], tag="ones_c")
        self.p.add("v", lambda e: e.memset(self.ones128_bf.ap()[:, :], 1.0),
                   writes=[TAP(None, "ones128", (0, 128, 0, 128))], tag="ones_m")
        self.ones_row_f32_t = es.enter_context(nc.sbuf_tensor("ones_rf", [1, 128], F32))
        self.p.add("v", lambda e: e.memset(self.ones_row_f32_t.ap()[:, :], 1.0),
                   writes=[TAP(None, "ones_rf", (0, 1, 0, 128))], tag="ones_rf")
        self.ones_row_bf_t = es.enter_context(nc.sbuf_tensor("ones_rb", [1, 128], BF16))
        self.p.add("v", lambda e: e.memset(self.ones_row_bf_t.ap()[:, :], 1.0),
                   writes=[TAP(None, "ones_rb", (0, 1, 0, 128))], tag="ones_rb")
        self.ones512_t = es.enter_context(nc.sbuf_tensor("ones512", [1, 512], BF16))
        self.p.add("v", lambda e: e.memset(self.ones512_t.ap()[:, :], 1.0),
                   writes=[TAP(None, "ones512", (0, 1, 0, 512))], tag="ones512")
        self.ident_t = es.enter_context(nc.sbuf_tensor("ident", [128, 128], BF16))
        self.ident_tap = TAP(self.ident_t.ap()[:, :], "ident", (0, 128, 0, 128))
        self.rank_t = es.enter_context(nc.sbuf_tensor("rankt", [1, 2], I32))

    def warm(self):
        ps = self.psF[5].t(0, 64, 0, 64)
        self.p.add("t", (lambda e, o=ps.ap, a=self.ident_tap.ap:
                         e.matmul(o, a[:, 0:64], a[:, 0:64], start=True, stop=True)),
                   reads=[self.ident_tap], writes=[ps], tag="warm")

    def ones_col(self):
        return TAP(self.ones_col_bf.ap()[:, 0:1], "ones_col", (0, 128, 0, 1))

    def ones_row_f32(self):
        return TAP(self.ones_row_f32_t.ap()[:, :], "ones_rf", (0, 1, 0, 128))

    def ones512(self):
        return TAP(self.ones512_t.ap()[:, :], "ones512", (0, 1, 0, 512))

    def ones_row_bf(self, *a):
        return TAP(self.ones_row_bf_t.ap()[:, :], "ones_rb", (0, 1, 0, 128))

    def rank_tap(self, side):
        return TAP(self.rank_t.ap()[0:1, side:side + 1], "rankt", (0, 1, side, side + 1))

    def ones128(self):
        return TAP(self.ones128_bf.ap()[:, :], "ones128", (0, 128, 0, 128))

    def din(self, name, shape, dtype=F32):
        h = self.nc.dram_tensor(name, list(shape), dtype, kind="ExternalInput")
        b = DBuf(h, name)
        self.inputs[name] = b
        return b

    def dinternal(self, name, shape, dtype=F32, shared_for=None):
        kw = {}
        if shared_for is not None:
            kw["addr_space"] = maybe_share_collective_output_space(shared_for, self.rg,
                                                                   prefer_local=True)
        h = self.nc.dram_tensor(name, list(shape), dtype, kind="Internal", **kw)
        return DBuf(h, name)

    def dout(self, name, shape, dtype=F32):
        h = self.nc.dram_tensor(name, list(shape), dtype, kind="ExternalOutput")
        return DBuf(h, name)

    def tap(self, name, fb, nch, width, dtype=F32):
        if name not in DEBUG_TAPS:
            return
        d = self.dout(f"dbg_{name}", [nch * 128, width], dtype)
        dr = d.handle.ap().rearrange("(c p) n -> p c n", p=128)
        for c in range(nch):
            self.p.add("y", (lambda e, c=c, a=fb.t(c).ap: e.dma_start(out=dr[:, c, :], in_=a)),
                       reads=[fb.t(c)], writes=[d.box((0, 1, c, c + 1))], kind="d",
                       tag=f"tap_{name}{c}")

    def load_weight_bf16(self, dram, E, e0=0, e1=None, tag=""):
        if e1 is None:
            e1 = E
        w = e1 - e0
        cb = self.slab.challoc(CH, w, BF16, tag or dram.name)
        dst = cb.buf.t()
        dstap = dst.ap.rearrange("p (c n) -> p c n", c=CH)
        src = dram.handle.ap().rearrange("(c p) n -> p c n", p=128)[:, :, e0:e1]
        self.p.add("g", (lambda e, o=dstap, a=src: e.dma_start(out=o, in_=a)),
                   reads=[dram.t()], writes=[dst], kind="d", tag=f"ldw_{dram.name}")
        return cb

    def load_col(self, dram, tag=""):
        b = self.slab.alloc(CH, F32, tag or ("col_" + dram.name))
        dst = b.t()
        src = dram.handle.ap().rearrange("(c p) -> p c", p=128)
        self.p.add("y", (lambda e, o=dst.ap, a=src:
                         e.dma_start(out=o, in_=a, allow_slow_non_contiguous=True)),
                   reads=[dram.t()], writes=[dst], kind="d", tag=f"ldc_{dram.name}")
        return b


class _PsBuf:

    def __init__(self, handle, name):
        self.handle = handle
        self.name = name

    def t(self, p0=0, p1=128, c0=0, c1=None):
        if c1 is None:
            c1 = self.handle.shape[1]
        ap = self.handle.ap()[p0:p1, c0:c1]
        return TAP(ap, self.name, (0, 128, 0, self.handle.shape[1]))



class State:
    def __init__(self, mk, xch):
        self.mk = mk
        self.x = xch
        self.x_spill = None
        self.l0f_cache = None


def ln_stats(mk, st, tag, want_nmr_bc=False):
    p, slab = mk.p, mk.slab
    x = st.x
    xbf = slab.falloc(CH, NCOL, BF16, f"xbf_{tag}")
    for c in range(CH):
        p.copy("v", xbf.t(c), x.t(c), tag=f"{tag}_xbf{c}")
    R_bc = slab.alloc(NCOL, F32, f"Rbc_{tag}")
    nmr_row = slab.alloc(NCOL, BF16, f"nmrrow_{tag}")
    nmr_bc = slab.alloc(NCOL, F32, f"nmrbc_{tag}") if want_nmr_bc else None
    x2 = slab.falloc(CH, 512, BF16, f"x2_{tag}")
    cinv = 1.0 / D
    for nt, (c0, c1) in enumerate(NT512):
        pa = mk.psF[0].t()
        pb = mk.psF[1].t()
        for c in range(CH):
            p.act(x2.t(c), xbf.t(c, c0, c1), AF.Square, tag=f"{tag}_x2_{nt}{c}")
        p.mmgroup([(mk.ones128(), xbf.t(c, c0, c1)) for c in range(CH)], pa,
                  tag=f"{tag}_Sx{nt}")
        p.mmgroup([(mk.ones128(), x2.t(c)) for c in range(CH)], pb,
                  tag=f"{tag}_Sxx{nt}")
        mu = slab.alloc(512, F32, "mu")
        te = slab.alloc(512, F32, "te")
        vpe = slab.alloc(512, F32, "vpe")
        r0 = slab.alloc(512, F32, "r0")
        p.ts("v", mu.t(), pa, cinv, ALU.mult, tag=f"{tag}_mu{nt}")
        p.tt("g", te.t(), mu.t(), mu.t(), ALU.mult, tag=f"{tag}_mu2{nt}")
        p.stt("v", vpe.t(), pb, cinv, te.t(), ALU.mult, ALU.subtract, tag=f"{tag}_var{nt}")
        p.ts("g", vpe.t(), vpe.t(), LN_EPS, ALU.add, tag=f"{tag}_vpe{nt}")
        p.act(te.t(), vpe.t(), AF.Ln, tag=f"{tag}_ln{nt}")
        p.act(r0.t(), te.t(), AF.Exp, scale=-0.5, tag=f"{tag}_rsq0{nt}")
        p.tt("g", te.t(), r0.t(), r0.t(), ALU.mult, tag=f"{tag}_nwa{nt}")
        p.tt("g", te.t(), te.t(), vpe.t(), ALU.mult, tag=f"{tag}_nwb{nt}")
        p.ts("v", te.t(), te.t(), -0.5, ALU.mult, 1.5, ALU.add, tag=f"{tag}_nwc{nt}")
        p.tt("v", R_bc.t(0, 128, c0, c1), r0.t(), te.t(), ALU.mult, tag=f"{tag}_R{nt}")
        p.stt("v", nmr_row.t(0, 1, c0, c1), mu.t(0, 1), -1.0, R_bc.t(0, 1, c0, c1),
              ALU.mult, ALU.mult, tag=f"{tag}_nmr{nt}")
        mk.warm()
        if want_nmr_bc:
            p.stt("v", nmr_bc.t(0, 128, c0, c1), mu.t(), -1.0, R_bc.t(0, 128, c0, c1),
                  ALU.mult, ALU.mult, tag=f"{tag}_nmrb{nt}")
        for b in (mu, te, vpe, r0):
            b.free()
    x2.free()
    return {"R_bc": R_bc, "nmr_row": nmr_row, "xbf": xbf, "nmr_bc": nmr_bc}


def spill_x(mk, st, tag):
    p = mk.p
    sp = st.x_spill
    spr = sp.handle.ap().rearrange("(c p) n -> p c n", p=128)
    for c in range(CH):
        p.add("y", (lambda e, c=c, a=st.x.t(c).ap: e.dma_start(out=spr[:, c, :], in_=a)),
              reads=[st.x.t(c)], writes=[sp.box((0, 1, c, c + 1))], kind="d",
              tag=f"{tag}_spill{c}")


def restream_x_chunk(mk, st, c, dstbuf, tag):
    p = mk.p
    spr = st.x_spill.handle.ap().rearrange("(c p) n -> p c n", p=128)
    p.add("y", (lambda e, o=dstbuf.t().ap, a=spr[:, c, :]: e.dma_start(out=o, in_=a)),
          reads=[st.x_spill.box((0, 1, c, c + 1))], writes=[dstbuf.t()],
          kind="d", tag=f"{tag}_restr{c}")


def load_row_bf(mk, dram, E, tag):
    p, slab = mk.p, mk.slab
    r32 = slab.alloc(E, F32, f"lr_{tag}")
    p.add("y", (lambda e, o=r32.t(0, 1).ap, a=dram.handle.ap()[None, :]:
                e.dma_start(out=o, in_=a)),
          reads=[dram.t()], writes=[r32.t(0, 1)], kind="d", tag=f"{tag}_ld")
    rb = slab.alloc(E, BF16, f"lrb_{tag}")
    p.copy("v", rb.t(0, 1), r32.t(0, 1), tag=f"{tag}_cast")
    r32.free()
    return rb


def proj_cs_row(mk, wcb, E, gcol_bf=None, tag=""):
    p, slab = mk.p, mk.slab
    csrow = slab.alloc(E, BF16, f"cs_{tag}")
    for s in range(E // 512):
        ps = mk.psF[2].t(0, 1, 0, 512)
        if gcol_bf is None:
            pairs = [(mk.ones_col(), wcb.t(c, s * 512, (s + 1) * 512)) for c in range(CH)]
        else:
            pairs = [(gcol_bf.t(0, 128, c, c + 1), wcb.t(c, s * 512, (s + 1) * 512))
                     for c in range(CH)]
        p.mmgroup(pairs, ps, tag=f"{tag}_cs{s}")
        p.copy("v", csrow.t(0, 1, s * 512, (s + 1) * 512), ps, tag=f"{tag}_csev{s}")
    return csrow



def sub_l0_ffn(mk, st, WD, first):
    p, slab = mk.p, mk.slab
    tag = "f1" if first else "f2"
    w_w = {g: mk.load_weight_bf16(WD[f"l0f_w{g}"], D, tag=f"{tag}_w{g}") for g in "fic"}
    ln = ln_stats(mk, st, tag)
    ln["xbf"].free()
    xs = slab.falloc(CH, NCOL, BF16, f"xs_{tag}")
    for c in range(CH):
        p.stt("v", xs.t(c), st.x.t(c), 1.0, ln["R_bc"].t(), ALU.mult, ALU.mult,
              tag=f"{tag}_xs{c}")
    mk.tap(f"{tag}_xs", xs, CH, NCOL, BF16)
    spill_x(mk, st, tag)
    st.x.free()
    ln["R_bc"].free()
    nmr = ln["nmr_row"]

    w_u = {g: mk.load_weight_bf16(WD[f"l0f_u{g}"], D, tag=f"{tag}_u{g}") for g in "fic"}
    cs = {g: proj_cs_row(mk, w_w[g], D, tag=f"{tag}_cs{g}") for g in "fic"}
    bcol = {g: mk.load_col(WD[f"l0f_b{g}"], tag=f"{tag}_b{g}") for g in "fico"}
    inc_col = mk.load_col(WD["l0f_inc"], tag=f"{tag}_inc")
    inc_bf = slab.alloc(CH, BF16, f"incbf_{tag}")
    p.copy("v", inc_bf.t(), inc_col.t(), tag=f"{tag}_incbf")

    g0b = {}
    for g in "fic":
        col = slab.alloc(CH, F32, f"{tag}_g0b{g}")
        for mj in range(CH):
            ps = mk.psF[3].t(0, 128, 0, 1)
            p.mmgroup([(w_u[g].t(c, mj * 128, (mj + 1) * 128), inc_bf.t(0, 128, c, c + 1))
                       for c in range(CH)], ps, tag=f"{tag}_g0b{g}{mj}")
            p.copy("v", col.t(0, 128, mj, mj + 1), ps, tag=f"{tag}_g0bev{g}{mj}")
        p.tt("v", col.t(), col.t(), bcol[g].t(), ALU.add, tag=f"{tag}_g0bb{g}")
        g0b[g] = col

    cbf = slab.falloc(CH, SL, BF16, f"cbf_{tag}")
    cbf2 = slab.falloc(CH, SL, BF16, f"cbf2_{tag}")

    for tstep in range(2):
        base = tstep * SL
        for mj in range(CH):
            for half in range(SL // 512):
                c0, c1 = half * 512, (half + 1) * 512
                banks = [mk.psF[i] for i in ((0, 1, 2) if (mj + half) % 2 == 0 else (3, 4, 5))]
                gps = {}
                for gi, g in enumerate("fic"):
                    pairs = [(w_w[g].t(c, mj * 128, (mj + 1) * 128), xs.t(c, base + c0, base + c1))
                             for c in range(CH)]
                    if tstep == 1:
                        pairs += [(w_u[g].t(c, mj * 128, (mj + 1) * 128), cbf.t(c, c0, c1))
                                  for c in range(CH)]
                    pairs.append((cs[g].t(0, 1, mj * 128, (mj + 1) * 128),
                                  nmr.t(0, 1, base + c0, base + c1)))
                    ps = banks[gi].t(0, 128, 0, 512)
                    p.mmgroup(pairs, ps, tag=f"{tag}_t{tstep}{g}m{mj}h{half}")
                    gps[g] = ps
                bsrc = g0b if tstep == 0 else bcol
                tf = slab.alloc(512, F32, "tf")
                ti = slab.alloc(512, F32, "ti")
                tg = slab.alloc(512, F32, "tg")
                p.act(tf.t(), gps["f"], AF.Sigmoid, bias=bsrc["f"].t(0, 128, mj, mj + 1),
                      tag=f"{tag}_sf{tstep}{mj}{half}")
                p.act(ti.t(), gps["i"], AF.Sigmoid, bias=bsrc["i"].t(0, 128, mj, mj + 1),
                      tag=f"{tag}_si{tstep}{mj}{half}")
                p.act(tg.t(), gps["c"], AF.Tanh, bias=bsrc["c"].t(0, 128, mj, mj + 1),
                      tag=f"{tag}_tg{tstep}{mj}{half}")
                pr = slab.alloc(512, F32, "pr")
                p.tt("v", pr.t(), ti.t(), tg.t(), ALU.mult, tag=f"{tag}_ig{tstep}{mj}{half}")
                if tstep == 0:
                    p.stt("v", cbf.t(mj, c0, c1), tf.t(), inc_col.t(0, 128, mj, mj + 1), pr.t(),
                          ALU.mult, ALU.add, tag=f"{tag}_c1{mj}{half}")
                else:
                    p.tt("v", tf.t(), tf.t(), cbf.t(mj, c0, c1), ALU.mult, tag=f"{tag}_fc{mj}{half}")
                    p.tt("v", cbf2.t(mj, c0, c1), tf.t(), pr.t(), ALU.add, tag=f"{tag}_c2{mj}{half}")
                for b in (tf, ti, tg, pr):
                    b.free()
                mk.pump()
                mk.warm()
        if tstep == 0:
            mk.tap(f"{tag}_c1", cbf, CH, SL, BF16)

    mk.tap(f"{tag}_cbf", cbf2, CH, SL, BF16)
    cbf.free()
    cbf = cbf2
    for g in "fic":
        w_u[g].free()
        cs[g].free()
        g0b[g].free()
    w_w["f"].free()
    w_w["i"].free()
    w_w["c"].free()

    w_wo = mk.load_weight_bf16(WD["l0f_wo"], D, tag=f"{tag}_wo")
    w_uo = mk.load_weight_bf16(WD["l0f_uo"], D, tag=f"{tag}_uo")
    cs_o = proj_cs_row(mk, w_wo, D, tag=f"{tag}_cso")
    xnew = slab.falloc(CH, NCOL, F32, f"xnew_{tag}")
    for mj in range(CH):
        xold = slab.alloc(NCOL, F32, f"xold{mj}")
        restream_x_chunk(mk, st, mj, xold, tag)
        for nt, (c0, c1) in enumerate(NT512):
            pairs = [(w_wo.t(c, mj * 128, (mj + 1) * 128), xs.t(c, c0, c1)) for c in range(CH)]
            cc0 = c0 % SL
            pairs += [(w_uo.t(c, mj * 128, (mj + 1) * 128), cbf.t(c, cc0, cc0 + 512))
                      for c in range(CH)]
            pairs.append((cs_o.t(0, 1, mj * 128, (mj + 1) * 128), nmr.t(0, 1, c0, c1)))
            ps = mk.psF[(2 * mj + nt) % 6].t(0, 128, 0, 512)
            p.mmgroup(pairs, ps, tag=f"{tag}_o{mj}{nt}")
            th = slab.alloc(512, F32, "th")
            p.act(th.t(), ps, AF.Tanh, bias=bcol["o"].t(0, 128, mj, mj + 1),
                  tag=f"{tag}_oth{mj}{nt}")
            p.stt("v", xnew.t(mj, c0, c1), th.t(), 0.5, xold.t(0, 128, c0, c1),
                  ALU.mult, ALU.add, tag=f"{tag}_res{mj}{nt}")
            th.free()
            mk.pump()
            mk.warm()
        xold.free()
    for b in (w_wo, w_uo, cs_o, xs, cbf, nmr, inc_col, inc_bf):
        b.free()
    for g in "fico":
        bcol[g].free()
    st.x = xnew


def matcol(mk, wcb, rhs_col_bf, extra_col=None, tag=""):
    p, slab = mk.p, mk.slab
    col = slab.alloc(CH, F32, f"mc_{tag}")
    ps = mk.psF[2]
    for mj in range(CH):
        pt = ps.t(0, 128, mj, mj + 1)
        p.mmgroup([(wcb.t(c, mj * 128, (mj + 1) * 128), rhs_col_bf.t(0, 128, c, c + 1))
                   for c in range(CH)], pt, tag=f"{tag}_mc{mj}")
    p.copy("v", col.t(), ps.t(0, 128, 0, CH), tag=f"{tag}_mcev")
    if extra_col is not None:
        p.tt("v", col.t(), col.t(), extra_col.t(), ALU.add, tag=f"{tag}_mcb")
    return col


def proj_fm(mk, src_bf, wcb, E, bias_col, evac, tag, cs_row=None, nmr=None,
            ncols=NCOL, extra_pairs=None, bias_row=None):
    p = mk.p
    nnt = ncols // 512
    for mj in range(E // 128):
        for nt in range(nnt):
            c0, c1 = nt * 512, (nt + 1) * 512
            pairs = [(wcb.t(c, mj * 128, (mj + 1) * 128), src_bf.t(c, c0, c1))
                     for c in range(CH)]
            if extra_pairs is not None:
                pairs += extra_pairs(mj, c0, c1)
            if cs_row is not None:
                pairs.append((cs_row.t(0, 1, mj * 128, (mj + 1) * 128), nmr.t(0, 1, c0, c1)))
            if bias_row is not None:
                pairs.append((bias_row.t(0, 1, mj * 128, (mj + 1) * 128), mk.ones512()))
            ps = mk.psF[(mj * nnt + nt) % 5].t()
            p.mmgroup(pairs, ps, tag=f"{tag}_p{mj}_{nt}")
            bias = bias_col.t(0, 128, mj, mj + 1) if bias_col is not None else None
            evac(mj, c0, c1, ps, bias)
            mk.pump()
            mk.warm()


def sub_l0_attn(mk, st, WD):
    p, slab = mk.p, mk.slab
    tag = "a0"
    w_k = mk.load_weight_bf16(WD["l0a_wk"], D, tag=f"{tag}_wk")
    gamma_col = mk.load_col(WD["l0a_gamma"], tag=f"{tag}_g")
    beta_col = mk.load_col(WD["l0a_beta"], tag=f"{tag}_b")
    bk_col = mk.load_col(WD["l0a_bk"], tag=f"{tag}_bk")
    bka_col = mk.load_col(WD["l0a_bka"], tag=f"{tag}_bka")
    bv_col = mk.load_col(WD["l0a_bv"], tag=f"{tag}_bv")
    ln = ln_stats(mk, st, tag)
    ln["xbf"].free()
    gbf = slab.alloc(CH, BF16, f"{tag}_gbf")
    bbf = slab.alloc(CH, BF16, f"{tag}_bbf")
    p.copy("v", gbf.t(), gamma_col.t(), tag=f"{tag}_gbf")
    p.copy("v", bbf.t(), beta_col.t(), tag=f"{tag}_bbf")
    xg = slab.falloc(CH, NCOL, BF16, f"xg_{tag}")
    for c in range(CH):
        p.stt("v", xg.t(c), st.x.t(c), gamma_col.t(0, 128, c, c + 1), ln["R_bc"].t(),
              ALU.mult, ALU.mult, tag=f"{tag}_xg{c}")
    spill_x(mk, st, tag)
    st.x.free()
    ln["R_bc"].free()
    nmr = ln["nmr_row"]

    cs_k = proj_cs_row(mk, w_k, D, gcol_bf=gbf, tag=f"{tag}_csk")
    br_k = bias_row_bf(mk, w_k, D, bbf, WD["l0a_bk"], f"{tag}_brk")
    keys = slab.falloc(CH, NCOL, BF16, f"keys_{tag}")

    def ev_keys(mj, c0, c1, ps, bias):
        p.copy("s", keys.t(mj, c0, c1), ps, tag=f"{tag}_evk{mj}")
    proj_fm(mk, xg, w_k, D, None, ev_keys, f"{tag}_k", cs_row=cs_k, nmr=nmr,
            bias_row=br_k)
    w_k.free()
    cs_k.free()
    br_k.free()

    w_v = mk.load_weight_bf16(WD["l0a_wv"], D, tag=f"{tag}_wv")
    cs_v = proj_cs_row(mk, w_v, D, gcol_bf=gbf, tag=f"{tag}_csv")
    br_v = bias_row_bf(mk, w_v, D, bbf, WD["l0a_bv"], f"{tag}_brv")
    values = slab.falloc(CH, NCOL, BF16, f"vals_{tag}")

    def ev_vals(mj, c0, c1, ps, bias):
        p.copy("s", values.t(mj, c0, c1), ps, tag=f"{tag}_evv{mj}")
    proj_fm(mk, xg, w_v, D, None, ev_vals, f"{tag}_v", cs_row=cs_v, nmr=nmr,
            bias_row=br_v)
    w_v.free()
    cs_v.free()
    br_v.free()
    xg.free()
    nmr.free()
    v_spill = mk.dinternal(f"v_spill", [D, NCOL], BF16)
    vsp = v_spill.handle.ap().rearrange("(c p) n -> p c n", p=128)
    for c in range(CH):
        p.add("y", (lambda e, c=c, a=values.t(c).ap: e.dma_start(out=vsp[:, c, :], in_=a)),
              reads=[values.t(c)], writes=[v_spill.box((0, 1, c, c + 1))], kind="d",
              tag=f"{tag}_vsp{c}")
    values.free()

    w_ka = mk.load_weight_bf16(WD["l0a_wka"], D, tag=f"{tag}_wka")
    br_ka = load_row_bf(mk, WD["l0a_bka"], D, f"{tag}_brka")
    katt = slab.falloc(CH, NCOL, BF16, f"katt_{tag}")

    def ev_katt(mj, c0, c1, ps, bias):
        p.copy("s", katt.t(mj, c0, c1), ps, tag=f"{tag}_evka{mj}")
    proj_fm(mk, keys, w_ka, D, None, ev_katt, f"{tag}_ka", bias_row=br_ka)
    w_ka.free()
    br_ka.free()

    A = slab.falloc(CH, NCOL, F32, f"A_{tag}")
    T1 = slab.alloc(16, F32, f"T1_{tag}")
    tmp2 = [slab.alloc(NCOL, F32, f"tmpa{i}") for i in range(2)]
    for c in range(CH):
        tm = tmp2[c % 2]
        p.act(tm.t(), katt.t(c), AF.Exp, scale=1.0 / 32.0, tag=f"{tag}_kw1_{c}")
        for b in range(NB):
            p.scan_add("v", A.t(c, b * SL, (b + 1) * SL), tm.t(0, 128, b * SL, (b + 1) * SL),
                       tag=f"{tag}_sc1_{c}{b}")
            p.copy("v", T1.t(0, 128, b * CH + c, b * CH + c + 1),
                   A.t(c, (b + 1) * SL - 1, (b + 1) * SL), tag=f"{tag}_t1_{c}{b}")
            mk.pump()
            mk.warm()

    carry1 = carry_exchange(mk, T1, f"{tag}_x1")
    T1.free()

    T2 = slab.alloc(16, F32, f"T2_{tag}")
    for c in range(CH):
        tm = tmp2[c % 2]
        zt = slab.alloc(NCOL, F32, "zshared")
        for b in range(NB):
            p.ts("v", A.t(c, b * SL, (b + 1) * SL), A.t(c, b * SL, (b + 1) * SL),
                 carry1.t(0, 128, b * CH + c, b * CH + c + 1), ALU.add, tag=f"{tag}_Ac{c}{b}")
        p.act(tm.t(), A.t(c), AF.Ln, tag=f"{tag}_lnA{c}")
        p.stt("v", tm.t(), katt.t(c), 1.0 / 32.0, tm.t(), ALU.mult, ALU.subtract,
              tag=f"{tag}_dv{c}")
        p.act(tm.t(), tm.t(), AF.Exp, tag=f"{tag}_kw2_{c}")
        p.tt("v", zt.t(), tm.t(), keys.t(c), ALU.mult, tag=f"{tag}_z{c}")
        for b in range(NB):
            p.scan_add("v", A.t(c, b * SL, (b + 1) * SL), zt.t(0, 128, b * SL, (b + 1) * SL),
                       tag=f"{tag}_sc2_{c}{b}")
            p.copy("v", T2.t(0, 128, b * CH + c, b * CH + c + 1),
                   A.t(c, (b + 1) * SL - 1, (b + 1) * SL), tag=f"{tag}_t2_{c}{b}")
            mk.pump()
            mk.warm()
        zt.free()
    katt.free()
    keys.free()
    for tmq in tmp2:
        tmq.free()

    carry2 = carry_exchange(mk, T2, f"{tag}_x2")
    T2.free()

    xnew = slab.falloc(CH, NCOL, F32, f"xnew_{tag}")
    values = slab.falloc(CH, NCOL, BF16, f"vals2_{tag}")
    for c in range(CH):
        p.add("y", (lambda e, c=c, o=values.t(c).ap: e.dma_start(out=o, in_=vsp[:, c, :])),
              reads=[v_spill.box((0, 1, c, c + 1))], writes=[values.t(c)], kind="d",
              tag=f"{tag}_vld{c}")
        xold = slab.alloc(NCOL, F32, f"xoldA{c}")
        restream_x_chunk(mk, st, c, xold, tag)
        for b in range(NB):
            p.ts("v", A.t(c, b * SL, (b + 1) * SL), A.t(c, b * SL, (b + 1) * SL),
                 carry2.t(0, 128, b * CH + c, b * CH + c + 1), ALU.add, tag=f"{tag}_Pc{c}{b}")
        p.tt("v", A.t(c), A.t(c), values.t(c), ALU.mult, tag=f"{tag}_pv{c}")
        p.tt("v", xnew.t(c), A.t(c), xold.t(), ALU.add, tag=f"{tag}_out{c}")
        mk.pump()
        mk.warm()
        xold.free()
    A.free()
    values.free()
    carry1.free()
    carry2.free()
    for b in (gamma_col, beta_col, bk_col, bka_col, bv_col, gbf, bbf):
        b.free()
    st.x = xnew


def carry_exchange(mk, Tcol, tag):
    p, slab = mk.p, mk.slab
    cin = mk.dinternal(f"cc{mk._cc_ctr}_in", [128 * 16], F32)
    cout = mk.dinternal(f"cc{mk._cc_ctr}_out", [NCORES * 128 * 16], F32,
                        shared_for="AllGather")
    mk._cc_ctr += 1
    cin_ap = cin.handle.ap().rearrange("(p j) -> p j", p=128)
    p.add("g", (lambda e, o=cin_ap, a=Tcol.t().ap: e.dma_start(out=o, in_=a)),
          reads=[Tcol.t()], writes=[cin.t()], kind="d", tag=f"{tag}_st")
    p.coll("AllGather", ALU.bypass, cin.t(), cout.t(), mk.rg, tag=f"{tag}_ag")
    tall = slab.alloc(NCORES * 16 * 16, F32, f"tall_{tag}")
    tall_tap = TAP(tall.slab.handle.ap()[0:8, tall.boff:tall.boff + 2048 * 4].bitcast(F32),
                   tall.slab.name, (0, 8, tall.boff, tall.boff + 2048 * 4))
    cout_ap = cout.handle.ap().rearrange("(r n) -> r n", r=NCORES)
    p.add("g", (lambda e, o=tall_tap.ap, a=cout_ap: e.dma_start(out=o, in_=a)),
          reads=[cout.t()], writes=[tall_tap], kind="d", tag=f"{tag}_ld")
    carry = slab.alloc(16, F32, f"carry_{tag}")
    ps = mk.psF[3]
    tall3 = tall_tap.ap.rearrange("r (pp j) -> r j pp", j=16)
    for j in range(16):
        p.mmgroup([(TAP(tall3[:, j, :], tall_tap.name, tall_tap.box), mk.pfx_tap)],
                  ps.t(0, 128, j, j + 1), tag=f"{tag}_car{j}")
    p.copy("v", carry.t(), ps.t(0, 128, 0, 16), tag=f"{tag}_carev")
    tall.free()
    return carry



def gmax_exchange(mk, col_tap, k, tag):
    p, slab = mk.p, mk.slab
    cin = mk.dinternal(f"cc{mk._cc_ctr}_in", [128 * k], F32)
    cout = mk.dinternal(f"cc{mk._cc_ctr}_out", [NCORES * 128 * k], F32,
                        shared_for="AllGather")
    mk._cc_ctr += 1
    cin_ap = cin.handle.ap().rearrange("(p j) -> p j", p=128)
    p.add("g", (lambda e, o=cin_ap, a=col_tap.ap: e.dma_start(out=o, in_=a)),
          reads=[col_tap], writes=[cin.t()], kind="d", tag=f"{tag}_st")
    p.coll("AllGather", ALU.bypass, cin.t(), cout.t(), mk.rg, tag=f"{tag}_ag")
    row = slab.alloc(NCORES * 128 * k, F32, f"gmr_{tag}")
    p.add("g", (lambda e, o=row.t(0, 1).ap, a=cout.handle.ap()[None, :]:
                e.dma_start(out=o, in_=a)),
          reads=[cout.t()], writes=[row.t(0, 1)], kind="d", tag=f"{tag}_ld")
    out = slab.alloc(k, F32, f"gmx_{tag}")
    r3 = row.t(0, 1).ap.rearrange("o (r p j) -> o j (r p)", r=NCORES, j=k)
    p.add("v", (lambda e, o=out.t(0, 1).ap, a=r3:
                e.tensor_reduce(out=o, in_=a, axis=AX, op=ALU.max)),
          reads=[row.t(0, 1)], writes=[out.t(0, 1)], tag=f"{tag}_red")
    row.free()
    return out


def scale_cols(mk, gmax_row, k, qmax, tag):
    p, slab = mk.p, mk.slab
    srow = slab.alloc(2 * k, F32, f"srow_{tag}")
    p.ts("v", srow.t(0, 1, 0, k), gmax_row.t(0, 1), 1.0 / qmax, ALU.mult, 1e-12, ALU.max,
         tag=f"{tag}_s")
    p.add("v", (lambda e, o=srow.t(0, 1, k, 2 * k).ap, a=srow.t(0, 1, 0, k).ap:
                e.reciprocal(out=o, in_=a)),
          reads=[srow.t(0, 1, 0, k)], writes=[srow.t(0, 1, k, 2 * k)], tag=f"{tag}_rs")
    ps = mk.psF[4].t(0, 128, 0, 2 * k)
    p.mmgroup([(mk.ones_row_f32(), srow.t(0, 1))], ps, tag=f"{tag}_bc")
    scol = slab.alloc(2 * k, F32, f"scol_{tag}")
    p.copy("v", scol.t(), ps, tag=f"{tag}_scev")
    srow.free()
    return scol


RK = 32768.0

def roundq(mk, out, in_, rs, s, tag, ibuf=None):
    p, slab = mk.p, mk.slab
    w = in_.box[3] - in_.box[2]
    own = ibuf is None
    if own:
        ibuf = slab.alloc((in_.box[3] - in_.box[2]) // 4 + 8, I32, f"ri_{tag}")
    it = ibuf.t(0, 128, 0, (in_.box[3] - in_.box[2]) // DT_SIZE[F32])
    p.ts("v", it, in_, rs, ALU.mult, RK + 0.5, ALU.add, tag=f"{tag}_ri")
    p.ts("v", out, it, -RK, ALU.add, s, ALU.mult, tag=f"{tag}_rq")
    if own:
        ibuf.free()


def sub_l1_prep_gen(mk, st, WD):
    p, slab = mk.p, mk.slab
    mk.dq = {}
    bufs = {
        "up": (WD["l1f_w_up"], 3072, 1024),
        "gate": (WD["l1f_w_gate"], 3072, 1024),
        "down": (WD["l1f_w_down"], 1024, 3072),
    }
    W = 512
    NBW = W // 32
    wtile = [slab.alloc(W, F32, f"dqw{i}") for i in range(2)]
    t1 = [slab.alloc(W, F32, "dqt0")] * 2
    fb = [slab.alloc(W, I32, "dqf0")] * 2
    ob = [slab.alloc(W, BF16, "dqo0")] * 2
    mxb = slab.alloc(2 * NBW, F32, "dqmx")
    sb = slab.alloc(2 * NBW, F32, "dqs")
    rb = slab.alloc(2 * NBW, F32, "dqr")
    mk._dq_transients = wtile + [t1[0], fb[0], ob[0], mxb, sb, rb]
    for nm, (dram, O, I) in bufs.items():
        dq = mk.dinternal(f"dq_{nm}", [O, I], BF16)
        mk.dq[nm] = dq
        src_ap = dram.handle.ap().rearrange("(c p) n -> p c n", p=128)
        dst_ap = dq.handle.ap().rearrange("(c p) n -> p c n", p=128)
        idx = 0
        for oc in range(O // 128):
            for isl in range(I // W):
                j = idx % 2
                idx += 1
                wt, tt1, tfb, tob = wtile[j], t1[j], fb[j], ob[j]
                mxs = mxb.t(0, 128, j * NBW, (j + 1) * NBW)
                sbs = sb.t(0, 128, j * NBW, (j + 1) * NBW)
                rbs = rb.t(0, 128, j * NBW, (j + 1) * NBW)
                i0 = isl * W
                p.add("y", (lambda e, o=wt.t().ap, a=src_ap[:, oc, i0:i0 + W]:
                            e.dma_start(out=o, in_=a)),
                      reads=[dram.t()], writes=[wt.t()], kind="d",
                      tag=f"dq_{nm}_ld{oc}_{isl}")
                w3 = wt.t().ap.rearrange("p (g b) -> p g b", b=32)
                p.add("v", (lambda e, o=mxs.ap, a=w3:
                            e.tensor_reduce(out=o, in_=a, axis=AX, op=ALU.max,
                                            apply_absolute_value=True)),
                      reads=[wt.t()], writes=[mxs], tag=f"dq_{nm}_mx{oc}_{isl}")
                p.ts("g", sbs, mxs, 1.0 / 7.0, ALU.mult, 1e-12, ALU.max,
                     tag=f"dq_{nm}_s{oc}_{isl}")
                p.add("v", (lambda e, o=rbs.ap, a=sbs.ap: e.reciprocal(out=o, in_=a)),
                      reads=[sbs], writes=[rbs], tag=f"dq_{nm}_r{oc}_{isl}")
                r3 = rbs.ap.unsqueeze(2).broadcast_to([128, NBW, 32])
                s3 = sbs.ap.unsqueeze(2).broadcast_to([128, NBW, 32])
                t13 = tt1.t().ap.rearrange("p (g b) -> p g b", b=32)
                p.add("g", (lambda e, o=t13, a=w3, b=r3:
                            e.tensor_tensor(out=o, in0=a, in1=b, op=ALU.mult)),
                      reads=[wt.t(), rbs], writes=[tt1.t()], tag=f"dq_{nm}_t{oc}_{isl}")
                p.ts("g", tt1.t(), tt1.t(), 32768.5, ALU.add, tag=f"dq_{nm}_z{oc}_{isl}")
                p.copy("v", tfb.t(), tt1.t(), tag=f"dq_{nm}_i{oc}_{isl}")
                p.tt("v", wt.t(), tfb.t(), tt1.t(), ALU.is_gt, tag=f"dq_{nm}_g{oc}_{isl}")
                p.tt("v", tt1.t(), tfb.t(), wt.t(), ALU.subtract, tag=f"dq_{nm}_s{oc}_{isl}")
                p.ts("g", tt1.t(), tt1.t(), -32768.0, ALU.add, tag=f"dq_{nm}_q{oc}_{isl}")
                ob3 = tob.t().ap.rearrange("p (g b) -> p g b", b=32)
                p.add("g", (lambda e, o=ob3, a=t13, b=s3:
                            e.tensor_tensor(out=o, in0=a, in1=b, op=ALU.mult)),
                      reads=[tt1.t(), sbs], writes=[tob.t()], tag=f"dq_{nm}_o{oc}_{isl}")
                p.add("g", (lambda e, o=dst_ap[:, oc, i0:i0 + W], a=tob.t().ap:
                            e.dma_start(out=o, in_=a)),
                      reads=[tob.t()], writes=[dq.box((0, 1, oc * 8 + isl, oc * 8 + isl + 1))],
                      kind="d", tag=f"dq_{nm}_st{oc}_{isl}")
                yield


def load_wT_slice(mk, dq, rows0, rows1, kchunk, tag):
    p, slab = mk.p, mk.slab
    w = rows1 - rows0
    b = slab.alloc(w, BF16, f"wt_{tag}")
    srcap = dq.handle.ap()[rows0:rows1, kchunk * 128:(kchunk + 1) * 128]
    p.add("y", (lambda e, o=b.t().ap, a=srcap: e.dma_start(out=o, in_=a, transpose=True)),
          reads=[dq.t()], writes=[b.t()], kind="d", tag=f"ldT_{tag}")
    return b


def sub_l1_ffn(mk, st, WD, first):
    p, slab = mk.p, mk.slab
    tag = "q1" if first else "q2"
    ln = ln_stats(mk, st, tag, want_nmr_bc=True)
    ln["xbf"].free()
    nmr_bc = ln["nmr_bc"]
    xn = slab.falloc(CH, NCOL, F32, f"xn_{tag}")
    mxc = slab.alloc(CH, F32, f"mxc_{tag}")
    for c in range(CH):
        p.stt("v", xn.t(c), st.x.t(c), 1.0, ln["R_bc"].t(), ALU.mult, ALU.mult,
              tag=f"{tag}_xr{c}")
        p.tt("g", xn.t(c), xn.t(c), nmr_bc.t(), ALU.add, tag=f"{tag}_xn{c}")
        p.red("v", mxc.t(0, 128, c, c + 1), xn.t(c), ALU.max, absval=True,
              tag=f"{tag}_mx{c}")
        mk.warm()
    spill_x(mk, st, tag)
    st.x.free()
    ln["R_bc"].free()
    nmr_bc.free()
    ln["nmr_row"].free()
    mxall = slab.alloc(1, F32, f"mxall_{tag}")
    p.red("v", mxall.t(), mxc.t(), ALU.max, tag=f"{tag}_mxall")
    gmx = gmax_exchange(mk, mxall.t(), 1, f"{tag}_b0")
    scol_x = scale_cols(mk, gmx, 1, 32767.0, f"{tag}_sx")
    gmx.free()
    mxall.free()
    mxc.free()
    qx = slab.falloc(CH, NCOL, BF16, f"qx_{tag}")
    for c in range(CH):
        qi = slab.alloc(NCOL, I32, "qi")
        p.ts("v", qi.t(), xn.t(c), scol_x.t(0, 128, 1, 2), ALU.mult, RK + 0.5, ALU.add,
             tag=f"{tag}_qa{c}")
        p.ts("v", qx.t(c), qi.t(), -RK, ALU.add, scol_x.t(0, 128, 0, 1), ALU.mult,
             tag=f"{tag}_qx{c}")
        mk.warm()
        qi.free()
        xn.bufs[c].free()
    xn.bufs = []
    scol_x.free()

    E = 3072
    gate = slab.falloc(24, NCOL, BF16, f"gate_{tag}")
    up_spill = mk.dinternal(f"up_spill_{tag}", [E, NCOL], BF16)
    ups_ap = up_spill.handle.ap().rearrange("(c p) n -> p c n", p=128)
    upmx = slab.alloc(96, F32, f"upmx_{tag}")
    gmxc = slab.alloc(96, F32, f"gmxc_{tag}")
    uptmp = [slab.alloc(NCOL, BF16, f"uptmp{i}") for i in range(2)]
    for nm, mxt in (("up", upmx), ("gate", gmxc)):
        dq = mk.dq[nm]
        for mq in range(4):
            r0, r1 = mq * 768, (mq + 1) * 768
            wts = [load_wT_slice(mk, dq, r0, r1, c, f"{tag}_{nm}{mq}_{c}") for c in range(CH)]
            for mloc in range(6):
                mj = mq * 6 + mloc
                dstt = gate.t if nm == "gate" else uptmp[mj % 2].t
                dst = (lambda c0, c1: gate.t(mj, c0, c1)) if nm == "gate" else                       (lambda c0, c1: uptmp[mj % 2].t(0, 128, c0, c1))
                for nt, (c0, c1) in enumerate(NT512):
                    pairs = [(wts[c].t(0, 128, mloc * 128, (mloc + 1) * 128),
                              qx.t(c, c0, c1)) for c in range(CH)]
                    ps = mk.psF[(mj * 4 + nt) % 5].t()
                    p.mmgroup(pairs, ps, tag=f"{tag}_{nm}mm{mj}_{nt}")
                    p.copy("s", dst(c0, c1), ps, tag=f"{tag}_{nm}ev{mj}_{nt}")
                    p.red("v", mxt.t(0, 128, mj * 4 + nt, mj * 4 + nt + 1),
                          dst(c0, c1), ALU.max, absval=True,
                          tag=f"{tag}_{nm}mx{mj}_{nt}")
                if nm == "up":
                    p.add("y", (lambda e, mj=mj, a=uptmp[mj % 2].t().ap:
                                e.dma_start(out=ups_ap[:, mj, :], in_=a)),
                          reads=[uptmp[mj % 2].t()],
                          writes=[up_spill.box((0, 1, mj, mj + 1))], kind="d",
                          tag=f"{tag}_upsp{mj}")
            for w in wts:
                w.free()
    for b in uptmp:
        b.free()
    qx.free()
    mx2 = slab.alloc(2, F32, f"mx2_{tag}")
    p.red("v", mx2.t(0, 128, 0, 1), upmx.t(), ALU.max, tag=f"{tag}_mxu")
    p.red("v", mx2.t(0, 128, 1, 2), gmxc.t(), ALU.max, tag=f"{tag}_mxg")
    upmx.free()
    gmxc.free()
    gmx1 = gmax_exchange(mk, mx2.t(), 2, f"{tag}_b1")
    scol1 = scale_cols(mk, gmx1, 2, 32767.0, f"{tag}_s1")
    gmx1.free()
    mx2.free()
    s_up = lambda: scol1.t(0, 128, 0, 1)
    s_g = lambda: scol1.t(0, 128, 1, 2)
    rs_up = lambda: scol1.t(0, 128, 2, 3)
    rs_g = lambda: scol1.t(0, 128, 3, 4)
    csg = slab.alloc(1, F32, f"csg_{tag}")
    p.ts("v", csg.t(), s_g(), 1.0 / 65536.0, ALU.mult, tag=f"{tag}_csg")
    m32sg = slab.alloc(1, F32, f"m32sg_{tag}")
    p.ts("v", m32sg.t(), s_g(), -RK, ALU.mult, tag=f"{tag}_m32sg")

    tmxc = slab.alloc(24, F32, f"tmx_{tag}")
    for mj0 in range(0, 24, 2):
        prs = []
        for mj in (mj0, mj0 + 1):
            bi1 = slab.alloc(NCOL, I32, "bi1")
            bi2 = slab.alloc(NCOL, I32, "bi2")
            bd1 = slab.alloc(NCOL, F32, "bd1")
            bd2 = slab.alloc(NCOL, F32, "bd2")
            prs.append((mj, bi1, bi2, bd1, bd2))
        for mj, bi1, bi2, bd1, bd2 in prs:
            p.ts("v", bi1.t(), gate.t(mj), rs_g(), ALU.mult, RK + 0.5, ALU.add,
                 tag=f"{tag}_Ba{mj}")
        for mj, bi1, bi2, bd1, bd2 in prs:
            p.act(bd2.t(), bi1.t(), AF.Sigmoid, scale=s_g(), bias=m32sg.t(),
                  tag=f"{tag}_Bsig{mj}")
        for mj, bi1, bi2, bd1, bd2 in prs:
            p.act(bi2.t(), bd2.t(), AF.Copy, scale=65536.0, bias=RK + 0.5,
                  tag=f"{tag}_Bq{mj}")
        for mj, bi1, bi2, bd1, bd2 in prs:
            p.act(bd1.t(), bi1.t(), AF.Copy, bias=-RK, tag=f"{tag}_Bd1{mj}")
        for mj, bi1, bi2, bd1, bd2 in prs:
            p.act(bd2.t(), bi2.t(), AF.Copy, bias=-RK, tag=f"{tag}_Bd2{mj}")
        for mj, bi1, bi2, bd1, bd2 in prs:
            p.tt("v", bd1.t(), bd1.t(), bd2.t(), ALU.mult, tag=f"{tag}_Bt{mj}")
        for mj, bi1, bi2, bd1, bd2 in prs:
            p.red("v", tmxc.t(0, 128, mj, mj + 1), bd1.t(), ALU.max, absval=True,
                  tag=f"{tag}_Bmx{mj}")
        for mj, bi1, bi2, bd1, bd2 in prs:
            p.ts("v", gate.t(mj), bd1.t(), csg.t(), ALU.mult, tag=f"{tag}_Btb{mj}")
        mk.warm()
        mk.warm()
        for mj, bi1, bi2, bd1, bd2 in prs:
            for b in (bi1, bi2, bd1, bd2):
                b.free()
    tmax = slab.alloc(1, F32, f"tmax_{tag}")
    p.red("v", tmax.t(), tmxc.t(), ALU.max, tag=f"{tag}_Btmax")
    p.tt("v", tmax.t(), tmax.t(), csg.t(0, 128, 0, 1), ALU.mult, tag=f"{tag}_Btsc")
    gmx2 = gmax_exchange(mk, tmax.t(), 1, f"{tag}_b2")
    scol2 = scale_cols(mk, gmx2, 1, 32767.0, f"{tag}_s2")
    gmx2.free()
    tmax.free()
    tmxc.free()
    s_t = lambda: scol2.t(0, 128, 0, 1)
    rs_t = lambda: scol2.t(0, 128, 1, 2)
    cstu = slab.alloc(1, F32, f"cstu_{tag}")
    p.tt("v", cstu.t(), s_t(), s_up(), ALU.mult, tag=f"{tag}_cstu")

    umxc = slab.alloc(24, F32, f"umx_{tag}")
    upld = [slab.alloc(NCOL, BF16, f"upld{i}") for i in range(4)]
    for mj0 in range(0, 24, 2):
        prs = []
        for mj in (mj0, mj0 + 1):
            upb = upld[mj % 4]
            p.add("y", (lambda e, mj=mj, o=upb.t().ap: e.dma_start(out=o, in_=ups_ap[:, mj, :])),
                  reads=[up_spill.box((0, 1, mj, mj + 1))], writes=[upb.t()], kind="d",
                  tag=f"{tag}_upld{mj}")
            bi1 = slab.alloc(NCOL, I32, "ci1")
            bi2 = slab.alloc(NCOL, I32, "ci2")
            bd1 = slab.alloc(NCOL, F32, "cd1")
            bd2 = slab.alloc(NCOL, F32, "cd2")
            prs.append((mj, upb, bi1, bi2, bd1, bd2))
        for mj, upb, bi1, bi2, bd1, bd2 in prs:
            p.ts("v", bi1.t(), gate.t(mj), rs_t(), ALU.mult, RK + 0.5, ALU.add,
                 tag=f"{tag}_Ca{mj}")
        for mj, upb, bi1, bi2, bd1, bd2 in prs:
            p.ts("v", bi2.t(), upb.t(), rs_up(), ALU.mult, RK + 0.5, ALU.add,
                 tag=f"{tag}_Cb{mj}")
        for mj, upb, bi1, bi2, bd1, bd2 in prs:
            p.act(bd1.t(), bi1.t(), AF.Copy, bias=-RK, tag=f"{tag}_Cd1{mj}")
        for mj, upb, bi1, bi2, bd1, bd2 in prs:
            p.act(bd2.t(), bi2.t(), AF.Copy, bias=-RK, tag=f"{tag}_Cd2{mj}")
        for mj, upb, bi1, bi2, bd1, bd2 in prs:
            p.tt("v", bd1.t(), bd1.t(), bd2.t(), ALU.mult, tag=f"{tag}_Cu{mj}")
        for mj, upb, bi1, bi2, bd1, bd2 in prs:
            p.red("v", umxc.t(0, 128, mj, mj + 1), bd1.t(), ALU.max, absval=True,
                  tag=f"{tag}_Cmx{mj}")
        for mj, upb, bi1, bi2, bd1, bd2 in prs:
            p.ts("v", gate.t(mj), bd1.t(), cstu.t(), ALU.mult, tag=f"{tag}_Cub{mj}")
        mk.warm()
        mk.warm()
        for mj, upb, bi1, bi2, bd1, bd2 in prs:
            for b in (bi1, bi2, bd1, bd2):
                b.free()
    for b in upld:
        b.free()
    umax = slab.alloc(1, F32, f"umax_{tag}")
    p.red("v", umax.t(), umxc.t(), ALU.max, tag=f"{tag}_Cumax")
    p.tt("v", umax.t(), umax.t(), cstu.t(0, 128, 0, 1), ALU.mult, tag=f"{tag}_Cusc")
    gmx3 = gmax_exchange(mk, umax.t(), 1, f"{tag}_b3")
    scol3 = scale_cols(mk, gmx3, 1, 32767.0, f"{tag}_s3")
    gmx3.free()
    umax.free()
    umxc.free()
    s_u = lambda: scol3.t(0, 128, 0, 1)
    rs_u = lambda: scol3.t(0, 128, 1, 2)

    for mj in range(24):
        bi1 = slab.alloc(NCOL, I32, "di1")
        p.ts("v", bi1.t(), gate.t(mj), rs_u(), ALU.mult, RK + 0.5, ALU.add,
             tag=f"{tag}_Da{mj}")
        p.ts("v", gate.t(mj), bi1.t(), -RK, ALU.add, s_u(), ALU.mult,
             tag=f"{tag}_Dq{mj}")
        mk.warm()
        bi1.free()
    m32sg.free()

    xnew = slab.falloc(CH, NCOL, F32, f"xnew_{tag}")
    dqd = mk.dq["down"]
    for half in range(4):
        o0 = half * 256
        wts = [load_wT_slice(mk, dqd, o0, o0 + 256, c, f"{tag}_dn{half}_{c}")
               for c in range(24)]
        for mloc in range(2):
            mj = half * 2 + mloc
            xold = slab.alloc(NCOL, F32, f"xoldq{mj}")
            restream_x_chunk(mk, st, mj, xold, tag)
            for nt, (c0, c1) in enumerate(NT512):
                pairs = [(wts[c].t(0, 128, mloc * 128, (mloc + 1) * 128),
                          gate.t(c, c0, c1)) for c in range(24)]
                ps = mk.psF[(mj * 4 + nt) % 5].t()
                p.mmgroup(pairs, ps, tag=f"{tag}_dn{mj}_{nt}")
                p.stt("v", xnew.t(mj, c0, c1), ps, 0.5, xold.t(0, 128, c0, c1),
                      ALU.mult, ALU.add, tag=f"{tag}_dres{mj}_{nt}")
            xold.free()
        for w in wts:
            w.free()
    gate.free()
    for b in (scol1, scol2, scol3, csg, cstu):
        b.free()
    st.x = xnew


class _TruncView:
    def __init__(self, fb, c):
        self.fb = fb
        self.c = c

    def t(self):
        return self.fb.t(self.c)


def bias_row_bf(mk, wcb, E, lnb_bf, bvec_dram, tag):
    p, slab = mk.p, mk.slab
    row = slab.alloc(E, F32, f"br_{tag}")
    for s in range(E // 512):
        ps = mk.psF[2].t(0, 1, 0, 512)
        p.mmgroup([(lnb_bf.t(0, 128, c, c + 1), wcb.t(c, s * 512, (s + 1) * 512))
                   for c in range(CH)], ps, tag=f"{tag}_br{s}")
        p.copy("v", row.t(0, 1, s * 512, (s + 1) * 512), ps, tag=f"{tag}_brev{s}")
    bro = slab.alloc(E, F32, f"brb_{tag}")
    p.add("y", (lambda e, o=bro.t(0, 1).ap, a=bvec_dram.handle.ap()[None, :]:
                e.dma_start(out=o, in_=a)),
          reads=[bvec_dram.t()], writes=[bro.t(0, 1)], kind="d", tag=f"{tag}_brld")
    out = slab.alloc(E, BF16, f"brf_{tag}")
    p.tt("v", out.t(0, 1), row.t(0, 1), bro.t(0, 1), ALU.add, tag=f"{tag}_bradd")
    row.free()
    bro.free()
    return out


def proj_tm(mk, src_bf, wcb, bias_row, nmr_row, cs_row, evac, tag,
            tchunks=16, srcsl=None):
    p = mk.p
    for tc in range(tchunks):
        t0 = tc * 128
        for nt in range(2):
            e0, e1 = nt * 512, (nt + 1) * 512
            pairs = [(src_bf.t(c, t0, t0 + 128), wcb.t(c, e0, e1)) for c in range(CH)]
            pairs.append((mk.ones_row_bf(0, 128), bias_row.t(0, 1, e0, e1)))
            if cs_row is not None:
                pairs.append((nmr_row.t(0, 1, t0, t0 + 128), cs_row.t(0, 1, e0, e1)))
            ps = mk.psF[(tc * 2 + nt) % 5].t()
            p.mmgroup(pairs, ps, tag=f"{tag}_tm{tc}_{nt}")
            evac(tc, e0, e1, ps)


def phi_evac_fm(mk, dst, tag):
    p, slab = mk.p, mk.slab

    def ev(mj, c0, c1, ps, bias):
        tmn = slab.alloc(512, F32, "phn")
        tmx = slab.alloc(512, F32, "phx")
        p.ts("v", tmn.t(), ps, bias, ALU.add, 0.0, ALU.min, tag=f"{tag}_mn{mj}")
        p.ts("v", tmx.t(), ps, bias, ALU.add, 0.0, ALU.max, tag=f"{tag}_mx{mj}")
        p.act(tmn.t(), tmn.t(), AF.Exp, tag=f"{tag}_ex{mj}")
        p.tt("v", dst.t(mj, c0, c1), tmn.t(), tmx.t(), ALU.add, tag=f"{tag}_ph{mj}")
        tmn.free()
        tmx.free()
    return ev


def phi_evac_tm(mk, dst, tag):
    p, slab = mk.p, mk.slab

    def ev(tc, e0, e1, ps):
        tmn = slab.alloc(512, F32, "phn")
        tmx = slab.alloc(512, F32, "phx")
        p.ts("v", tmn.t(), ps, 0.0, ALU.min, tag=f"{tag}_mn{tc}")
        p.ts("v", tmx.t(), ps, 0.0, ALU.max, tag=f"{tag}_mx{tc}")
        p.act(tmn.t(), tmn.t(), AF.Exp, tag=f"{tag}_ex{tc}")
        p.tt("v", dst.t(tc, e0, e1), tmn.t(), tmx.t(), ALU.add, tag=f"{tag}_ph{tc}")
        tmn.free()
        tmx.free()
    return ev


def sub_l1_attn(mk, st, WD):
    p, slab = mk.p, mk.slab
    tag = "w5"
    ln = ln_stats(mk, st, tag, want_nmr_bc=True)
    ln["xbf"].free()
    lng_col = mk.load_col(WD["l1a_ln_g"], tag=f"{tag}_lng")
    lnb_col = mk.load_col(WD["l1a_ln_b"], tag=f"{tag}_lnb")
    lng_bf = slab.alloc(CH, BF16, f"{tag}_lngbf")
    lnb_bf = slab.alloc(CH, BF16, f"{tag}_lnbbf")
    p.copy("v", lng_bf.t(), lng_col.t(), tag=f"{tag}_lngc")
    p.copy("v", lnb_bf.t(), lnb_col.t(), tag=f"{tag}_lnbc")
    xg = slab.falloc(CH, NCOL, BF16, f"xg_{tag}")
    for c in range(CH):
        p.stt("v", xg.t(c), st.x.t(c), lng_col.t(0, 128, c, c + 1), ln["R_bc"].t(),
              ALU.mult, ALU.mult, tag=f"{tag}_xg{c}")
    mk.tap("w5_xg", xg, CH, NCOL, BF16)
    xln_spill = mk.dinternal("xln_spill", [D, NCOL], F32)
    xlsp = xln_spill.handle.ap().rearrange("(c p) n -> p c n", p=128)
    xlt = slab.alloc(NCOL, F32, f"xlt_{tag}")
    for c in range(CH):
        p.stt("v", xlt.t(), st.x.t(c), 1.0, ln["R_bc"].t(), ALU.mult, ALU.mult,
              tag=f"{tag}_xl{c}")
        p.tt("g", xlt.t(), xlt.t(), ln["nmr_bc"].t(), ALU.add, tag=f"{tag}_xlb{c}")
        p.add("y", (lambda e, c=c, a=xlt.t().ap: e.dma_start(out=xlsp[:, c, :], in_=a)),
              reads=[xlt.t()], writes=[xln_spill.box((0, 1, c, c + 1))], kind="d",
              tag=f"{tag}_xlsp{c}")
    xlt.free()
    ln["nmr_bc"].free()
    spill_x(mk, st, tag)
    st.x.free()
    ln["R_bc"].free()
    nmr = ln["nmr_row"]

    HW = CH * 128
    hstage = slab.alloc(HW, BF16, f"hst_{tag}")
    for c in range(CH):
        for b2 in range(NB):
            for j in range(2):
                csrc = b2 * SL + (0 if j == 0 else SL - HALO)
                p.copy("v", hstage.t(0, 128, c * 128 + b2 * 64 + j * 32,
                                     c * 128 + b2 * 64 + j * 32 + 32),
                       xg.t(c, csrc, csrc + HALO), tag=f"{tag}_hs{c}{b2}{j}")
    nstage = slab.alloc(128, BF16, f"nst_{tag}")
    for b2 in range(NB):
        for j in range(2):
            csrc = b2 * SL + (0 if j == 0 else SL - HALO)
            p.copy("v", nstage.t(0, 1, b2 * 64 + j * 32, b2 * 64 + j * 32 + 32),
                   nmr.t(0, 1, csrc, csrc + HALO), tag=f"{tag}_hsn{b2}{j}")
    TOTW = 128 * HW + 128
    hcin = mk.dinternal(f"cc{mk._cc_ctr}_in", [TOTW], BF16)
    hcout = mk.dinternal(f"cc{mk._cc_ctr}_out", [NCORES * TOTW], BF16,
                         shared_for="AllGather")
    mk._cc_ctr += 1
    p.add("g", (lambda e, o=hcin.handle.ap()[0:128 * HW].rearrange("(p n) -> p n", p=128),
                a=hstage.t().ap: e.dma_start(out=o, in_=a)),
          reads=[hstage.t()], writes=[hcin.box((0, 1, 0, 1))], kind="d", tag=f"{tag}_hciw")
    p.add("g", (lambda e, o=hcin.handle.ap()[None, 128 * HW:TOTW],
                a=nstage.t(0, 1).ap: e.dma_start(out=o, in_=a)),
          reads=[nstage.t(0, 1)], writes=[hcin.box((0, 1, 1, 2))], kind="d",
          tag=f"{tag}_hcin")
    p.coll("AllGather", ALU.bypass, hcin.t(), hcout.t(), mk.rg, tag=f"{tag}_hag")
    hstage.free()
    nstage.free()
    hh = slab.alloc(CH * 128, BF16, f"hh_{tag}")
    nmrh = slab.alloc(128, BF16, f"nmrh_{tag}")
    hco = hcout.handle.ap().rearrange("(r q) -> r q", r=NCORES)

    def _halo_dma(e, side):
        with e.register(f"rk{side}") as rk:
            e.reg_load(rk, mk.rank_tap(side).ap)
            ro = e.snap(rk)
            src = hco[bass.ds(ro, 1), 0:128 * HW].rearrange(
                "r (p c b2 j i) -> r p c b2 j i", p=128, c=CH, b2=NB, j=2)[0, :, :, :, 1 - side, :]
            dst = hh.t().ap.rearrange("p (c b2 j i) -> p c b2 j i", c=CH, b2=NB, j=2)[
                :, :, :, side, :]
            return e.dma_start(out=dst, in_=src)

    for side in range(2):
        sd = side
        p.add("g", (lambda e, sd=sd: _halo_dma(e, sd)),
              reads=[hcout.t(), mk.rank_tap(sd)], writes=[hh.t()], kind="d",
              tag=f"{tag}_hld{sd}")

    def _halo_nmr_dma(e, side):
        with e.register(f"rn{side}") as rk:
            e.reg_load(rk, mk.rank_tap(side).ap)
            ro = e.snap(rk)
            src = hco[bass.ds(ro, 1), 128 * HW:TOTW].rearrange(
                "r (b2 j i) -> r b2 j i", b2=NB, j=2)[:, :, 1 - side, :]
            dst = nmrh.t(0, 1).ap.rearrange("p (b2 j i) -> p b2 j i", b2=NB, j=2)[
                :, :, side, :]
            return e.dma_start(out=dst, in_=src)

    for side in range(2):
        sd = side
        p.add("g", (lambda e, sd=sd: _halo_nmr_dma(e, sd)),
              reads=[hcout.t(), mk.rank_tap(sd)], writes=[nmrh.t(0, 1)], kind="d",
              tag=f"{tag}_hldn{sd}")


    if W5_TRUNC <= 1:
        xnew = slab.falloc(CH, NCOL, F32, "xtr")
        for c in range(CH):
            restream_x_chunk(mk, st, c, _TruncView(xnew, c), "tr")
        st.x = xnew
        return
    w_klin = mk.load_weight_bf16(WD["l1a_wk_lin"], D, tag=f"{tag}_wkl")
    w_vlin = mk.load_weight_bf16(WD["l1a_wv_lin"], D, tag=f"{tag}_wvl")
    cs_kl = proj_cs_row(mk, w_klin, D, gcol_bf=lng_bf, tag=f"{tag}_cskl")
    br_kl = bias_row_bf(mk, w_klin, D, lnb_bf, WD["l1a_bk_lin"], f"{tag}_brkl")
    phk = slab.falloc(16, D, BF16, f"phk_{tag}")
    proj_tm(mk, xg, w_klin, br_kl, nmr, cs_kl, phi_evac_tm(mk, phk, f"{tag}_phk"),
            f"{tag}_kl")
    w_klin.free()
    cs_kl.free()
    br_kl.free()
    cs_vl = proj_cs_row(mk, w_vlin, D, gcol_bf=lng_bf, tag=f"{tag}_csvl")
    br_vl = bias_row_bf(mk, w_vlin, D, lnb_bf, WD["l1a_bv_lin"], f"{tag}_brvl")
    vtm = slab.falloc(16, D, BF16, f"vtm_{tag}")

    def ev_v(tc, e0, e1, ps):
        p.copy("s", vtm.t(tc, e0, e1), ps, tag=f"{tag}_evv{tc}_{e0}")
    proj_tm(mk, xg, w_vlin, br_vl, nmr, cs_vl, ev_v, f"{tag}_vl")
    w_vlin.free()
    cs_vl.free()
    br_vl.free()
    kv_in = mk.dinternal(f"cc{mk._cc_ctr}_in", [NB * D * D + NB * D], F32)
    kv_out = mk.dinternal(f"cc{mk._cc_ctr}_out", [NB * D * D + NB * D], F32,
                          shared_for="AllReduce")
    mk._cc_ctr += 1
    kvr = kv_in.handle.ap()[0:NB * D * D].rearrange("(b dc eh p n) -> b dc eh p n",
                                                    b=NB, dc=CH, eh=2, p=128)
    kstage = slab.alloc(512, F32, f"kvst_{tag}")
    for b2 in range(NB):
        for dc in range(CH):
            for eh in range(2):
                ps = mk.psF[(dc * 2 + eh) % 4].t()
                p.mmgroup([(phk.t(b2 * 8 + tc, dc * 128, (dc + 1) * 128),
                            vtm.t(b2 * 8 + tc, eh * 512, (eh + 1) * 512))
                           for tc in range(8)], ps, tag=f"{tag}_kv{b2}{dc}{eh}")
                p.copy("v", kstage.t(), ps, tag=f"{tag}_kvev{b2}{dc}{eh}")
                p.add("y", (lambda e, o=kvr[b2, dc, eh], a=kstage.t().ap:
                            e.dma_start(out=o, in_=a)),
                      reads=[kstage.t()], writes=[kv_in.box((0, 1, b2 * 16 + dc * 2 + eh,
                                                             b2 * 16 + dc * 2 + eh + 1))],
                      kind="d", tag=f"{tag}_kvw{b2}{dc}{eh}")
    ksr = kv_in.handle.ap()[NB * D * D:].rearrange("(p j) -> p j", p=128)
    kss = slab.alloc(16, F32, f"kss_{tag}")
    for b2 in range(NB):
        for dc in range(CH):
            ps = mk.psF[4].t(0, 128, 0, 1)
            p.mmgroup([(phk.t(b2 * 8 + tc, dc * 128, (dc + 1) * 128),
                        mk.ones_col()) for tc in range(8)], ps,
                      tag=f"{tag}_ks{b2}{dc}")
            p.copy("v", kss.t(0, 128, b2 * 8 + dc, b2 * 8 + dc + 1), ps,
                   tag=f"{tag}_ksev{b2}{dc}")
    p.add("y", (lambda e, o=ksr, a=kss.t().ap: e.dma_start(out=o, in_=a)),
          reads=[kss.t()], writes=[kv_in.box((0, 1, 100, 101))], kind="d",
          tag=f"{tag}_ksw")
    kss.free()
    kstage.free()
    phk.free()
    vtm.free()
    p.coll("AllReduce", ALU.add, kv_in.t(), kv_out.t(), mk.rg, tag=f"{tag}_kvar")


    if W5_TRUNC <= 2:
        xnew = slab.falloc(CH, NCOL, F32, "xtr")
        for c in range(CH):
            restream_x_chunk(mk, st, c, _TruncView(xnew, c), "tr")
        st.x = xnew
        return
    w_qloc = mk.load_weight_bf16(WD["l1a_wq_loc"], D, tag=f"{tag}_wql")
    cs_ql = proj_cs_row(mk, w_qloc, D, gcol_bf=lng_bf, tag=f"{tag}_csql")
    br_ql = bias_row_bf(mk, w_qloc, D, lnb_bf, WD["l1a_bq_loc"], f"{tag}_brql")
    qloc = slab.falloc(CH, NCOL, BF16, f"qloc_{tag}")

    def ev_ql(mj, c0, c1, ps, bias):
        p.copy("s", qloc.t(mj, c0, c1), ps, tag=f"{tag}_evql{mj}")
    proj_fm(mk, xg, w_qloc, D, None, ev_ql, f"{tag}_ql", cs_row=cs_ql, nmr=nmr,
            bias_row=br_ql)
    w_qloc.free()
    cs_ql.free()
    br_ql.free()

    w_kloc = mk.load_weight_bf16(WD["l1a_wk_loc"], D, tag=f"{tag}_wkc")
    cs_kc = proj_cs_row(mk, w_kloc, D, gcol_bf=lng_bf, tag=f"{tag}_cskc")
    br_kc = bias_row_bf(mk, w_kloc, D, lnb_bf, WD["l1a_bk_loc"], f"{tag}_brkc")
    kloc = slab.falloc(CH, NCOL, BF16, f"kloc_{tag}")

    def ev_kc(mj, c0, c1, ps, bias):
        p.copy("s", kloc.t(mj, c0, c1), ps, tag=f"{tag}_evkc{mj}")
    proj_fm(mk, xg, w_kloc, D, None, ev_kc, f"{tag}_kc", cs_row=cs_kc, nmr=nmr,
            bias_row=br_kc)
    khalo = slab.falloc(CH, 128, BF16, f"khalo_{tag}")
    ones128r = None
    for mj in range(CH):
        pairs = [(w_kloc.t(c, mj * 128, (mj + 1) * 128),
                  TAP(hh.t().ap.rearrange("p (c n) -> p c n", c=CH)[:, c, :],
                      hh.slab.name, hh.t().box)) for c in range(CH)]
        pairs.append((cs_kc.t(0, 1, mj * 128, (mj + 1) * 128), nmrh.t(0, 1)))
        pairs.append((br_kc.t(0, 1, mj * 128, (mj + 1) * 128),
                      TAP(mk.ones512().ap[:, 0:128], "ones512", (0, 1, 0, 512))))
        ps = mk.psF[mj % 4].t(0, 128, 0, 128)
        p.mmgroup(pairs, ps, tag=f"{tag}_kh{mj}")
        p.copy("s", khalo.t(mj), ps, tag=f"{tag}_evkh{mj}")
    w_kloc.free()
    cs_kc.free()
    br_kc.free()

    w_vloc = mk.load_weight_bf16(WD["l1a_wv_loc"], D, tag=f"{tag}_wvc")
    cs_vc = proj_cs_row(mk, w_vloc, D, gcol_bf=lng_bf, tag=f"{tag}_csvc")
    br_vc = bias_row_bf(mk, w_vloc, D, lnb_bf, WD["l1a_bv_loc"], f"{tag}_brvc")
    vext = slab.falloc(NB * 9, D, BF16, f"vext_{tag}")

    def ev_vx(tc, e0, e1, ps):
        b2, tl = tc // 8, tc % 8
        base = b2 * 9 + tl
        for rr in range(3):
            p.copy("s", vext.t(base, e0, e1, p0=32 + rr * 32, p1=64 + rr * 32),
                   TAP(ps.ap[rr * 32:(rr + 1) * 32, :], ps.name,
                       (rr * 32, (rr + 1) * 32, ps.box[2], ps.box[3])),
                   tag=f"{tag}_evvx{tc}a{rr}")
        p.copy("s", vext.t(base + 1, e0, e1, p0=0, p1=32),
               TAP(ps.ap[96:128, :], ps.name, (96, 128, ps.box[2], ps.box[3])),
               tag=f"{tag}_evvx{tc}b")
    proj_tm(mk, xg, w_vloc, br_vc, nmr, cs_vc, ev_vx, f"{tag}_vx")
    for nt in range(2):
        e0, e1 = nt * 512, (nt + 1) * 512
        pairs = [(TAP(hh.t().ap.rearrange("p (c n) -> p c n", c=CH)[:, c, :],
                      hh.slab.name, hh.t().box), w_vloc.t(c, e0, e1)) for c in range(CH)]
        pairs.append((mk.ones_row_bf(0, 128), br_vc.t(0, 1, e0, e1)))
        pairs.append((nmrh.t(0, 1), cs_vc.t(0, 1, e0, e1)))
        ps = mk.psF[4].t()
        p.mmgroup(pairs, ps, tag=f"{tag}_vh{nt}")
        for b2 in range(NB):
            for j in range(2):
                r0 = b2 * 64 + j * 32
                if j == 0:
                    dst = vext.t(b2 * 9 + 0, e0, e1, p0=0, p1=32)
                else:
                    dst = vext.t(b2 * 9 + 8, e0, e1, p0=32, p1=64)
                p.copy("s", dst, TAP(ps.ap[r0:r0 + 32, :], ps.name,
                                     (ps.box[0] + r0, ps.box[0] + r0 + 32, ps.box[2], ps.box[3])),
                       tag=f"{tag}_evvh{nt}{b2}{j}")
    w_vloc.free()
    cs_vc.free()
    br_vc.free()
    xg_spill = mk.dinternal("xg_spill", [D, NCOL], BF16)
    xgsp = xg_spill.handle.ap().rearrange("(c p) n -> p c n", p=128)
    for c in range(CH):
        p.add("y", (lambda e, c=c, a=xg.t(c).ap: e.dma_start(out=xgsp[:, c, :], in_=a)),
              reads=[xg.t(c)], writes=[xg_spill.box((0, 1, c, c + 1))], kind="d",
              tag=f"{tag}_xgsp{c}")
    xg.free()


    if W5_TRUNC <= 3:
        xnew = slab.falloc(CH, NCOL, F32, "xtr")
        for c in range(CH):
            restream_x_chunk(mk, st, c, _TruncView(xnew, c), "tr")
        st.x = xnew
        return
    btol = mk.din("btol", [128, 16 * 192])
    b8 = {}
    btbuf = slab.alloc(16 * 192, F32, f"btol_{tag}")
    p.add("y", (lambda e, o=btbuf.t().ap, a=btol.handle.ap()[:, :]: e.dma_start(out=o, in_=a)),
          reads=[btol.t()], writes=[btbuf.t()], kind="d", tag=f"{tag}_btld")
    for v in ("I", "F", "L"):
        mdram = mk.din(f"wmask{v}", [128, 192])
        mbuf = slab.alloc(192, F32, f"m{v}_{tag}")
        p.add("y", (lambda e, o=mbuf.t().ap, a=mdram.handle.ap()[:, :]: e.dma_start(out=o, in_=a)),
              reads=[mdram.t()], writes=[mbuf.t()], kind="d", tag=f"{tag}_m{v}ld")
        bb = slab.alloc(16 * 192, BF16, f"b8{v}_{tag}")
        tmpb = slab.alloc(4 * 192, F32, f"b8t_{tag}")
        m3 = mbuf.t().ap.unsqueeze(1).broadcast_to([128, 4, 192])
        for hq in range(4):
            s0, s1 = hq * 4 * 192, (hq + 1) * 4 * 192
            p.add("v", (lambda e, o=tmpb.t().ap.rearrange("p (h j) -> p h j", h=4),
                        a=btbuf.t(0, 128, s0, s1).ap.rearrange("p (h j) -> p h j", h=4),
                        b=m3: e.tensor_tensor(out=o, in0=a, in1=b, op=ALU.add)),
                  reads=[btbuf.t(0, 128, s0, s1), mbuf.t()], writes=[tmpb.t()],
                  tag=f"{tag}_b8a{v}{hq}")
            p.ts("v", bb.t(0, 128, s0, s1), tmpb.t(), 8.0, ALU.mult, tag=f"{tag}_b8m{v}{hq}")
        tmpb.free()
        mbuf.free()
        b8[v] = bb
    btbuf.free()

    win = slab.falloc(16, D, BF16, f"win_{tag}")
    pbfs = [slab.alloc(4 * 192, BF16, f"pbf{i}_{tag}") for i in range(2)]
    nrm4 = slab.alloc(8, F32, f"nrm4_{tag}")
    rsum = slab.alloc(8, F32, f"rsum_{tag}")
    rrec = slab.alloc(8, F32, f"rrec_{tag}")
    atsbs = [slab.alloc(1024, BF16, f"atsb{i}_{tag}") for i in range(2)]
    _bqctr = [0]
    for b2 in range(NB):
        for q in range(8):
            _bqctr[0] += 1
            if W5_NBQ is not None and _bqctr[0] > W5_NBQ:
                continue
            t0own = b2 * SL + q * 128
            bsel = b8["F"] if q == 0 else (b8["L"] if q == 7 else b8["I"])
            segs = []
            ke0 = q * 128
            if q == 0:
                segs.append((0, 32, lambda h, b2=b2: khalo.t(h // 2, b2 * 64, b2 * 64 + 32,
                                                             p0=(h % 2) * 64, p1=(h % 2) * 64 + 64)))
                segs.append((32, 192, lambda h, b2=b2: kloc.t(h // 2, b2 * SL, b2 * SL + 160,
                                                              p0=(h % 2) * 64, p1=(h % 2) * 64 + 64)))
            elif q == 7:
                segs.append((0, 160, lambda h, b2=b2: kloc.t(h // 2, b2 * SL + 864, b2 * SL + SL,
                                                             p0=(h % 2) * 64, p1=(h % 2) * 64 + 64)))
                segs.append((160, 192, lambda h, b2=b2: khalo.t(h // 2, b2 * 64 + 32, b2 * 64 + 64,
                                                                p0=(h % 2) * 64, p1=(h % 2) * 64 + 64)))
            else:
                o0 = b2 * SL + ke0 - 32
                segs.append((0, 192, lambda h, o0=o0: kloc.t(h // 2, o0, o0 + 192,
                                                             p0=(h % 2) * 64, p1=(h % 2) * 64 + 64)))
            for hg in range(4):
                psS = [mk.psF[2 * (hg % 2)], mk.psF[2 * (hg % 2) + 1]]
                psT = mk.psB[hg % 2]
                psO = mk.psF[4 + hg % 2]
                pbf = pbfs[hg % 2]
                atsb = atsbs[hg % 2]
                rof = hg % 2 * 4
                for hh_ in range(4):
                    h = hg * 4 + hh_
                    bank = psS[hh_ // 2]
                    colo = (hh_ % 2) * 192
                    pst = bank.t(0, 128, colo, colo + 192)
                    qtap = qloc.t(h // 2, t0own, t0own + 128,
                                  p0=(h % 2) * 64, p1=(h % 2) * 64 + 64)
                    def em_sc(e, pst=pst.ap, qtap=qtap.ap, segs=segs, h=h, bsel=bsel, colo=colo):
                        inst = None
                        first = True
                        for (s0, s1, kf) in segs:
                            inst = e.matmul(pst[:, s0:s1], qtap, kf(h).ap,
                                            start=first, stop=False)
                            first = False
                        inst = e.matmul(pst, mk.ident_tap.ap,
                                        bsel.t(0, 128, h * 192, (h + 1) * 192).ap,
                                        start=False, stop=True)
                        return inst
                    rd = [qloc.t(h // 2, t0own, t0own + 128), kloc.t(h // 2, 0, NCOL),
                          khalo.t(h // 2), bsel.t(0, 128, h * 192, (h + 1) * 192),
                          mk.ident_tap]
                    p.add("t", em_sc, reads=rd, writes=[pst], tag=f"{tag}_sc{b2}{q}{h}")
                for bi in range(2):
                    bank = psS[bi]
                    p.add("v", (lambda e, o=nrm4.t(0, 128, hg % 2 * 4 + bi * 2,
                                                   hg % 2 * 4 + bi * 2 + 2).ap,
                                a=bank.t(0, 128, 0, 384).ap.rearrange("p (h j) -> p h j", h=2):
                                e.tensor_reduce(out=o, in_=a, axis=AX, op=ALU.max, negate=True)),
                          reads=[bank.t(0, 128, 0, 384)],
                          writes=[nrm4.t(0, 128, hg % 2 * 4 + bi * 2, hg % 2 * 4 + bi * 2 + 2)],
                          tag=f"{tag}_nrm{b2}{q}{hg}{bi}")
                p.ts("v", nrm4.t(0, 128, hg % 2 * 4, hg % 2 * 4 + 4),
                     nrm4.t(0, 128, hg % 2 * 4, hg % 2 * 4 + 4), 0.125, ALU.mult,
                     tag=f"{tag}_nsc{b2}{q}{hg}")
                for hh_ in range(4):
                    h = hg * 4 + hh_
                    bank = psS[hh_ // 2]
                    colo = (hh_ % 2) * 192
                    p.act(pbf.t(0, 128, hh_ * 192, (hh_ + 1) * 192),
                          bank.t(0, 128, colo, colo + 192), AF.Exp,
                          bias=nrm4.t(0, 128, hg % 2 * 4 + hh_, hg % 2 * 4 + hh_ + 1),
                          scale=0.125,
                          accum=rsum.t(0, 128, rof + hh_, rof + hh_ + 1),
                          tag=f"{tag}_exp{b2}{q}{h}")
                p.add("v", (lambda e, o=rrec.t(0, 128, rof, rof + 4).ap,
                            a=rsum.t(0, 128, rof, rof + 4).ap: e.reciprocal(out=o, in_=a)),
                      reads=[rsum.t(0, 128, rof, rof + 4)],
                      writes=[rrec.t(0, 128, rof, rof + 4)], tag=f"{tag}_rr{b2}{q}{hg}")
                for hh_ in range(4):
                    p.transpose(psT.t(0, 128, hh_ * 128, hh_ * 128 + 128),
                                pbf.t(0, 128, hh_ * 192, hh_ * 192 + 128), mk.ident_tap,
                                tag=f"{tag}_tp{b2}{q}{hg}{hh_}a")
                    p.transpose(psT.t(0, 64, 512 + hh_ * 128, 512 + hh_ * 128 + 128),
                                pbf.t(0, 128, hh_ * 192 + 128, hh_ * 192 + 192), mk.ident_tap,
                                tag=f"{tag}_tp{b2}{q}{hg}{hh_}b")
                p.copy("v", atsb.t(0, 128, 0, 512), psT.t(0, 128, 0, 512),
                       tag=f"{tag}_at{b2}{q}{hg}")
                p.copy("v", atsb.t(0, 64, 512, 1024), psT.t(0, 64, 512, 1024),
                       tag=f"{tag}_at{b2}{q}{hg}b")
                for hh_ in range(4):
                    h = hg * 4 + hh_
                    vc1 = vext.t(b2 * 9 + q, h * 64, (h + 1) * 64)
                    vc2 = vext.t(b2 * 9 + q + 1, h * 64, (h + 1) * 64, p0=0, p1=64)
                    pso = psO.t(0, 128, hh_ * 64, (hh_ + 1) * 64)
                    p.mmgroup([(atsb.t(0, 128, hh_ * 128, hh_ * 128 + 128), vc1),
                               (atsb.t(0, 64, 512 + hh_ * 128, 512 + hh_ * 128 + 128), vc2)],
                              pso, tag=f"{tag}_ao{b2}{q}{h}")
                    p.ts("v", win.t(b2 * 8 + q, h * 64, (h + 1) * 64), pso,
                         rrec.t(0, 128, rof + hh_, rof + hh_ + 1), ALU.mult,
                         tag=f"{tag}_wo{b2}{q}{h}")
    qloc.free()
    kloc.free()
    khalo.free()
    vext.free()
    hh.free()
    nmrh.free()
    for b in pbfs:
        b.free()
    nrm4.free()
    rsum.free()
    rrec.free()
    for b in atsbs:
        b.free()
    for v in b8:
        b8[v].free()

    mk.tap("w5_win", win, 16, D, BF16)

    if W5_TRUNC <= 4:
        xnew = slab.falloc(CH, NCOL, F32, "xtr")
        for c in range(CH):
            restream_x_chunk(mk, st, c, _TruncView(xnew, c), "tr")
        st.x = xnew
        return
    winfm = slab.falloc(CH, NCOL, BF16, f"winfm_{tag}")
    for mj in range(CH):
        for grp in range(2):
            bank = mk.psB[(mj * 2 + grp) % 2]
            for k8 in range(8):
                tq = grp * 8 + k8
                p.transpose(bank.t(0, 128, k8 * 128, (k8 + 1) * 128),
                            win.t(tq, mj * 128, (mj + 1) * 128), mk.ident_tap,
                            tag=f"{tag}_wt{mj}{tq}")
            p.copy("v", winfm.t(mj, grp * 1024, (grp + 1) * 1024), bank.t(),
                   tag=f"{tag}_wtev{mj}{grp}")
    win.free()

    xg = slab.falloc(CH, NCOL, BF16, f"xg2_{tag}")
    for c in range(CH):
        p.add("y", (lambda e, c=c, o=xg.t(c).ap: e.dma_start(out=o, in_=xgsp[:, c, :])),
              reads=[xg_spill.box((0, 1, c, c + 1))], writes=[xg.t(c)], kind="d",
              tag=f"{tag}_xgld{c}")
    w_qlin = mk.load_weight_bf16(WD["l1a_wq_lin"], D, tag=f"{tag}_wqn")
    cs_qn = proj_cs_row(mk, w_qlin, D, gcol_bf=lng_bf, tag=f"{tag}_csqn")
    bc_qn = matcol(mk, w_qlin, lnb_bf, extra_col=mk.load_col(WD["l1a_bq_lin"], tag=f"{tag}_bqn"),
                   tag=f"{tag}_bcqn")
    phq = slab.falloc(CH, NCOL, BF16, f"phq_{tag}")
    proj_fm(mk, xg, w_qlin, D, bc_qn, phi_evac_fm(mk, phq, f"{tag}_phq"), f"{tag}_qn",
            cs_row=cs_qn, nmr=nmr)
    w_qlin.free()
    cs_qn.free()
    bc_qn.free()
    kvbf = slab.falloc(16, D, BF16, f"kvbf_{tag}")
    kvo = kv_out.handle.ap()[0:NB * D * D].rearrange("(b dc eh p n) -> b dc eh p n",
                                                     b=NB, dc=CH, eh=2, p=128)
    for b2 in range(NB):
        for dc in range(CH):
            for eh in range(2):
                p.add("g", (lambda e, o=kvbf.t(b2 * 8 + dc, eh * 512, (eh + 1) * 512).ap,
                            a=kvo[b2, dc, eh]: e.dma_start(out=o, in_=a)),
                      reads=[kv_out.t()], writes=[kvbf.t(b2 * 8 + dc, eh * 512, (eh + 1) * 512)],
                      kind="d", tag=f"{tag}_kvrb{b2}{dc}{eh}")
    kso = kv_out.handle.ap()[NB * D * D:]
    ksumc = slab.alloc(16, BF16, f"ksumc_{tag}")
    p.add("g", (lambda e, o=ksumc.t().ap,
                a=kso.rearrange("(p j) -> p j", p=128):
                e.dma_start(out=o, in_=a)),
          reads=[kv_out.t()], writes=[ksumc.t()], kind="d", tag=f"{tag}_ksrb")
    denrow = slab.alloc(NCOL, F32, f"den_{tag}")
    for b2 in range(NB):
        for nt in range(2):
            c0 = b2 * SL + nt * 512
            ps = mk.psF[5].t(0, 1, 0, 512)
            p.mmgroup([(ksumc.t(0, 128, b2 * 8 + dc, b2 * 8 + dc + 1),
                        phq.t(dc, c0, c0 + 512)) for dc in range(CH)], ps,
                      tag=f"{tag}_dn{b2}{nt}")
            p.copy("v", denrow.t(0, 1, c0, c0 + 512), ps, tag=f"{tag}_dnev{b2}{nt}")
    p.ts("v", denrow.t(0, 1), denrow.t(0, 1), 1e-6, ALU.add, tag=f"{tag}_deps")
    p.act(denrow.t(0, 1), denrow.t(0, 1), AF.Ln, tag=f"{tag}_dln")
    p.act(denrow.t(0, 1), denrow.t(0, 1), AF.Exp, scale=-1.0, tag=f"{tag}_drec")
    r2bc = slab.alloc(NCOL, F32, f"r2bc_{tag}")
    for nt, (c0, c1) in enumerate(NT512):
        ps = mk.psF[5].t()
        p.mmgroup([(mk.ones_row_f32(), denrow.t(0, 1, c0, c1))], ps, tag=f"{tag}_r2b{nt}")
        p.copy("v", r2bc.t(0, 128, c0, c1), ps, tag=f"{tag}_r2e{nt}")
    denrow.free()
    ksumc.free()
    linfm = slab.falloc(CH, NCOL, BF16, f"lin_{tag}")
    for b2 in range(NB):
        for mj in range(CH):
            for nt in range(2):
                c0 = b2 * SL + nt * 512
                pairs = [(kvbf.t(b2 * 8 + dc, mj * 128, (mj + 1) * 128),
                          phq.t(dc, c0, c0 + 512)) for dc in range(CH)]
                ps = mk.psF[(mj * 2 + nt) % 4].t()
                p.mmgroup(pairs, ps, tag=f"{tag}_nm{b2}{mj}{nt}")
                p.tt("v", linfm.t(mj, c0, c0 + 512), ps,
                     TAP(r2bc.t(0, 128, c0 - b2 * SL + b2 * SL, c0 + 512).ap, "slab",
                         r2bc.t().box) if False else r2bc.t(0, 128, c0, c0 + 512),
                     ALU.mult, tag=f"{tag}_lin{b2}{mj}{nt}")
    mk.tap("w5_lin", linfm, CH, NCOL, BF16)
    phq.free()
    kvbf.free()
    r2bc.free()


    if W5_TRUNC <= 5:
        xnew = slab.falloc(CH, NCOL, F32, "xtr")
        for c in range(CH):
            restream_x_chunk(mk, st, c, _TruncView(xnew, c), "tr")
        st.x = xnew
        return
    w_g1 = mk.load_weight_bf16(WD["l1a_wg1"], D, tag=f"{tag}_wg1a")
    w_g1b = mk.load_weight_bf16(WD["l1a_wg1b"], D, tag=f"{tag}_wg1b")
    w_g1c = mk.load_weight_bf16(WD["l1a_wg1c"], D, tag=f"{tag}_wg1c")
    cs_g1 = proj_cs_row(mk, w_g1c, D, gcol_bf=lng_bf, tag=f"{tag}_csg1")
    bc_g1 = matcol(mk, w_g1c, lnb_bf, extra_col=mk.load_col(WD["l1a_bg1"], tag=f"{tag}_bg1"),
                   tag=f"{tag}_bcg1")
    g1 = slab.falloc(CH, NCOL, BF16, f"g1_{tag}")
    for mj in range(CH):
        for nt, (c0, c1) in enumerate(NT512):
            pairs = [(w_g1.t(c, mj * 128, (mj + 1) * 128), linfm.t(c, c0, c1))
                     for c in range(CH)]
            pairs += [(w_g1b.t(c, mj * 128, (mj + 1) * 128), winfm.t(c, c0, c1))
                      for c in range(CH)]
            pairs += [(w_g1c.t(c, mj * 128, (mj + 1) * 128), xg.t(c, c0, c1))
                      for c in range(CH)]
            pairs.append((cs_g1.t(0, 1, mj * 128, (mj + 1) * 128), nmr.t(0, 1, c0, c1)))
            ps = mk.psF[(mj * 4 + nt) % 5].t()
            p.mmgroup(pairs, ps, tag=f"{tag}_g1{mj}{nt}")
            p.act(g1.t(mj, c0, c1), ps, AF.Relu, bias=bc_g1.t(0, 128, mj, mj + 1),
                  tag=f"{tag}_g1ev{mj}{nt}")
    w_g1.free()
    w_g1b.free()
    w_g1c.free()
    cs_g1.free()
    bc_g1.free()
    xg.free()
    nmr.free()

    mk.tap("w5_g1", g1, CH, NCOL, BF16)
    w_g2 = mk.load_weight_bf16(WD["l1a_wg2"], D, tag=f"{tag}_wg2")
    bg2_col = mk.load_col(WD["l1a_bg2"], tag=f"{tag}_bg2")
    oa = slab.falloc(CH, NCOL, BF16, f"oa_{tag}")
    for mj in range(CH):
        for nt, (c0, c1) in enumerate(NT512):
            pairs = [(w_g2.t(c, mj * 128, (mj + 1) * 128), g1.t(c, c0, c1))
                     for c in range(CH)]
            ps = mk.psF[(mj * 4 + nt) % 5].t()
            p.mmgroup(pairs, ps, tag=f"{tag}_g2{mj}{nt}")
            gt = slab.alloc(512, F32, "gt2")
            dd = slab.alloc(512, F32, "dd2")
            p.act(gt.t(), ps, AF.Sigmoid, bias=bg2_col.t(0, 128, mj, mj + 1),
                  tag=f"{tag}_g2s{mj}{nt}")
            p.tt("v", dd.t(), linfm.t(mj, c0, c1), winfm.t(mj, c0, c1), ALU.subtract,
                 tag=f"{tag}_d{mj}{nt}")
            p.tt("v", dd.t(), gt.t(), dd.t(), ALU.mult, tag=f"{tag}_gd{mj}{nt}")
            p.tt("v", oa.t(mj, c0, c1), dd.t(), winfm.t(mj, c0, c1), ALU.add,
                 tag=f"{tag}_oa{mj}{nt}")
            gt.free()
            dd.free()
    w_g2.free()
    bg2_col.free()
    linfm.free()
    winfm.free()
    g1.free()

    mk.tap("w5_oa", oa, CH, NCOL, BF16)
    w_o = mk.load_weight_bf16(WD["l1a_wo"], D, tag=f"{tag}_wo")
    bo_col = mk.load_col(WD["l1a_bo"], tag=f"{tag}_bo")
    xnew = slab.falloc(CH, NCOL, F32, f"xnew_{tag}")
    for mj in range(CH):
        xold = slab.alloc(NCOL, F32, f"xoldw{mj}")
        restream_x_chunk(mk, st, mj, xold, tag)
        xlold = slab.alloc(NCOL, F32, f"xlold{mj}")
        p.add("y", (lambda e, mj=mj, o=xlold.t().ap: e.dma_start(out=o, in_=xlsp[:, mj, :])),
              reads=[xln_spill.box((0, 1, mj, mj + 1))], writes=[xlold.t()], kind="d",
              tag=f"{tag}_xlld{mj}")
        for nt, (c0, c1) in enumerate(NT512):
            pairs = [(w_o.t(c, mj * 128, (mj + 1) * 128), oa.t(c, c0, c1))
                     for c in range(CH)]
            ps = mk.psF[(mj * 4 + nt) % 5].t()
            p.mmgroup(pairs, ps, tag=f"{tag}_wo{mj}{nt}")
            p.stt("v", xnew.t(mj, c0, c1), ps, bo_col.t(0, 128, mj, mj + 1),
                  xold.t(0, 128, c0, c1), ALU.add, ALU.add, tag=f"{tag}_wres{mj}{nt}")
            p.tt("v", xnew.t(mj, c0, c1), xnew.t(mj, c0, c1), xlold.t(0, 128, c0, c1),
                 ALU.add, tag=f"{tag}_wres2{mj}{nt}")
        xold.free()
        xlold.free()
    w_o.free()
    bo_col.free()
    oa.free()
    for b in (lng_col, lnb_col, lng_bf, lnb_bf):
        b.free()
    st.x = xnew



def build_nc(n_steps=None):
    if n_steps is None:
        n_steps = N_STEPS
    es = ExitStack()
    nc = bass.Bass("TRN2", target_bir_lowering=False)
    mk = MK(nc, es, n_steps)
    p = mk.p

    x_in = mk.din("x_in", [D, NCOL])
    y_out = mk.dout("y_out", [D, NCOL])

    WD = {}
    for nm in ["l0f_wf", "l0f_wi", "l0f_wc", "l0f_wo",
               "l0f_uf", "l0f_ui", "l0f_uc", "l0f_uo",
               "l0a_wk", "l0a_wka", "l0a_wv"]:
        WD[nm] = mk.din(nm, [D, D])
    for nm in ["l0f_bf", "l0f_bi", "l0f_bc", "l0f_bo", "l0f_inc",
               "l0a_bk", "l0a_bka", "l0a_bv", "l0a_gamma", "l0a_beta"]:
        WD[nm] = mk.din(nm, [D])
    WD["l1f_w_up"] = mk.din("l1f_w_up", [3072, D])
    WD["l1f_w_gate"] = mk.din("l1f_w_gate", [3072, D])
    WD["l1f_w_down"] = mk.din("l1f_w_down", [D, 3072])
    for nm in ["l1a_wq_lin", "l1a_wk_lin", "l1a_wv_lin", "l1a_wq_loc", "l1a_wk_loc",
               "l1a_wv_loc", "l1a_wo", "l1a_wg1", "l1a_wg1b", "l1a_wg1c", "l1a_wg2"]:
        WD[nm] = mk.din(nm, [D, D])
    for nm in ["l1a_bq_lin", "l1a_bk_lin", "l1a_bv_lin", "l1a_bq_loc", "l1a_bk_loc",
               "l1a_bv_loc", "l1a_bo", "l1a_bg1", "l1a_bg2", "l1a_ln_g", "l1a_ln_b"]:
        WD[nm] = mk.din(nm, [D])
    ident_in = mk.din("identity_in", [128, 128])
    p.add("g", (lambda e, o=mk.ident_tap.ap, a=ident_in.handle.ap()[:, :]:
                e.dma_start(out=o, in_=a)),
          reads=[ident_in.t()], writes=[mk.ident_tap], kind="d", tag="ldident")
    rank_in = mk.din("rank_ids", [2], I32)
    p.add("y", (lambda e, o=mk.rank_t.ap()[0:1, :], a=rank_in.handle.ap()[None, :]:
                e.dma_start(out=o, in_=a)),
          reads=[rank_in.t()], writes=[TAP(None, "rankt", (0, 1, 0, 2))], kind="d",
          tag="ldrank")

    pfx = mk.din("pfx_mask", [8])
    pfx_buf = mk.slab.alloc(8, F32, "pfx")
    mk.pfx_tap = TAP(mk.slab.handle.ap()[0:8, pfx_buf.boff:pfx_buf.boff + 4].bitcast(F32),
                     "slab", (0, 8, pfx_buf.boff, pfx_buf.boff + 4))
    p.add("y", (lambda e, o=mk.pfx_tap.ap, a=pfx.handle.ap()[:, None]: e.dma_start(out=o, in_=a)),
          reads=[pfx.t()], writes=[mk.pfx_tap], kind="d", tag="ldpfx")

    xch = mk.slab.falloc(CH, NCOL, F32, "x0")
    xin_r = x_in.handle.ap().rearrange("(c p) n -> p c n", p=128)
    for c in range(CH):
        p.add("y", (lambda e, c=c, o=xch.t(c).ap: e.dma_start(out=o, in_=xin_r[:, c, :])),
              reads=[x_in.t()], writes=[xch.t(c)], kind="d", tag=f"ldx{c}")

    st = State(mk, xch)
    st.x_spill = mk.dinternal("x_spill", [D, NCOL])
    if n_steps >= 4:
        gen = sub_l1_prep_gen(mk, st, WD)
        mk.pump = lambda n=2: [next(gen, None) for _ in range(n)]

    steps = [
        lambda: sub_l0_ffn(mk, st, WD, True),
        lambda: sub_l0_attn(mk, st, WD),
        lambda: sub_l0_ffn(mk, st, WD, False),
        lambda: sub_l1_ffn(mk, st, WD, True),
        lambda: sub_l1_attn(mk, st, WD),
        lambda: sub_l1_ffn(mk, st, WD, False),
    ]
    for i in range(n_steps):
        if i == 3:
            mk.pump(10000)
            for b in mk._dq_transients:
                b.free()
        steps[i]()

    yout_r = y_out.handle.ap().rearrange("(c p) n -> p c n", p=128)
    for c in range(CH):
        p.add("y", (lambda e, c=c, a=st.x.t(c).ap: e.dma_start(out=yout_r[:, c, :], in_=a)),
              reads=[st.x.t(c)], writes=[y_out.t()], kind="d", tag=f"sty{c}")

    p.finalize()
    with nc.Block() as block:
        p.emit_all(block)
    es.close()
    return nc



def stage_inputs(inputs):
    f32 = np.float32

    def T(a):
        return np.ascontiguousarray(np.asarray(a, f32).T)

    def V(a):
        return np.ascontiguousarray(np.asarray(a, f32))

    x = np.asarray(inputs["L0_attn_hidden_states"], f32)
    shared = {
        "l0f_wf": T(inputs["L0_ffn_wf"]), "l0f_wi": T(inputs["L0_ffn_wi"]),
        "l0f_wc": T(inputs["L0_ffn_wc"]), "l0f_wo": T(inputs["L0_ffn_wo"]),
        "l0f_uf": T(inputs["L0_ffn_uf"]), "l0f_ui": T(inputs["L0_ffn_ui"]),
        "l0f_uc": T(inputs["L0_ffn_uc"]), "l0f_uo": T(inputs["L0_ffn_uo"]),
        "l0f_bf": V(inputs["L0_ffn_bf"]), "l0f_bi": V(inputs["L0_ffn_bi"]),
        "l0f_bc": V(inputs["L0_ffn_bc"]), "l0f_bo": V(inputs["L0_ffn_bo"]),
        "l0f_inc": V(inputs["L0_ffn_in_c"]),
        "l0a_wk": T(inputs["L0_attn_Wk"]), "l0a_wka": T(inputs["L0_attn_Wka"]),
        "l0a_wv": T(inputs["L0_attn_Wv"]),
        "l0a_bk": V(inputs["L0_attn_bk"]), "l0a_bka": V(inputs["L0_attn_bka"]),
        "l0a_bv": V(inputs["L0_attn_bv"]),
        "l0a_gamma": V(inputs["L0_attn_gamma"]), "l0a_beta": V(inputs["L0_attn_beta"]),
        "l1f_w_up": V(inputs["L1_ffn_w_up"]), "l1f_w_gate": V(inputs["L1_ffn_w_gate"]),
        "l1f_w_down": V(inputs["L1_ffn_w_down"]),
        "l1a_wq_lin": V(inputs["L1_attn_wq_lin"]), "l1a_wk_lin": V(inputs["L1_attn_wk_lin"]),
        "l1a_wv_lin": V(inputs["L1_attn_wv_lin"]), "l1a_wq_loc": V(inputs["L1_attn_wq_loc"]),
        "l1a_wk_loc": V(inputs["L1_attn_wk_loc"]), "l1a_wv_loc": V(inputs["L1_attn_wv_loc"]),
        "l1a_wo": V(inputs["L1_attn_wo"]),
        "l1a_wg1": V(inputs["L1_attn_wg1"][0:1024]),
        "l1a_wg1b": V(inputs["L1_attn_wg1"][1024:2048]),
        "l1a_wg1c": V(inputs["L1_attn_wg1"][2048:3072]),
        "l1a_wg2": V(inputs["L1_attn_wg2"]),
        "l1a_bq_lin": V(inputs["L1_attn_bq_lin"]), "l1a_bk_lin": V(inputs["L1_attn_bk_lin"]),
        "l1a_bv_lin": V(inputs["L1_attn_bv_lin"]), "l1a_bq_loc": V(inputs["L1_attn_bq_loc"]),
        "l1a_bk_loc": V(inputs["L1_attn_bk_loc"]), "l1a_bv_loc": V(inputs["L1_attn_bv_loc"]),
        "l1a_bo": V(inputs["L1_attn_bo"]), "l1a_bg1": V(inputs["L1_attn_bg1"]),
        "l1a_bg2": V(inputs["L1_attn_bg2"]),
        "l1a_ln_g": V(inputs["L1_attn_ln_g"]), "l1a_ln_b": V(inputs["L1_attn_ln_b"]),
        "identity_in": np.eye(128, dtype=f32),
    }
    rel = np.asarray(inputs["L1_attn_rel_bias"], f32)
    pp, jj = np.meshgrid(np.arange(128), np.arange(192), indexing="ij")
    idx = np.clip(jj - pp + 32, 0, 128)
    btol = rel[idx, :]
    btol2 = np.ascontiguousarray(btol.transpose(0, 2, 1).reshape(128, 16 * 192))
    band = (jj - pp >= 0) & (jj - pp <= 64)
    maskI = np.where(band, 0.0, -1e30).astype(f32)
    shared["btol"] = btol2
    shared["wmaskI"] = maskI
    in_maps = []
    for k in range(NCORES):
        sl = slice(k * SL, (k + 1) * SL)
        xk = np.ascontiguousarray(x[:, sl, :].transpose(2, 0, 1).reshape(D, NCOL))
        m = dict(shared)
        m["x_in"] = xk
        m["pfx_mask"] = (np.arange(8) < k).astype(np.float32)
        mF = maskI.copy()
        if k == 0:
            mF = np.where(jj < 32, np.float32(-1e30), mF).astype(f32)
        mL = maskI.copy()
        if k == NCORES - 1:
            mL = np.where(jj >= 160, np.float32(-1e30), mL).astype(f32)
        m["wmaskF"] = mF
        m["wmaskL"] = mL
        m["rank_ids"] = np.array([(k - 1) % NCORES, (k + 1) % NCORES], np.int32)
        in_maps.append(m)
    return in_maps


def gather(results):
    out = np.empty((NB, SEQ, D), dtype=np.float32)
    for k in range(NCORES):
        yk = np.asarray(results[k]["y_out"])
        out[:, k * SL:(k + 1) * SL, :] = yk.reshape(D, NB, SL).transpose(1, 2, 0)
    return out


def kernel(**inputs):
    nc = build_nc()
    in_maps = stage_inputs(inputs)
    res = run_bass_kernel_spmd(nc, in_maps, core_ids=list(range(NCORES)))
    return gather(res.results)
